# version 11
# baseline (speedup 1.0000x reference)
import contextlib
import math
import numpy as np

import concourse.bass as bass
import concourse.mybir as mybir
from concourse.bass import IndirectOffsetOnAxis
from concourse.bass_utils import run_bass_kernel_spmd

F32 = mybir.dt.float32
F16 = mybir.dt.float16
BF16 = mybir.dt.bfloat16
I32 = mybir.dt.int32
AF = mybir.ActivationFunctionType
ALU = mybir.AluOpType
AXX = mybir.AxisListType.X

NCORES = 8
S = 8192
D = 4096
HD = 128
NHL = 4
TLOC = S // NCORES
SCALE = 1.0 / math.sqrt(HD)
EPS = 1e-5

CCH = 3 * NHL
ICH = D // 128
TT = 512
NTT = S // TT
TQ = 512
NQT = S // TQ
KCH = S // 128
GS = 3
SUB = 512
NSUB = TLOC // SUB
OCH = D // 128


class Sem:
    def __init__(self, handle, name):
        self.h = handle
        self.name = name
        self.n = 0


class Prog:
    QUEUES = ("sync", "tensor", "vector", "scalar", "gpsimd")

    def __init__(self):
        self.q = {k: [] for k in self.QUEUES}

    def op(self, qn, fn, waits=(), inc=None, inc_amt=1, defer=None):
        for s, v in waits:
            assert v <= s.n, f"wait on {s.name} for {v} but only {s.n} recorded"
        wl = [(s.h, v) for s, v in waits if v > 0]
        if inc is not None:
            inc.n += inc_amt
            ih, ia = inc.h, inc_amt
        else:
            ih, ia = None, 0

        def run(eng, wl=wl, fn=fn, ih=ih, ia=ia):
            for h, v in wl:
                eng.wait_ge(h, v)
            ins = fn(eng)
            if ih is not None:
                if ins is None:
                    ins = eng.nop()
                ins.then_inc(ih, ia)
        if defer is not None:
            defer.append((qn, run))
        else:
            self.q[qn].append(run)

    def flush(self, defer):
        for qn, run in defer:
            self.q[qn].append(run)
        defer.clear()


def build_nc():
    nc = bass.Bass("TRN2", target_bir_lowering=False)

    x32 = nc.dram_tensor("x32", [S, D], F32, kind="ExternalInput").ap()
    x32m = nc.dram_tensor("x32m", [TLOC, D], F32, kind="ExternalInput").ap()
    qkvw = nc.dram_tensor("qkvw", [D, CCH * 128], F32, kind="ExternalInput").ap()
    qkvb = nc.dram_tensor("qkvb", [128, CCH], F32, kind="ExternalInput").ap()
    ow = nc.dram_tensor("ow", [D, D], F32, kind="ExternalInput").ap()
    obt = nc.dram_tensor("obt", [128, OCH], F32, kind="ExternalInput").ap()
    Wt = nc.dram_tensor("W", [D, D], F32, kind="ExternalInput").ap()
    bt = nc.dram_tensor("bt", [128, OCH], F32, kind="ExternalInput").ap()
    A = nc.dram_tensor("A", [32, D], F32, kind="ExternalInput").ap()
    BmT = nc.dram_tensor("BmT", [32, D], F32, kind="ExternalInput").ap()
    Wr = nc.dram_tensor("Wr", [4, D], F32, kind="ExternalInput").ap()
    sel = nc.dram_tensor("sel", [4, 32], F16, kind="ExternalInput").ap()
    id16 = nc.dram_tensor("id16", [128, 128], F16, kind="ExternalInput").ap()
    idbf = nc.dram_tensor("idbf", [128, 128], BF16, kind="ExternalInput").ap()
    id32 = nc.dram_tensor("id32", [128, 128], F32, kind="ExternalInput").ap()
    gidx = nc.dram_tensor("gidx", [128, OCH], I32, kind="ExternalInput").ap()

    y_out = nc.dram_tensor("y_out", [TLOC, D], F32, kind="ExternalOutput").ap()
    x16d = nc.dram_tensor("x16d", [S, D], F16).ap()
    x16md = nc.dram_tensor("x16md", [TLOC, D], F16).ap()
    w16d = nc.dram_tensor("w16d", [D, D], F16).ap()
    qTd = nc.dram_tensor("qTd", [NHL * 128, S], F16).ap()
    kTd = nc.dram_tensor("kTd", [NHL * 128, S], F16).ap()
    vd = nc.dram_tensor("vd", [S, NHL * 128], BF16).ap()
    ag_in = nc.dram_tensor("ag_in", [NHL * 128, S], F16).ap()
    ag_out = nc.dram_tensor("ag_out", [D, S], F16, addr_space="Shared").ap()
    ctx_mine_d = nc.dram_tensor("ctx_mine_d", [D, TLOC], F16).ap()
    ag_out_flat = ag_out.rearrange("a b -> (a b)").rearrange("(n one) -> n one", one=1)

    p = Prog()
    stack = contextlib.ExitStack()
    nc._kernel_stack = stack

    def sem(name):
        return Sem(stack.enter_context(nc.semaphore(name)), name)

    sCast = sem("sCast")
    sCastM = sem("sCastM")
    sCastW = sem("sCastW")
    sWld = sem("sWld")
    sXT = sem("sXT")
    sMM = sem("sMM")
    sEv = sem("sEv")
    sVt = sem("sVt")
    sVe = sem("sVe")
    sSp = sem("sSp")
    sNq = sem("sNq")
    sNm = sem("sNm")
    sBh = sem("sBh")
    sLd = sem("sLd")
    sST = sem("sST")
    sEX = sem("sEX")
    sPV = sem("sPV")
    sTR = sem("sTR")
    sTRb = sem("sTRb")
    sEp = sem("sEp")
    sAG = sem("sAG")
    sCC = sem("sCC")
    sGa = sem("sGa")
    sOwl = sem("sOwl")
    sXtp = sem("sXtp")
    sPm = sem("sPm")
    sPe = sem("sPe")
    sPv = sem("sPv")
    sWtl = sem("sWtl")
    sPs = sem("sPs")
    sOut = sem("sOut")
    sIdx = sem("sIdx")
    sCst = sem("sCst")

    ident16 = stack.enter_context(nc.sbuf_tensor("ident16", [128, 128], F16))
    identbf = stack.enter_context(nc.sbuf_tensor("identbf", [128, 128], BF16))
    ident32 = stack.enter_context(nc.sbuf_tensor("ident32", [128, 128], F32))
    ones16 = stack.enter_context(nc.sbuf_tensor("ones16", [128, 1], F16))
    onesbf = stack.enter_context(nc.sbuf_tensor("onesbf", [128, 1], BF16))
    ones32r = stack.enter_context(nc.sbuf_tensor("ones32r", [1, 128], F32))
    qkvb_sb = stack.enter_context(nc.sbuf_tensor("qkvb_sb", [128, CCH], F32))
    bias_h = stack.enter_context(nc.sbuf_tensor("bias_h", [128, NHL], F32))
    norm_acc = stack.enter_context(nc.sbuf_tensor("norm_acc", [1, 2 * NHL], F32))
    rowtmp = stack.enter_context(nc.sbuf_tensor("rowtmp", [1, 8], F32))

    p.op("sync", lambda e: e.dma_start(out=ident16[:, :], in_=id16[:, :]), inc=sCst, inc_amt=16)
    p.op("sync", lambda e: e.dma_start(out=identbf[:, :], in_=idbf[:, :]), inc=sCst, inc_amt=16)
    p.op("sync", lambda e: e.dma_start(out=ident32[:, :], in_=id32[:, :]), inc=sCst, inc_amt=16)
    p.op("sync", lambda e: e.dma_start(out=qkvb_sb[:, :], in_=qkvb[:, :]), inc=sCst, inc_amt=16)
    p.op("vector", lambda e: e.memset(ones16[:, :], 1.0), inc=sPv)
    p.op("vector", lambda e: e.memset(onesbf[:, :], 1.0), inc=sPv)
    p.op("vector", lambda e: e.memset(ones32r[:, :], 1.0), inc=sPv)
    p.op("vector", lambda e: e.memset(norm_acc[:, :], 0.0), inc=sPv)
    CONST_LD = sCst.n
    CONST_V = sPv.n

    ph1 = contextlib.ExitStack()
    qkvw_sb = ph1.enter_context(nc.sbuf_tensor("qkvw_sb", [128, CCH * ICH, 128], F16))
    xt_sb = [ph1.enter_context(nc.sbuf_tensor(f"xt{i}", [128, ICH, TT], F16)) for i in range(2)]
    stg = [ph1.enter_context(nc.sbuf_tensor(f"stg{i}", [128, TT], F16)) for i in range(3)]
    stgv = [ph1.enter_context(nc.sbuf_tensor(f"stgv{i}", [128, TT], BF16)) for i in range(2)]
    vstg = [ph1.enter_context(nc.sbuf_tensor(f"vstg{i}", [128, TT], BF16)) for i in range(2)]
    sq_sb = [ph1.enter_context(nc.sbuf_tensor(f"sq_sb{i}", [128, TT], F16)) for i in range(2)]
    ps_qkv = ph1.enter_context(nc.psum_tensor("ps_qkv", [128, 3 * TT], F32))
    ps_vt = ph1.enter_context(nc.psum_tensor("ps_vt", [128, TT], BF16))
    ps_qn = ph1.enter_context(nc.psum_tensor("ps_qn", [128, 512], F32))
    ps_bb = ph1.enter_context(nc.psum_tensor("ps_bb", [128, 512], F32))

    for c in range(CCH):
        p.op("gpsimd",
             lambda e, c=c: e.dma_start(
                 out=qkvw_sb[:, c * ICH:(c + 1) * ICH, :],
                 in_=qkvw[:, c * 128:(c + 1) * 128].rearrange("(i p) c -> p i c", p=128)),
             inc=sWld, inc_amt=16)
    QKVW_LD = sWld.n

    panel_mm_done = {}
    qk_spill_val = [0, 0, 0]
    v_spill_val = [0, 0]
    vstg_evac_val = [0, 0]
    sq_mm_val = [0, 0]
    qk_ctr = 0
    v_ctr = 0
    norm_defer = []
    for t in range(NTT):
        cwaits = []
        if t >= 2:
            cwaits.append((sXT, 16 * 32 * (t - 1)))
        p.op("gpsimd", lambda e, t=t: e.dma_start(out=x16d[t * TT:(t + 1) * TT, :],
                                                  in_=x32[t * TT:(t + 1) * TT, :]),
             waits=cwaits, inc=sCast, inc_amt=16)
        waits = [(sCast, 16 * (t + 1))]
        if t >= 2:
            waits.append((sMM, panel_mm_done[t - 2]))
        for i in range(ICH):
            p.op("sync",
                 lambda e, t=t, i=i: e.dma_start(
                     out=xt_sb[t % 2][:, i, :],
                     in_=x16d[t * TT:(t + 1) * TT, i * 128:(i + 1) * 128],
                     transpose=True),
                 waits=waits if i == 0 else (), inc=sXT, inc_amt=16)

        for c in range(CCH):
            tile_idx = t * CCH + c
            bank = tile_idx % 3
            waits = [(sXT, 16 * ICH * (t + 1))]
            if tile_idx == 0:
                waits.append((sWld, QKVW_LD))
                waits.append((sCst, CONST_LD))
                waits.append((sPv, CONST_V))
            if tile_idx >= 3:
                waits.append((sEv, tile_idx - 3 + 1))
            for i in range(ICH):
                p.op("tensor",
                     lambda e, c=c, t=t, i=i, bank=bank: e.matmul(
                         ps_qkv[:, bank * TT:(bank + 1) * TT],
                         qkvw_sb[:, c * ICH + i, :],
                         xt_sb[t % 2][:, i, :],
                         start=(i == 0), stop=(i == ICH - 1)),
                     waits=waits if i == 0 else (),
                     inc=sMM if i == ICH - 1 else None)
            mm_val = sMM.n
            p.flush(norm_defer)

            lh, kind = c // 3, c % 3
            if kind in (0, 1):
                bidx = qk_ctr % 3
                sqb = sq_sb[qk_ctr % 2]
                sq_war = sq_mm_val[qk_ctr % 2]
                qk_ctr += 1
                sb = stg[bidx]
                waits = [(sMM, mm_val), (sCst, CONST_LD)]
                if qk_spill_val[bidx] > 0:
                    waits.append((sSp, qk_spill_val[bidx]))
                p.op("scalar",
                     lambda e, sb=sb, bank=bank, c=c: e.activation(
                         sb[:, :], ps_qkv[:, bank * TT:(bank + 1) * TT],
                         AF.Identity, bias=qkvb_sb[:, c:c + 1], scale=1.0),
                     waits=waits, inc=sEv)
                ev_val = sEv.n
                dst = qTd if kind == 0 else kTd
                p.op("sync",
                     lambda e, sb=sb, dst=dst, lh=lh, t=t: e.dma_start(
                         out=dst[lh * 128:(lh + 1) * 128, t * TT:(t + 1) * TT],
                         in_=sb[:, :]),
                     waits=[(sEv, ev_val)], inc=sSp, inc_amt=16)
                qk_spill_val[bidx] = sSp.n
                waits = [(sEv, ev_val)]
                if sq_war > 0:
                    waits.append((sNm, sq_war))
                p.op("vector",
                     lambda e, sb=sb, sqb=sqb: e.tensor_mul(sqb[:, :], sb[:, :], sb[:, :]),
                     waits=waits, inc=sNq)
                p.op("tensor",
                     lambda e, sqb=sqb: e.matmul(ps_qn[0:1, 0:TT], ones16[:, :], sqb[:, :],
                                                 start=True, stop=True),
                     waits=[(sNq, sNq.n), (sNq, sNq.n)], inc=sNm, defer=norm_defer)
                sq_mm_val[(qk_ctr - 1) % 2] = sNm.n
                acc_col = lh * 2 + kind
                p.op("vector",
                     lambda e: e.reduce_max(rowtmp[0:1, 0:1], ps_qn[0:1, 0:TT], axis=AXX),
                     waits=[(sNm, sNm.n)], inc=sNq)
                p.op("vector",
                     lambda e, acc_col=acc_col: e.tensor_max(
                         norm_acc[0:1, acc_col:acc_col + 1],
                         norm_acc[0:1, acc_col:acc_col + 1],
                         rowtmp[0:1, 0:1]),
                     waits=(), inc=sNq)
            else:
                sbv = stgv[v_ctr % 2]
                vb = vstg[v_ctr % 2]
                waits = [(sMM, mm_val), (sCst, CONST_LD)]
                if vstg_evac_val[v_ctr % 2] > 0:
                    waits.append((sVt, vstg_evac_val[v_ctr % 2]))
                p.op("scalar",
                     lambda e, sbv=sbv, bank=bank, c=c: e.activation(
                         sbv[:, :], ps_qkv[:, bank * TT:(bank + 1) * TT],
                         AF.Identity, bias=qkvb_sb[:, c:c + 1], scale=1.0),
                     waits=waits, inc=sEv)
                ev_val = sEv.n
                for j in range(4):
                    p.op("tensor",
                         lambda e, sbv=sbv, j=j: e.transpose(
                             ps_vt[:, j * 128:(j + 1) * 128],
                             sbv[:, j * 128:(j + 1) * 128], identbf[:, :]),
                         waits=([(sEv, ev_val), (sVe, sVe.n)] if j == 0 else ()),
                         inc=sVt if j == 3 else None)
                vstg_evac_val[v_ctr % 2] = sVt.n
                vt_val = sVt.n
                waits = [(sVt, vt_val)]
                if v_spill_val[v_ctr % 2] > 0:
                    waits.append((sSp, v_spill_val[v_ctr % 2]))
                p.op("vector",
                     lambda e, vb=vb: e.tensor_copy(vb[:, :], ps_vt[:, :]),
                     waits=waits, inc=sVe)
                ve_val = sVe.n
                p.op("sync",
                     lambda e, vb=vb, lh=lh, t=t: e.dma_start(
                         out=vd[t * TT:(t + 1) * TT, lh * 128:(lh + 1) * 128]
                         .rearrange("(j p) d -> p j d", p=128),
                         in_=vb[:, :].rearrange("p (j d) -> p j d", j=4)),
                     waits=[(sVe, ve_val)], inc=sSp, inc_amt=16)
                v_spill_val[v_ctr % 2] = sSp.n
                v_ctr += 1
        panel_mm_done[t] = sMM.n
    p.flush(norm_defer)

    nq0 = sNq.n
    for lh in range(NHL):
        p.op("vector",
             lambda e, lh=lh: e.tensor_mul(rowtmp[0:1, 1:2],
                                           norm_acc[0:1, 2 * lh:2 * lh + 1],
                                           norm_acc[0:1, 2 * lh + 1:2 * lh + 2]),
             waits=[(sNq, nq0)], inc=sNq)
        p.op("scalar",
             lambda e: e.activation(rowtmp[0:1, 2:3], rowtmp[0:1, 1:2], AF.Ln),
             waits=[(sNq, sNq.n)], inc=sBh)
        p.op("scalar",
             lambda e: e.activation(rowtmp[0:1, 3:4], rowtmp[0:1, 2:3], AF.Exp, scale=0.5),
             waits=(), inc=None)
        p.op("scalar",
             lambda e: e.activation(rowtmp[0:1, 5:6], rowtmp[0:1, 3:4], AF.Identity),
             waits=(), inc=sBh)
        p.op("vector",
             lambda e: e.tensor_single_scalar(rowtmp[0:1, 4:5], rowtmp[0:1, 3:4],
                                              -SCALE, ALU.mult),
             waits=[(sBh, sBh.n)], inc=sNq)
        p.op("tensor",
             lambda e: e.matmul(ps_bb[:, 0:1], ones32r[:, :], rowtmp[0:1, 4:5],
                                start=True, stop=True),
             waits=[(sNq, sNq.n), (sBh, sBh.n)], inc=sNm)
        p.op("vector",
             lambda e, lh=lh: e.tensor_copy(bias_h[:, lh:lh + 1], ps_bb[:, 0:1]),
             waits=[(sNm, sNm.n)], inc=sBh)
    BH_DONE = sBh.n
    P1_SPILLS = sSp.n
    p.op("gpsimd", lambda e: e.dma_start(out=x16md[:, :], in_=x32m[:, :]),
         waits=[(sCast, 16 * NTT)], inc=sCastM, inc_amt=16)
    for c in range(8):
        p.op("gpsimd", lambda e, c=c: e.dma_start(out=w16d[c * 512:(c + 1) * 512, :],
                                                  in_=Wt[c * 512:(c + 1) * 512, :]),
             inc=sCastW, inc_amt=16)

    ph1.close()
    ph2 = contextlib.ExitStack()
    kT_sb = [ph2.enter_context(nc.sbuf_tensor(f"kT{i}", [128, S], F16)) for i in range(2)]
    qT_sb = [ph2.enter_context(nc.sbuf_tensor(f"qT{i}", [128, S], F16)) for i in range(2)]
    v_sb = [ph2.enter_context(nc.sbuf_tensor(f"v{i}", [128, KCH, 128], BF16)) for i in range(2)]
    pt_sb = [ph2.enter_context(nc.sbuf_tensor(f"pt{i}", [128, 2 * TQ], BF16)) for i in range(3)]
    tree_bufs = [ph2.enter_context(nc.sbuf_tensor(f"tr{i}", [128, TQ], BF16)) for i in range(16)]
    rrow = ph2.enter_context(nc.sbuf_tensor("rrow", [1, 2 * TQ], F32))
    rb_sb = ph2.enter_context(nc.sbuf_tensor("rb_sb", [128, 2 * TQ], F32))
    ctxn = [ph2.enter_context(nc.sbuf_tensor(f"ctxn{i}", [128, TQ], F16)) for i in range(2)]
    ps_st = [ph2.enter_context(nc.psum_tensor(f"ps_stt{i}", [128, 2 * TQ], F32))
             for i in range(3)]
    ps_ctx2 = ph2.enter_context(nc.psum_tensor("ps_ctx2", [128, 2 * TQ], F32))

    st_free = [0, 0, 0]
    pt_free_pv = [0, 0, 0]
    pt_free_trA = [0, 0, 0]
    pt_free_trB = [0, 0, 0]
    ctxn_dma_val = [0, 0]
    head_pv_done = {}

    def ap_of(tok):
        if tok[0] == "pt":
            return pt_sb[tok[1]][:, tok[2] * TQ:(tok[2] + 1) * TQ]
        return tok[1][:, :]

    class Tree:

        def __init__(self, pool):
            self.free = list(pool)
            self.occ = {}

        def insert_pair_ops(self, tokx, toky):
            ops = []
            outb = self.free.pop()
            ops.append(lambda e, a=tokx, bb=toky, outb=outb: e.tensor_add(
                outb[:, :], ap_of(a), ap_of(bb)))
            cur = ("buf", outb)
            lvl = 0
            while lvl in self.occ:
                prev = self.occ.pop(lvl)
                outb = prev[1]
                ops.append(lambda e, a=cur, bb=prev, outb=outb: e.tensor_add(
                    outb[:, :], ap_of(a), ap_of(bb)))
                if cur[1] is not outb:
                    self.free.append(cur[1])
                cur = ("buf", outb)
                lvl += 1
            self.occ[lvl] = cur
            return ops

        def root(self):
            assert len(self.occ) == 1
            return self.occ.popitem()[1][1]

    for lh in range(NHL):
        b = lh % 2
        waits = [(sSp, P1_SPILLS), (sBh, BH_DONE)]
        if lh >= 2:
            waits.append((sPV, head_pv_done[lh - 2]))
        p.op("sync", lambda e, b=b, lh=lh: e.dma_start(
            out=kT_sb[b][:, :], in_=kTd[lh * 128:(lh + 1) * 128, :]),
            waits=waits, inc=sLd, inc_amt=16)
        p.op("sync", lambda e, b=b, lh=lh: e.dma_start(
            out=qT_sb[b][:, :], in_=qTd[lh * 128:(lh + 1) * 128, :]),
            inc=sLd, inc_amt=16)
        p.op("sync", lambda e, b=b, lh=lh: e.dma_start(
            out=v_sb[b][:, :, :],
            in_=vd[:, lh * 128:(lh + 1) * 128].rearrange("(j p) d -> p j d", p=128)),
            inc=sLd, inc_amt=16)
        ld_val = sLd.n

        for qp in range(NQT // 2):
            qA, qB = 2 * qp, 2 * qp + 1
            ctx_free_val = sEp.n
            treeA = Tree(tree_bufs[0:8])
            treeB = Tree(tree_bufs[8:16])
            exp_vals = {}

            def emit_pv(kc, ctx_free_val=None):
                par = kc % 3
                ptb = pt_sb[par]
                w2 = [(sEX, exp_vals[kc])]
                if kc == 0:
                    w2.append((sEp, ctx_free_val))
                p.op("tensor",
                     lambda e, b=b, kc=kc, ptb=ptb: e.matmul(
                         ps_ctx2[:, 0:TQ], v_sb[b][:, kc, :], ptb[:, 0:TQ],
                         start=(kc == 0), stop=(kc == KCH - 1)),
                     waits=w2, inc=None)
                p.op("tensor",
                     lambda e, b=b, kc=kc, ptb=ptb: e.matmul(
                         ps_ctx2[:, TQ:2 * TQ], v_sb[b][:, kc, :], ptb[:, TQ:2 * TQ],
                         start=(kc == 0), stop=(kc == KCH - 1)),
                     waits=(), inc=sPV)
                pt_free_pv[par] = sPV.n

            for kc in range(KCH):
                par = kc % 3
                stps = ps_st[par]
                ptb = pt_sb[par]
                waits = []
                if kc == 0 and qp == 0:
                    waits.append((sLd, ld_val))
                    if lh == 0:
                        waits.append((sBh, BH_DONE))
                if kc == 0:
                    waits.append((sEp, ctx_free_val))
                if st_free[par] > 0:
                    waits.append((sEX, st_free[par]))
                p.op("tensor",
                     lambda e, b=b, kc=kc, qA=qA, stps=stps: e.matmul(
                         stps[:, 0:TQ],
                         kT_sb[b][:, kc * 128:(kc + 1) * 128],
                         qT_sb[b][:, qA * TQ:(qA + 1) * TQ],
                         start=True, stop=True),
                     waits=waits, inc=None)
                p.op("tensor",
                     lambda e, b=b, kc=kc, qB=qB, stps=stps: e.matmul(
                         stps[:, TQ:2 * TQ],
                         kT_sb[b][:, kc * 128:(kc + 1) * 128],
                         qT_sb[b][:, qB * TQ:(qB + 1) * TQ],
                         start=True, stop=True),
                     waits=(), inc=sST)
                st_val = sST.n
                if kc >= 1:
                    emit_pv(kc - 1, ctx_free_val)

                waits = [(sST, st_val)]
                if pt_free_pv[par] > 0:
                    waits.append((sPV, pt_free_pv[par]))
                    waits.append((sTR, pt_free_trA[par]))
                    waits.append((sTRb, pt_free_trB[par]))
                p.op("scalar",
                     lambda e, ptb=ptb, stps=stps, lh=lh: e.activation(
                         ptb[:, :], stps[:, :], AF.Exp,
                         bias=bias_h[:, lh:lh + 1], scale=SCALE),
                     waits=waits, inc=sEX)
                exp_vals[kc] = sEX.n
                st_free[par] = sEX.n

                if kc % 2 == 1:
                    par0 = (kc - 1) % 3
                    opsA = treeA.insert_pair_ops(("pt", par0, 0), ("pt", par, 0))
                    for oi, fn_ in enumerate(opsA):
                        p.op("vector", fn_,
                             waits=[(sEX, exp_vals[kc])] if oi == 0 else (),
                             inc=sTR if oi == len(opsA) - 1 else None)
                    pt_free_trA[par0] = pt_free_trA[par] = sTR.n
                    opsB = treeB.insert_pair_ops(("pt", par0, 1), ("pt", par, 1))
                    for oi, fn_ in enumerate(opsB):
                        p.op("gpsimd", fn_,
                             waits=[(sEX, exp_vals[kc])] if oi == 0 else (),
                             inc=sTRb if oi == len(opsB) - 1 else None)
                    pt_free_trB[par0] = pt_free_trB[par] = sTRb.n
            emit_pv(KCH - 1)
            pv_all = sPV.n

            rootA, rootB = treeA.root(), treeB.root()
            trA_val, trB_val = sTR.n, sTRb.n
            p.op("tensor",
                 lambda e, rootA=rootA: e.matmul(ps_st[0][0:1, 0:TQ], onesbf[:, :],
                                                 rootA[:, :], start=True, stop=True),
                 waits=[(sTR, trA_val), (sEX, exp_vals[KCH - 1]), (sPv, CONST_V)],
                 inc=None)
            p.op("tensor",
                 lambda e, rootB=rootB: e.matmul(ps_st[1][0:1, 0:TQ], onesbf[:, :],
                                                 rootB[:, :], start=True, stop=True),
                 waits=[(sTRb, trB_val)], inc=sNm)
            mm_val = sNm.n
            p.op("vector",
                 lambda e: e.reciprocal(rrow[0:1, 0:TQ], ps_st[0][0:1, 0:TQ]),
                 waits=[(sNm, mm_val)], inc=None)
            p.op("vector",
                 lambda e: e.reciprocal(rrow[0:1, TQ:2 * TQ], ps_st[1][0:1, 0:TQ]),
                 waits=(), inc=sEp)
            rec_val = sEp.n
            p.op("tensor",
                 lambda e: e.matmul(ps_st[0][:, 0:TQ], ones32r[:, :], rrow[0:1, 0:TQ],
                                    start=True, stop=True),
                 waits=[(sEp, rec_val)], inc=None)
            p.op("tensor",
                 lambda e: e.matmul(ps_st[1][:, 0:TQ], ones32r[:, :], rrow[0:1, TQ:2 * TQ],
                                    start=True, stop=True),
                 waits=(), inc=sNm)
            bc_val = sNm.n
            p.op("vector",
                 lambda e: e.tensor_copy(rb_sb[:, 0:TQ], ps_st[0][:, 0:TQ]),
                 waits=[(sNm, bc_val)], inc=None)
            p.op("vector",
                 lambda e: e.tensor_copy(rb_sb[:, TQ:2 * TQ], ps_st[1][:, 0:TQ]),
                 waits=(), inc=sEp)
            rb_val = sEp.n
            waits = [(sPV, pv_all), (sEp, rb_val)]
            if ctxn_dma_val[0] > 0:
                waits.append((sAG, max(ctxn_dma_val[0], ctxn_dma_val[1])))
            p.op("vector",
                 lambda e: e.tensor_tensor(ctxn[0][:, :], ps_ctx2[:, 0:TQ],
                                           rb_sb[:, 0:TQ], op=ALU.mult),
                 waits=waits, inc=sEp)
            evA = sEp.n
            p.op("vector",
                 lambda e: e.tensor_tensor(ctxn[1][:, :], ps_ctx2[:, TQ:2 * TQ],
                                           rb_sb[:, TQ:2 * TQ], op=ALU.mult),
                 waits=(), inc=sEp)
            evB = sEp.n
            p.op("sync",
                 lambda e, lh=lh, qA=qA: e.dma_start(
                     out=ag_in[lh * 128:(lh + 1) * 128, qA * TQ:(qA + 1) * TQ],
                     in_=ctxn[0][:, :]),
                 waits=[(sEp, evA)], inc=sAG, inc_amt=16)
            ctxn_dma_val[0] = sAG.n
            p.op("sync",
                 lambda e, lh=lh, qB=qB: e.dma_start(
                     out=ag_in[lh * 128:(lh + 1) * 128, qB * TQ:(qB + 1) * TQ],
                     in_=ctxn[1][:, :]),
                 waits=[(sEp, evB)], inc=sAG, inc_amt=16)
            ctxn_dma_val[1] = sAG.n
        head_pv_done[lh] = sPV.n

    AG_ALL = sAG.n

    p.op("gpsimd",
         lambda e: e.collective_compute(
             "AllGather", ALU.bypass, replica_groups=[list(range(NCORES))],
             ins=[ag_in.opt()], outs=[ag_out.opt()]),
         waits=[(sAG, AG_ALL)], inc=sCC, inc_amt=1)

    ph2.close()
    ph3 = contextlib.ExitStack()
    y1_16 = ph3.enter_context(nc.sbuf_tensor("y1_16", [128, OCH, TLOC], F16))
    idx_sb = ph3.enter_context(nc.sbuf_tensor("idx_sb", [128, OCH], I32))
    gstg = [ph3.enter_context(nc.sbuf_tensor(f"gstg{i}", [128, TLOC], F16)) for i in range(2)]
    p.op("sync", lambda e: e.dma_start(out=idx_sb[:, :], in_=gidx[:, :]),
         waits=[(sAG, AG_ALL)], inc=sIdx, inc_amt=16)
    gstg_out_val = [0, 0]
    for c in range(OCH):
        waits = [(sIdx, 16), (sCC, 1)] if c == 0 else []
        if gstg_out_val[c % 2] > 0:
            waits.append((sGa, gstg_out_val[c % 2]))
        p.op("gpsimd",
             lambda e, c=c: e.indirect_dma_start(
                 gstg[c % 2][:, :], None, ag_out_flat[0:128, :],
                 IndirectOffsetOnAxis(idx_sb[:, c:c + 1], axis=0)),
             waits=waits, inc=sGa, inc_amt=16)
        gv = sGa.n
        p.op("gpsimd",
             lambda e, c=c: e.dma_start(
                 out=ctx_mine_d[c * 128:(c + 1) * 128, :], in_=gstg[c % 2][:, :]),
             waits=[(sGa, gv)], inc=sGa, inc_amt=16)
        gstg_out_val[c % 2] = sGa.n
    GATHER_ALL = sGa.n

    pA = contextlib.ExitStack()
    ctxp = pA.enter_context(nc.sbuf_tensor("ctxp", [128, OCH, SUB], F16))
    owp = [pA.enter_context(nc.sbuf_tensor(f"owp{i}", [128, ICH, 128], F16)) for i in range(2)]
    xtp = [pA.enter_context(nc.sbuf_tensor(f"xtp{i}", [128, SUB], F16)) for i in range(2)]
    ysb = pA.enter_context(nc.sbuf_tensor("ysb", [128, OCH, SUB], F32))
    y16t = pA.enter_context(nc.sbuf_tensor("y16t", [128, SUB], F16))
    sq16t = pA.enter_context(nc.sbuf_tensor("sq16t", [128, SUB], F16))
    obt_sb = pA.enter_context(nc.sbuf_tensor("obt_sb", [128, OCH], F32))
    mrow = pA.enter_context(nc.sbuf_tensor("mrow", [1, 2 * SUB], F32))
    vrow = pA.enter_context(nc.sbuf_tensor("vrow", [1, 2 * SUB], F32))
    mb_sb = pA.enter_context(nc.sbuf_tensor("mb_sb", [128, SUB], F32))
    rstdb = pA.enter_context(nc.sbuf_tensor("rstdb", [128, SUB], F32))
    tnorm = [pA.enter_context(nc.sbuf_tensor(f"tnorm{i}", [128, SUB], F32)) for i in range(2)]
    ps_op = pA.enter_context(nc.psum_tensor("ps_op", [128, 2 * SUB], F32))
    ps_st1 = pA.enter_context(nc.psum_tensor("ps_st1", [128, 2 * SUB], F32))
    ps_bc = pA.enter_context(nc.psum_tensor("ps_bc", [128, SUB], F32))

    p.op("sync", lambda e: e.dma_start(out=obt_sb[:, :], in_=obt[:, :]),
         waits=[(sAG, AG_ALL)], inc=sCst, inc_amt=16)
    OBT_LD = sCst.n

    ow_war = {}
    xt_war = {}
    evac_war = {}
    tnorm_war = [0, 0]
    ga_war = 0
    for sub in range(NSUB):
        waits = [(sGa, GATHER_ALL)]
        if sub > 0:
            waits.append((sPm, ga_war))
        p.op("sync",
             lambda e, sub=sub: e.dma_start(
                 out=ctxp[:, :, :],
                 in_=ctx_mine_d[:, sub * SUB:(sub + 1) * SUB]
                 .rearrange("(c p) t -> p c t", p=128)),
             waits=waits, inc=sGa, inc_amt=16)
        ga_val = sGa.n

        for c in range(OCH):
            ci = sub * OCH + c
            ob_ = owp[ci % 2]
            waits = []
            if ci >= 2:
                waits.append((sPm, ow_war[ci - 2]))
            p.op("gpsimd",
                 lambda e, ob_=ob_, c=c: e.dma_start(
                     out=ob_[:, :, :],
                     in_=ow[:, c * 128:(c + 1) * 128].rearrange("(i p) c -> p i c", p=128)),
                 waits=waits, inc=sOwl, inc_amt=16)
            owl_val = sOwl.n
            xb = xtp[ci % 2]
            waits = [(sCastM, 16)]
            if ci >= 2:
                waits.append((sPv, xt_war[ci - 2]))
            p.op("sync",
                 lambda e, xb=xb, c=c, sub=sub: e.dma_start(
                     out=xb[:, :],
                     in_=x16md[sub * SUB:(sub + 1) * SUB, c * 128:(c + 1) * 128],
                     transpose=True),
                 waits=waits, inc=sXtp, inc_amt=16)
            xtp_val = sXtp.n
            bank = ci % 2
            waits = [(sOwl, owl_val), (sGa, ga_val)]
            if ci >= 2:
                waits.append((sPe, evac_war[ci - 2]))
            for i in range(ICH):
                p.op("tensor",
                     lambda e, ob_=ob_, i=i, bank=bank: e.matmul(
                         ps_op[:, bank * SUB:(bank + 1) * SUB],
                         ob_[:, i, :], ctxp[:, i, :],
                         start=(i == 0), stop=(i == ICH - 1)),
                     waits=waits if i == 0 else (),
                     inc=sPm if i == ICH - 1 else None)
            ow_war[ci] = sPm.n
            mm_val = sPm.n
            tn = tnorm[ci % 2]
            waits = [(sPm, mm_val), (sCst, OBT_LD)]
            if tnorm_war[ci % 2] > 0:
                waits.append((sPv, tnorm_war[ci % 2]))
            p.op("scalar",
                 lambda e, tn=tn, bank=bank, c=c: e.activation(
                     tn[:, :], ps_op[:, bank * SUB:(bank + 1) * SUB],
                     AF.Identity, bias=obt_sb[:, c:c + 1], scale=1.0),
                 waits=waits, inc=sPe)
            evac_war[ci] = sPe.n
            p.op("vector",
                 lambda e, c=c, xb=xb, tn=tn: e.tensor_add(ysb[:, c, :], tn[:, :], xb[:, :]),
                 waits=[(sPe, sPe.n), (sXtp, xtp_val)], inc=sPv)
            tnorm_war[ci % 2] = sPv.n
            xt_war[ci] = sPv.n
            p.op("vector",
                 lambda e, c=c: e.tensor_copy(y16t[:, :], ysb[:, c, :]),
                 waits=(), inc=sPv)
            p.op("vector",
                 lambda e: e.tensor_mul(sq16t[:, :], y16t[:, :], y16t[:, :]),
                 waits=(), inc=sPv)
            sqv = sPv.n
            p.op("tensor",
                 lambda e, c=c: e.matmul(ps_st1[0:1, 0:SUB], ones16[:, :], y16t[:, :],
                                         start=(c == 0), stop=(c == OCH - 1)),
                 waits=[(sPv, sqv)],
                 inc=None)
            p.op("tensor",
                 lambda e, c=c: e.matmul(ps_st1[0:1, SUB:2 * SUB], ones16[:, :], sq16t[:, :],
                                         start=(c == 0), stop=(c == OCH - 1)),
                 waits=(),
                 inc=sPm if c == OCH - 1 else None)
        ga_war = sPm.n
        st_done = sPm.n
        p.op("vector",
             lambda e: e.tensor_single_scalar(mrow[0:1, 0:SUB], ps_st1[0:1, 0:SUB],
                                              1.0 / D, ALU.mult),
             waits=[(sPm, st_done)], inc=sPv)
        p.op("vector",
             lambda e: e.tensor_single_scalar(mrow[0:1, SUB:2 * SUB], ps_st1[0:1, SUB:2 * SUB],
                                              1.0 / D, ALU.mult),
             waits=(), inc=sPv)
        p.op("vector",
             lambda e: e.tensor_mul(vrow[0:1, 0:SUB], mrow[0:1, 0:SUB], mrow[0:1, 0:SUB]),
             waits=(), inc=sPv)
        p.op("vector",
             lambda e: e.tensor_sub(vrow[0:1, 0:SUB], mrow[0:1, SUB:2 * SUB],
                                    vrow[0:1, 0:SUB]),
             waits=(), inc=sPv)
        p.op("vector",
             lambda e: e.tensor_single_scalar(vrow[0:1, 0:SUB], vrow[0:1, 0:SUB],
                                              EPS, ALU.add),
             waits=(), inc=sPv)
        varv = sPv.n
        p.op("scalar",
             lambda e: e.activation(vrow[0:1, SUB:2 * SUB], vrow[0:1, 0:SUB], AF.Ln),
             waits=[(sPv, varv)], inc=sPs)
        p.op("scalar",
             lambda e: e.activation(vrow[0:1, 0:SUB], vrow[0:1, SUB:2 * SUB], AF.Exp,
                                    scale=-0.5),
             waits=(), inc=None)
        p.op("scalar",
             lambda e: e.activation(vrow[0:1, SUB:2 * SUB], vrow[0:1, 0:SUB], AF.Identity),
             waits=(), inc=sPs)
        rstd_val = sPs.n
        p.op("tensor",
             lambda e: e.matmul(ps_bc[:, :], ones32r[:, :], mrow[0:1, 0:SUB],
                                start=True, stop=True),
             waits=[(sPs, rstd_val)], inc=sPm)
        p.op("vector",
             lambda e: e.tensor_copy(mb_sb[:, :], ps_bc[:, :]),
             waits=[(sPm, sPm.n)], inc=sPv)
        p.op("tensor",
             lambda e: e.matmul(ps_bc[:, :], ones32r[:, :], vrow[0:1, 0:SUB],
                                start=True, stop=True),
             waits=[(sPv, sPv.n)], inc=sPm)
        p.op("vector",
             lambda e: e.tensor_copy(rstdb[:, :], ps_bc[:, :]),
             waits=[(sPm, sPm.n)], inc=sPv)
        bc_done = sPv.n
        for c in range(OCH):
            tn = tnorm[c % 2]
            waits = [(sPv, bc_done)]
            p.op("vector",
                 lambda e, c=c, tn=tn: e.tensor_sub(tn[:, :], ysb[:, c, :], mb_sb[:, :]),
                 waits=waits, inc=None)
            p.op("vector",
                 lambda e, c=c, sub=sub, tn=tn: e.tensor_mul(
                     y1_16[:, c, sub * SUB:(sub + 1) * SUB], tn[:, :], rstdb[:, :]),
                 waits=(), inc=sPv)
        tnorm_war = [sPv.n, sPv.n]
    Y1_DONE = sPv.n
    pA.close()

    pBs = contextlib.ExitStack()
    bm16 = pBs.enter_context(nc.sbuf_tensor("bm16", [32, D], F16))
    pr16 = pBs.enter_context(nc.sbuf_tensor("pr16", [32, NSUB, SUB], F16))
    pB = contextlib.ExitStack()
    a16 = pB.enter_context(nc.sbuf_tensor("a16", [32, D], F16))
    aT16 = pB.enter_context(nc.sbuf_tensor("aT16", [128, ICH, 32], F16))
    wr16 = pB.enter_context(nc.sbuf_tensor("wr16", [4, D], F16))
    wrT16 = pB.enter_context(nc.sbuf_tensor("wrT16", [128, ICH, 4], F16))
    sel16 = pB.enter_context(nc.sbuf_tensor("sel16", [4, 32], F16))
    gl32 = pB.enter_context(nc.sbuf_tensor("gl32", [4, SUB], F32))
    ge32 = pB.enter_context(nc.sbuf_tensor("ge32", [4, SUB], F32))
    rb4 = pB.enter_context(nc.sbuf_tensor("rb4", [4, SUB], F32))
    grow2 = pB.enter_context(nc.sbuf_tensor("grow2", [1, SUB], F32))
    ones4 = pB.enter_context(nc.sbuf_tensor("ones4", [4, 1], F32))
    gem16 = pB.enter_context(nc.sbuf_tensor("gem16", [4, SUB], F16))
    grep_sb = pB.enter_context(nc.sbuf_tensor("grep_sb", [32, SUB], F32))
    ps_g = pB.enter_context(nc.psum_tensor("ps_g", [128, 512], F32))
    ps_g2 = pB.enter_context(nc.psum_tensor("ps_g2", [128, 512], F32))
    ps_tA = pB.enter_context(nc.psum_tensor("ps_tA", [128, 64], F16))

    p.op("vector", lambda e: e.memset(ones4[:, :], 1.0),
         waits=[(sPv, Y1_DONE)], inc=sPv)
    CONST_V = sPv.n
    p.op("gpsimd", lambda e: e.dma_start(out=a16[:, :], in_=A[:, :]),
         waits=[(sPv, Y1_DONE)], inc=sGa, inc_amt=16)
    p.op("gpsimd", lambda e: e.dma_start(out=wr16[:, :], in_=Wr[:, :]), inc=sGa, inc_amt=16)
    p.op("gpsimd", lambda e: e.dma_start(out=bm16[:, :], in_=BmT[:, :]), inc=sGa, inc_amt=16)
    p.op("sync", lambda e: e.dma_start(out=sel16[:, :], in_=sel[:, :]),
         waits=[(sPv, Y1_DONE)], inc=sCst, inc_amt=16)
    SMALL_G = sGa.n
    SMALL_C = sCst.n
    for i in range(ICH):
        p.op("tensor",
             lambda e, i=i: e.transpose(ps_tA[:, 0:32],
                                        a16[:, i * 128:(i + 1) * 128], ident16[0:32, 0:32]),
             waits=[(sGa, SMALL_G), (sCst, CONST_LD), (sPv, sPv.n)], inc=sPm)
        p.op("vector",
             lambda e, i=i: e.tensor_copy(aT16[:, i, :], ps_tA[:, 0:32]),
             waits=[(sPm, sPm.n)], inc=sPv)
        p.op("tensor",
             lambda e, i=i: e.transpose(ps_tA[:, 32:36],
                                        wr16[:, i * 128:(i + 1) * 128], ident16[0:4, 0:4]),
             waits=[(sPv, sPv.n)], inc=sPm)
        p.op("vector",
             lambda e, i=i: e.tensor_copy(wrT16[:, i, :], ps_tA[:, 32:36]),
             waits=[(sPm, sPm.n)], inc=sPv)
    TRS_V = sPv.n

    for sub in range(NSUB):
        for i in range(ICH):
            p.op("tensor",
                 lambda e, i=i, sub=sub: e.matmul(
                     ps_g[0:4, 0:SUB], wrT16[:, i, :],
                     y1_16[:, i, sub * SUB:(sub + 1) * SUB],
                     start=(i == 0), stop=(i == ICH - 1)),
                 waits=[(sPv, TRS_V)] if i == 0 else (),
                 inc=sPm if i == ICH - 1 else None)
        p.op("vector",
             lambda e: e.tensor_copy(gl32[:, :], ps_g[0:4, 0:SUB]),
             waits=[(sPm, sPm.n)], inc=sPv)
        glv = sPv.n
        p.op("scalar",
             lambda e: e.activation(ge32[:, :], gl32[:, :], AF.Exp),
             waits=[(sPv, glv)], inc=None)
        p.op("scalar",
             lambda e: e.activation(rb4[:, :], ge32[:, :], AF.Identity),
             waits=(), inc=sPs)
        exp_drain = sPs.n
        p.op("tensor",
             lambda e: e.matmul(ps_g2[0:1, 0:SUB], ones4[:, :], ge32[:, :],
                                start=True, stop=True),
             waits=[(sPs, exp_drain), (sPv, CONST_V)], inc=sPm)
        p.op("vector",
             lambda e: e.reciprocal(grow2[0:1, :], ps_g2[0:1, 0:SUB]),
             waits=[(sPm, sPm.n)], inc=sPv)
        p.op("tensor",
             lambda e: e.matmul(ps_g2[0:4, 0:SUB], ones32r[0:1, 0:4], grow2[0:1, :],
                                start=True, stop=True),
             waits=[(sPv, sPv.n)], inc=sPm)
        p.op("vector",
             lambda e: e.tensor_copy(rb4[:, :], ps_g2[0:4, 0:SUB]),
             waits=[(sPm, sPm.n)], inc=sPv)
        p.op("vector",
             lambda e: e.tensor_tensor(gem16[:, :], ge32[:, :], rb4[:, :], op=ALU.mult),
             waits=(), inc=sPv)
        gem_done = sPv.n
        p.op("tensor",
             lambda e: e.matmul(ps_g[0:32, 0:SUB], sel16[:, :], gem16[:, :],
                                start=True, stop=True),
             waits=[(sPv, gem_done), (sCst, SMALL_C)], inc=sPm)
        p.op("vector",
             lambda e: e.tensor_copy(grep_sb[:, :], ps_g[0:32, 0:SUB]),
             waits=[(sPm, sPm.n)], inc=sPv)
        for i in range(ICH):
            p.op("tensor",
                 lambda e, i=i, sub=sub: e.matmul(
                     ps_g2[0:32, 0:SUB], aT16[:, i, :],
                     y1_16[:, i, sub * SUB:(sub + 1) * SUB],
                     start=(i == 0), stop=(i == ICH - 1)),
                 waits=[(sPv, sPv.n)] if i == 0 else (),
                 inc=sPm if i == ICH - 1 else None)
        p.op("vector",
             lambda e, sub=sub: e.tensor_tensor(pr16[:, sub, :], ps_g2[0:32, 0:SUB],
                                                grep_sb[:, :], op=ALU.mult),
             waits=[(sPm, sPm.n)], inc=sPv)
    PR_DONE = sPv.n
    pB.close()

    pC = contextlib.ExitStack()
    y2_16 = pC.enter_context(nc.sbuf_tensor("y2_16", [128, OCH, TLOC], F16, side="right"))
    ps_st2 = pC.enter_context(nc.psum_tensor("ps_st2", [128, 2 * NSUB * SUB], F32, side="right"))
    pW = contextlib.ExitStack()
    OSUP = 256
    wtp = [pW.enter_context(nc.sbuf_tensor(f"wtp{i}", [128, ICH, OSUP], F16)) for i in range(2)]
    ftmp = [pW.enter_context(nc.sbuf_tensor(f"ftmp{i}", [128, SUB], F32)) for i in range(2)]
    fsq = pW.enter_context(nc.sbuf_tensor("fsq", [128, SUB], F16))
    bt_sb = pW.enter_context(nc.sbuf_tensor("bt_sb", [128, OCH], F32))
    ps_f = pW.enter_context(nc.psum_tensor("ps_f", [128, 2 * SUB], F32))

    p.op("sync", lambda e: e.dma_start(out=bt_sb[:, :], in_=bt[:, :]),
         waits=[(sPv, PR_DONE)], inc=sCst, inc_amt=16)
    BT_LD = sCst.n

    NOS = D // OSUP
    OCPS = OSUP // 128
    wt_war = {}
    f_evac_war = {}
    ftmp_war = [0, 0]
    for osup in range(NOS):
        wb = wtp[osup % 2]
        waits = [(sCastW, 16 * 8)]
        if osup == 0:
            waits.append((sPv, PR_DONE))
        if osup >= 2:
            waits.append((sPm, wt_war[osup - 2]))
        for i in range(ICH):
            p.op("sync",
                 lambda e, wb=wb, osup=osup, i=i: e.dma_start(
                     out=wb[:, i, :],
                     in_=w16d[osup * OSUP:(osup + 1) * OSUP, i * 128:(i + 1) * 128],
                     transpose=True),
                 waits=waits if i == 0 else (), inc=sWtl, inc_amt=16)
        wt_val = sWtl.n
        for sub in range(NSUB):
            for oc in range(OCPS):
                og = osup * OCPS + oc
                fi = (osup * NSUB + sub) * OCPS + oc
                bank = fi % 2
                waits = [(sWtl, wt_val), (sPv, PR_DONE)]
                if fi >= 2:
                    waits.append((sPe, f_evac_war[fi - 2]))
                for i in range(ICH):
                    p.op("tensor",
                         lambda e, wb=wb, i=i, oc=oc, sub=sub, bank=bank: e.matmul(
                             ps_f[:, bank * SUB:(bank + 1) * SUB],
                             wb[:, i, oc * 128:(oc + 1) * 128],
                             y1_16[:, i, sub * SUB:(sub + 1) * SUB],
                             start=(i == 0), stop=False),
                         waits=waits if i == 0 else (), inc=None)
                p.op("tensor",
                     lambda e, og=og, sub=sub, bank=bank: e.matmul(
                         ps_f[:, bank * SUB:(bank + 1) * SUB],
                         bm16[:, og * 128:(og + 1) * 128],
                         pr16[:, sub, :],
                         start=False, stop=True),
                     waits=(), inc=sPm)
                mm_val = sPm.n
                ft = ftmp[fi % 2]
                waits = [(sPm, mm_val), (sCst, BT_LD)]
                if ftmp_war[fi % 2] > 0:
                    waits.append((sPv, ftmp_war[fi % 2]))
                p.op("scalar",
                     lambda e, ft=ft, bank=bank, og=og: e.activation(
                         ft[:, :], ps_f[:, bank * SUB:(bank + 1) * SUB],
                         AF.Identity, bias=bt_sb[:, og:og + 1], scale=1.0),
                     waits=waits, inc=sPe)
                f_evac_war[fi] = sPe.n
                p.op("vector",
                     lambda e, og=og, sub=sub, ft=ft: e.tensor_add(
                         y2_16[:, og, sub * SUB:(sub + 1) * SUB],
                         ft[:, :],
                         y1_16[:, og, sub * SUB:(sub + 1) * SUB]),
                     waits=[(sPe, sPe.n)], inc=sPv)
                ftmp_war[fi % 2] = sPv.n
                p.op("vector",
                     lambda e, og=og, sub=sub: e.tensor_mul(
                         fsq[:, :],
                         y2_16[:, og, sub * SUB:(sub + 1) * SUB],
                         y2_16[:, og, sub * SUB:(sub + 1) * SUB]),
                     waits=(), inc=sPv)
                sq_val = sPv.n
                first = (osup == 0 and oc == 0)
                last = (osup == NOS - 1 and oc == OCPS - 1)
                p.op("tensor",
                     lambda e, og=og, sub=sub, first=first, last=last: e.matmul(
                         ps_st2[0:1, (2 * sub) * SUB:(2 * sub + 1) * SUB], ones16[:, :],
                         y2_16[:, og, sub * SUB:(sub + 1) * SUB],
                         start=first, stop=last),
                     waits=[(sPv, sq_val)], inc=None)
                p.op("tensor",
                     lambda e, sub=sub, first=first, last=last: e.matmul(
                         ps_st2[0:1, (2 * sub + 1) * SUB:(2 * sub + 2) * SUB],
                         ones16[:, :], fsq[:, :],
                         start=first, stop=last),
                     waits=(), inc=sPm if last else None)
        wt_war[osup] = sPm.n
    FFN_DONE = sPm.n
    pW.close()
    pBs.close()
    ph3.close()

    pD = contextlib.ExitStack()
    m2row = pD.enter_context(nc.sbuf_tensor("m2row", [1, 2 * SUB], F32))
    v2row = pD.enter_context(nc.sbuf_tensor("v2row", [1, 2 * SUB], F32))
    mtok = pD.enter_context(nc.sbuf_tensor("mtok", [128, 2], F32))
    y2tok = pD.enter_context(nc.sbuf_tensor("y2tok", [128, OCH * 128], F16))
    fin32 = [pD.enter_context(nc.sbuf_tensor(f"fin{i}", [128, D], F32)) for i in range(2)]
    ps_t = pD.enter_context(nc.psum_tensor("ps_t", [128, 1024], F32))
    ps_t16 = ps_t.ap().bitcast(F16)

    fin_dma_val = [0, 0]
    for sub in range(NSUB):
        p.op("vector",
             lambda e, sub=sub: e.tensor_single_scalar(
                 m2row[0:1, 0:SUB], ps_st2[0:1, (2 * sub) * SUB:(2 * sub + 1) * SUB],
                 1.0 / D, ALU.mult),
             waits=[(sPm, FFN_DONE)], inc=sPv)
        p.op("vector",
             lambda e, sub=sub: e.tensor_single_scalar(
                 m2row[0:1, SUB:2 * SUB],
                 ps_st2[0:1, (2 * sub + 1) * SUB:(2 * sub + 2) * SUB],
                 1.0 / D, ALU.mult),
             waits=(), inc=sPv)
        p.op("vector",
             lambda e: e.tensor_mul(v2row[0:1, 0:SUB], m2row[0:1, 0:SUB],
                                    m2row[0:1, 0:SUB]),
             waits=(), inc=sPv)
        p.op("vector",
             lambda e: e.tensor_sub(v2row[0:1, 0:SUB], m2row[0:1, SUB:2 * SUB],
                                    v2row[0:1, 0:SUB]),
             waits=(), inc=sPv)
        p.op("vector",
             lambda e: e.tensor_single_scalar(v2row[0:1, 0:SUB], v2row[0:1, 0:SUB],
                                              EPS, ALU.add),
             waits=(), inc=sPv)
        varv = sPv.n
        p.op("scalar",
             lambda e: e.activation(v2row[0:1, SUB:2 * SUB], v2row[0:1, 0:SUB], AF.Ln),
             waits=[(sPv, varv)], inc=sPs)
        p.op("scalar",
             lambda e: e.activation(v2row[0:1, 0:SUB], v2row[0:1, SUB:2 * SUB], AF.Exp,
                                    scale=-0.5),
             waits=(), inc=None)
        p.op("scalar",
             lambda e: e.activation(v2row[0:1, SUB:2 * SUB], v2row[0:1, 0:SUB], AF.Identity),
             waits=(), inc=sPs)
        row_val = sPs.n
        for j in range(SUB // 128):
            p.op("tensor",
                 lambda e, j=j: e.transpose(ps_t.ap()[:, 0:1],
                                            m2row[0:1, j * 128:(j + 1) * 128],
                                            ident32[0:1, 0:1]),
                 waits=[(sPs, row_val), (sPv, sPv.n)], inc=sPm)
            p.op("vector",
                 lambda e: e.tensor_copy(mtok[:, 0:1], ps_t.ap()[:, 0:1]),
                 waits=[(sPm, sPm.n)], inc=sPv)
            p.op("tensor",
                 lambda e, j=j: e.transpose(ps_t.ap()[:, 0:1],
                                            v2row[0:1, j * 128:(j + 1) * 128],
                                            ident32[0:1, 0:1]),
                 waits=[(sPv, sPv.n)], inc=sPm)
            p.op("vector",
                 lambda e: e.tensor_copy(mtok[:, 1:2], ps_t.ap()[:, 0:1]),
                 waits=[(sPm, sPm.n)], inc=sPv)
            tr_copy_val = [sPv.n, sPv.n]
            for oc in range(OCH):
                par = oc % 2
                reg = ps_t16[:, 0:128] if par == 0 else ps_t16[:, 1024:1152]
                p.op("tensor",
                     lambda e, oc=oc, sub=sub, j=j, reg=reg: e.transpose(
                         reg,
                         y2_16[:, oc, sub * SUB + j * 128: sub * SUB + (j + 1) * 128],
                         ident16[:, :]),
                     waits=[(sPv, tr_copy_val[par])], inc=sPm)
                p.op("vector",
                     lambda e, oc=oc, reg=reg: e.tensor_copy(
                         y2tok[:, oc * 128:(oc + 1) * 128], reg),
                     waits=[(sPm, sPm.n)], inc=sPv)
                tr_copy_val[par] = sPv.n
            fb = fin32[j % 2]
            waits = [(sPv, sPv.n)]
            if fin_dma_val[j % 2] > 0:
                waits.append((sOut, fin_dma_val[j % 2]))
            p.op("vector",
                 lambda e, fb=fb: e.tensor_scalar(
                     fb[:, :], y2tok[:, :],
                     mtok[:, 0:1], mtok[:, 1:2],
                     op0=ALU.subtract, op1=ALU.mult),
                 waits=waits, inc=sPv)
            p.op("gpsimd",
                 lambda e, fb=fb, sub=sub, j=j: e.dma_start(
                     out=y_out[sub * SUB + j * 128: sub * SUB + (j + 1) * 128, :],
                     in_=fb[:, :]),
                 waits=[(sPv, sPv.n)], inc=sOut, inc_amt=16)
            fin_dma_val[j % 2] = sOut.n

    p.op("gpsimd", lambda e: None, waits=[(sOut, sOut.n)])

    with nc.Block() as block:
        @block.sync
        def _(e):
            for f in p.q["sync"]:
                f(e)

        @block.tensor
        def _(e):
            for f in p.q["tensor"]:
                f(e)

        @block.vector
        def _(e):
            for f in p.q["vector"]:
                f(e)

        @block.scalar
        def _(e):
            for f in p.q["scalar"]:
                f(e)

        @block.gpsimd
        def _(e):
            for f in p.q["gpsimd"]:
                f(e)

    return nc


def kernel(L0_attn_x, L0_attn_attn_qkvw, L0_attn_attn_qkvb, L0_attn_attn_ow,
           L0_attn_attn_ob, L0_ffn_W, L0_ffn_b, L0_ffn_A, L0_ffn_Bm, L0_ffn_Wr):
    import ml_dtypes
    x = np.ascontiguousarray(np.asarray(L0_attn_x, dtype=np.float32).reshape(S, D))
    qkvw = np.asarray(L0_attn_attn_qkvw, dtype=np.float32)
    qkvb = np.asarray(L0_attn_attn_qkvb, dtype=np.float32)
    ow = np.ascontiguousarray(np.asarray(L0_attn_attn_ow, dtype=np.float32))
    ob = np.asarray(L0_attn_attn_ob, dtype=np.float32)
    W = np.ascontiguousarray(np.asarray(L0_ffn_W, dtype=np.float32))
    b = np.asarray(L0_ffn_b, dtype=np.float32)
    A = np.ascontiguousarray(np.asarray(L0_ffn_A, dtype=np.float32).reshape(32, D))
    Bm = np.asarray(L0_ffn_Bm, dtype=np.float32)
    Wr = np.ascontiguousarray(np.asarray(L0_ffn_Wr, dtype=np.float32))

    BmT = np.ascontiguousarray(Bm.transpose(0, 2, 1).reshape(32, D))
    sel = np.zeros((4, 32), dtype=np.float16)
    for e_ in range(4):
        sel[e_, e_ * 8:(e_ + 1) * 8] = 2.0
    id16 = np.eye(128, dtype=np.float16)
    idbf = np.eye(128).astype(ml_dtypes.bfloat16).view(np.uint16)
    id32 = np.eye(128, dtype=np.float32)
    obt = np.ascontiguousarray(ob.reshape(OCH, 128).T)
    btt = np.ascontiguousarray(b.reshape(OCH, 128).T)

    nc = build_nc()

    in_maps = []
    for r in range(NCORES):
        heads = range(NHL * r, NHL * (r + 1))
        cols = []
        bcols = []
        for h in heads:
            for k in range(3):
                cols.append(qkvw[:, k * D + h * 128:k * D + (h + 1) * 128])
                bcols.append(qkvb[k * D + h * 128:k * D + (h + 1) * 128])
        qkvw_r = np.ascontiguousarray(np.concatenate(cols, axis=1))
        qkvb_r = np.ascontiguousarray(np.stack(bcols, axis=1))
        gi = np.zeros((128, OCH), dtype=np.int32)
        for c in range(OCH):
            f = c * 128 + np.arange(128)
            gi[:, c] = f * S + r * TLOC
        in_maps.append({
            "x32": x,
            "x32m": np.ascontiguousarray(x[r * TLOC:(r + 1) * TLOC, :]),
            "qkvw": qkvw_r,
            "qkvb": qkvb_r,
            "ow": ow,
            "obt": obt,
            "W": W,
            "bt": btt,
            "A": A,
            "BmT": BmT,
            "Wr": Wr,
            "sel": sel,
            "id16": id16,
            "idbf": idbf,
            "id32": id32,
            "gidx": gi,
        })

    res = run_bass_kernel_spmd(nc, in_maps, core_ids=list(range(NCORES)))
    out = np.concatenate([res.results[r]["y_out"] for r in range(NCORES)], axis=0)
    return out.reshape(1, S, D).astype(np.float32)


# version 12
# speedup vs baseline: 1.1120x; 1.1120x over previous
import contextlib
import math
import numpy as np

import concourse.bass as bass
import concourse.mybir as mybir
from concourse.bass import IndirectOffsetOnAxis
from concourse.bass_utils import run_bass_kernel_spmd

F32 = mybir.dt.float32
F16 = mybir.dt.float16
BF16 = mybir.dt.bfloat16
I32 = mybir.dt.int32
AF = mybir.ActivationFunctionType
ALU = mybir.AluOpType
AXX = mybir.AxisListType.X

NCORES = 8
S = 8192
D = 4096
HD = 128
NHL = 4
TLOC = S // NCORES
SCALE = 1.0 / math.sqrt(HD)
EPS = 1e-5

CCH = 3 * NHL
ICH = D // 128
TT = 512
NTT = S // TT
TQ = 512
NQT = S // TQ
KCH = S // 128
GS = 3
SUB = 512
NSUB = TLOC // SUB
OCH = D // 128


class Sem:
    def __init__(self, handle, name):
        self.h = handle
        self.name = name
        self.n = 0


class Prog:
    QUEUES = ("sync", "tensor", "vector", "scalar", "gpsimd")

    def __init__(self):
        self.q = {k: [] for k in self.QUEUES}

    def op(self, qn, fn, waits=(), inc=None, inc_amt=1, defer=None):
        for s, v in waits:
            assert v <= s.n, f"wait on {s.name} for {v} but only {s.n} recorded"
        wl = [(s.h, v) for s, v in waits if v > 0]
        if inc is not None:
            inc.n += inc_amt
            ih, ia = inc.h, inc_amt
        else:
            ih, ia = None, 0

        def run(eng, wl=wl, fn=fn, ih=ih, ia=ia):
            for h, v in wl:
                eng.wait_ge(h, v)
            ins = fn(eng)
            if ih is not None:
                if ins is None:
                    ins = eng.nop()
                ins.then_inc(ih, ia)
        if defer is not None:
            defer.append((qn, run))
        else:
            self.q[qn].append(run)

    def flush(self, defer):
        for qn, run in defer:
            self.q[qn].append(run)
        defer.clear()


def build_nc():
    nc = bass.Bass("TRN2", target_bir_lowering=False)

    x32 = nc.dram_tensor("x32", [S, D], F32, kind="ExternalInput").ap()
    x32m = nc.dram_tensor("x32m", [TLOC, D], F32, kind="ExternalInput").ap()
    qkvw = nc.dram_tensor("qkvw", [D, CCH * 128], F32, kind="ExternalInput").ap()
    qkvb = nc.dram_tensor("qkvb", [128, CCH], F32, kind="ExternalInput").ap()
    ow = nc.dram_tensor("ow", [D, D], F32, kind="ExternalInput").ap()
    obt = nc.dram_tensor("obt", [128, OCH], F32, kind="ExternalInput").ap()
    Wt = nc.dram_tensor("W", [D, D], F32, kind="ExternalInput").ap()
    bt = nc.dram_tensor("bt", [128, OCH], F32, kind="ExternalInput").ap()
    A = nc.dram_tensor("A", [32, D], F32, kind="ExternalInput").ap()
    BmT = nc.dram_tensor("BmT", [32, D], F32, kind="ExternalInput").ap()
    Wr = nc.dram_tensor("Wr", [4, D], F32, kind="ExternalInput").ap()
    sel = nc.dram_tensor("sel", [4, 32], F16, kind="ExternalInput").ap()
    id16 = nc.dram_tensor("id16", [128, 128], F16, kind="ExternalInput").ap()
    idbf = nc.dram_tensor("idbf", [128, 128], BF16, kind="ExternalInput").ap()
    id32 = nc.dram_tensor("id32", [128, 128], F32, kind="ExternalInput").ap()
    gidx = nc.dram_tensor("gidx", [128, OCH], I32, kind="ExternalInput").ap()

    y_out = nc.dram_tensor("y_out", [TLOC, D], F32, kind="ExternalOutput").ap()
    x16d = nc.dram_tensor("x16d", [S, D], F16).ap()
    x16md = nc.dram_tensor("x16md", [TLOC, D], F16).ap()
    w16d = nc.dram_tensor("w16d", [D, D], F16).ap()
    qTd = nc.dram_tensor("qTd", [NHL * 128, S], F16).ap()
    kTd = nc.dram_tensor("kTd", [NHL * 128, S], F16).ap()
    vd = nc.dram_tensor("vd", [S, NHL * 128], BF16).ap()
    ag_in = nc.dram_tensor("ag_in", [NHL * 128, S], F16).ap()
    ag_out = nc.dram_tensor("ag_out", [D, S], F16, addr_space="Shared").ap()
    ctx_mine_d = nc.dram_tensor("ctx_mine_d", [D, TLOC], F16).ap()
    ag_out_flat = ag_out.rearrange("a b -> (a b)").rearrange("(n one) -> n one", one=1)

    p = Prog()
    stack = contextlib.ExitStack()
    nc._kernel_stack = stack

    def sem(name):
        return Sem(stack.enter_context(nc.semaphore(name)), name)

    sCast = sem("sCast")
    sCastM = sem("sCastM")
    sCastW = sem("sCastW")
    sWld = sem("sWld")
    sXT = sem("sXT")
    sMM = sem("sMM")
    sEv = sem("sEv")
    sVt = sem("sVt")
    sVe = sem("sVe")
    sSp = sem("sSp")
    sNq = sem("sNq")
    sNm = sem("sNm")
    sBh = sem("sBh")
    sLd = sem("sLd")
    sST = sem("sST")
    sEX = sem("sEX")
    sPV = sem("sPV")
    sTR = sem("sTR")
    sTRb = sem("sTRb")
    sEp = sem("sEp")
    sAG = sem("sAG")
    sCC = sem("sCC")
    sGa = sem("sGa")
    sOwl = sem("sOwl")
    sXtp = sem("sXtp")
    sPm = sem("sPm")
    sPe = sem("sPe")
    sPv = sem("sPv")
    sWtl = sem("sWtl")
    sPs = sem("sPs")
    sOut = sem("sOut")
    sIdx = sem("sIdx")
    sCst = sem("sCst")

    ident16 = stack.enter_context(nc.sbuf_tensor("ident16", [128, 128], F16))
    identbf = stack.enter_context(nc.sbuf_tensor("identbf", [128, 128], BF16))
    ident32 = stack.enter_context(nc.sbuf_tensor("ident32", [128, 128], F32))
    ones16 = stack.enter_context(nc.sbuf_tensor("ones16", [128, 1], F16))
    onesbf = stack.enter_context(nc.sbuf_tensor("onesbf", [128, 1], BF16))
    ones32r = stack.enter_context(nc.sbuf_tensor("ones32r", [1, 128], F32))
    qkvb_sb = stack.enter_context(nc.sbuf_tensor("qkvb_sb", [128, CCH], F32))
    bias_h = stack.enter_context(nc.sbuf_tensor("bias_h", [128, NHL], F32))
    norm_acc = stack.enter_context(nc.sbuf_tensor("norm_acc", [1, 2 * NHL], F32))
    rowtmp = stack.enter_context(nc.sbuf_tensor("rowtmp", [1, 8], F32))

    p.op("sync", lambda e: e.dma_start(out=ident16[:, :], in_=id16[:, :]), inc=sCst, inc_amt=16)
    p.op("sync", lambda e: e.dma_start(out=identbf[:, :], in_=idbf[:, :]), inc=sCst, inc_amt=16)
    p.op("sync", lambda e: e.dma_start(out=ident32[:, :], in_=id32[:, :]), inc=sCst, inc_amt=16)
    p.op("sync", lambda e: e.dma_start(out=qkvb_sb[:, :], in_=qkvb[:, :]), inc=sCst, inc_amt=16)
    p.op("vector", lambda e: e.memset(ones16[:, :], 1.0), inc=sPv)
    p.op("vector", lambda e: e.memset(onesbf[:, :], 1.0), inc=sPv)
    p.op("vector", lambda e: e.memset(ones32r[:, :], 1.0), inc=sPv)
    p.op("vector", lambda e: e.memset(norm_acc[:, :], 0.0), inc=sPv)
    CONST_LD = sCst.n
    CONST_V = sPv.n

    ph1 = contextlib.ExitStack()
    qkvw_sb = ph1.enter_context(nc.sbuf_tensor("qkvw_sb", [128, CCH * ICH, 128], F16))
    xt_sb = [ph1.enter_context(nc.sbuf_tensor(f"xt{i}", [128, ICH, TT], F16)) for i in range(2)]
    stg = [ph1.enter_context(nc.sbuf_tensor(f"stg{i}", [128, TT], F16)) for i in range(3)]
    stgv = [ph1.enter_context(nc.sbuf_tensor(f"stgv{i}", [128, TT], BF16)) for i in range(2)]
    vstg = [ph1.enter_context(nc.sbuf_tensor(f"vstg{i}", [128, TT], BF16)) for i in range(2)]
    sq_sb = [ph1.enter_context(nc.sbuf_tensor(f"sq_sb{i}", [128, TT], F16)) for i in range(2)]
    ps_qkv = ph1.enter_context(nc.psum_tensor("ps_qkv", [128, 3 * TT], F32))
    ps_vt = ph1.enter_context(nc.psum_tensor("ps_vt", [128, TT], BF16))
    ps_qn = ph1.enter_context(nc.psum_tensor("ps_qn", [128, 512], F32))
    ps_bb = ph1.enter_context(nc.psum_tensor("ps_bb", [128, 512], F32))

    for c in range(CCH):
        p.op("gpsimd",
             lambda e, c=c: e.dma_start(
                 out=qkvw_sb[:, c * ICH:(c + 1) * ICH, :],
                 in_=qkvw[:, c * 128:(c + 1) * 128].rearrange("(i p) c -> p i c", p=128)),
             inc=sWld, inc_amt=16)
    QKVW_LD = sWld.n

    panel_mm_done = {}
    qk_spill_val = [0, 0, 0]
    v_spill_val = [0, 0]
    vstg_evac_val = [0, 0]
    sq_mm_val = [0, 0]
    qk_ctr = 0
    v_ctr = 0
    norm_defer = []
    for t in range(NTT):
        cwaits = []
        if t >= 2:
            cwaits.append((sXT, 16 * 32 * (t - 1)))
        p.op("gpsimd", lambda e, t=t: e.dma_start(out=x16d[t * TT:(t + 1) * TT, :],
                                                  in_=x32[t * TT:(t + 1) * TT, :]),
             waits=cwaits, inc=sCast, inc_amt=16)
        waits = [(sCast, 16 * (t + 1))]
        if t >= 2:
            waits.append((sMM, panel_mm_done[t - 2]))
        for i in range(ICH):
            p.op("sync",
                 lambda e, t=t, i=i: e.dma_start(
                     out=xt_sb[t % 2][:, i, :],
                     in_=x16d[t * TT:(t + 1) * TT, i * 128:(i + 1) * 128],
                     transpose=True),
                 waits=waits if i == 0 else (), inc=sXT, inc_amt=16)

        for c in range(CCH):
            tile_idx = t * CCH + c
            bank = tile_idx % 3
            waits = [(sXT, 16 * ICH * (t + 1))]
            if tile_idx == 0:
                waits.append((sWld, QKVW_LD))
                waits.append((sCst, CONST_LD))
                waits.append((sPv, CONST_V))
            if tile_idx >= 3:
                waits.append((sEv, tile_idx - 3 + 1))
            for i in range(ICH):
                p.op("tensor",
                     lambda e, c=c, t=t, i=i, bank=bank: e.matmul(
                         ps_qkv[:, bank * TT:(bank + 1) * TT],
                         qkvw_sb[:, c * ICH + i, :],
                         xt_sb[t % 2][:, i, :],
                         start=(i == 0), stop=(i == ICH - 1)),
                     waits=waits if i == 0 else (),
                     inc=sMM if i == ICH - 1 else None)
            mm_val = sMM.n
            p.flush(norm_defer)

            lh, kind = c // 3, c % 3
            if kind in (0, 1):
                bidx = qk_ctr % 3
                sqb = sq_sb[qk_ctr % 2]
                sq_war = sq_mm_val[qk_ctr % 2]
                qk_ctr += 1
                sb = stg[bidx]
                waits = [(sMM, mm_val), (sCst, CONST_LD)]
                if qk_spill_val[bidx] > 0:
                    waits.append((sSp, qk_spill_val[bidx]))
                p.op("scalar",
                     lambda e, sb=sb, bank=bank, c=c: e.activation(
                         sb[:, :], ps_qkv[:, bank * TT:(bank + 1) * TT],
                         AF.Identity, bias=qkvb_sb[:, c:c + 1], scale=1.0),
                     waits=waits, inc=sEv)
                ev_val = sEv.n
                dst = qTd if kind == 0 else kTd
                p.op("sync",
                     lambda e, sb=sb, dst=dst, lh=lh, t=t: e.dma_start(
                         out=dst[lh * 128:(lh + 1) * 128, t * TT:(t + 1) * TT],
                         in_=sb[:, :]),
                     waits=[(sEv, ev_val)], inc=sSp, inc_amt=16)
                qk_spill_val[bidx] = sSp.n
                waits = [(sEv, ev_val)]
                if sq_war > 0:
                    waits.append((sNm, sq_war))
                p.op("vector",
                     lambda e, sb=sb, sqb=sqb: e.tensor_mul(sqb[:, :], sb[:, :], sb[:, :]),
                     waits=waits, inc=sNq)
                p.op("tensor",
                     lambda e, sqb=sqb: e.matmul(ps_qn[0:1, 0:TT], ones16[:, :], sqb[:, :],
                                                 start=True, stop=True),
                     waits=[(sNq, sNq.n), (sNq, sNq.n)], inc=sNm, defer=norm_defer)
                sq_mm_val[(qk_ctr - 1) % 2] = sNm.n
                acc_col = lh * 2 + kind
                p.op("vector",
                     lambda e: e.reduce_max(rowtmp[0:1, 0:1], ps_qn[0:1, 0:TT], axis=AXX),
                     waits=[(sNm, sNm.n)], inc=sNq)
                p.op("vector",
                     lambda e, acc_col=acc_col: e.tensor_max(
                         norm_acc[0:1, acc_col:acc_col + 1],
                         norm_acc[0:1, acc_col:acc_col + 1],
                         rowtmp[0:1, 0:1]),
                     waits=(), inc=sNq)
            else:
                sbv = stgv[v_ctr % 2]
                vb = vstg[v_ctr % 2]
                waits = [(sMM, mm_val), (sCst, CONST_LD)]
                if vstg_evac_val[v_ctr % 2] > 0:
                    waits.append((sVt, vstg_evac_val[v_ctr % 2]))
                p.op("scalar",
                     lambda e, sbv=sbv, bank=bank, c=c: e.activation(
                         sbv[:, :], ps_qkv[:, bank * TT:(bank + 1) * TT],
                         AF.Identity, bias=qkvb_sb[:, c:c + 1], scale=1.0),
                     waits=waits, inc=sEv)
                ev_val = sEv.n
                for j in range(4):
                    p.op("tensor",
                         lambda e, sbv=sbv, j=j: e.transpose(
                             ps_vt[:, j * 128:(j + 1) * 128],
                             sbv[:, j * 128:(j + 1) * 128], identbf[:, :]),
                         waits=([(sEv, ev_val), (sVe, sVe.n)] if j == 0 else ()),
                         inc=sVt if j == 3 else None)
                vstg_evac_val[v_ctr % 2] = sVt.n
                vt_val = sVt.n
                waits = [(sVt, vt_val)]
                if v_spill_val[v_ctr % 2] > 0:
                    waits.append((sSp, v_spill_val[v_ctr % 2]))
                p.op("vector",
                     lambda e, vb=vb: e.tensor_copy(vb[:, :], ps_vt[:, :]),
                     waits=waits, inc=sVe)
                ve_val = sVe.n
                p.op("sync",
                     lambda e, vb=vb, lh=lh, t=t: e.dma_start(
                         out=vd[t * TT:(t + 1) * TT, lh * 128:(lh + 1) * 128]
                         .rearrange("(j p) d -> p j d", p=128),
                         in_=vb[:, :].rearrange("p (j d) -> p j d", j=4)),
                     waits=[(sVe, ve_val)], inc=sSp, inc_amt=16)
                v_spill_val[v_ctr % 2] = sSp.n
                v_ctr += 1
        panel_mm_done[t] = sMM.n
    p.flush(norm_defer)

    nq0 = sNq.n
    for lh in range(NHL):
        p.op("vector",
             lambda e, lh=lh: e.tensor_mul(rowtmp[0:1, 1:2],
                                           norm_acc[0:1, 2 * lh:2 * lh + 1],
                                           norm_acc[0:1, 2 * lh + 1:2 * lh + 2]),
             waits=[(sNq, nq0)], inc=sNq)
        p.op("scalar",
             lambda e: e.activation(rowtmp[0:1, 2:3], rowtmp[0:1, 1:2], AF.Ln),
             waits=[(sNq, sNq.n)], inc=sBh)
        p.op("scalar",
             lambda e: e.activation(rowtmp[0:1, 3:4], rowtmp[0:1, 2:3], AF.Exp, scale=0.5),
             waits=(), inc=None)
        p.op("scalar",
             lambda e: e.activation(rowtmp[0:1, 5:6], rowtmp[0:1, 3:4], AF.Identity),
             waits=(), inc=sBh)
        p.op("vector",
             lambda e: e.tensor_single_scalar(rowtmp[0:1, 4:5], rowtmp[0:1, 3:4],
                                              -SCALE, ALU.mult),
             waits=[(sBh, sBh.n)], inc=sNq)
        p.op("tensor",
             lambda e: e.matmul(ps_bb[:, 0:1], ones32r[:, :], rowtmp[0:1, 4:5],
                                start=True, stop=True),
             waits=[(sNq, sNq.n), (sBh, sBh.n)], inc=sNm)
        p.op("vector",
             lambda e, lh=lh: e.tensor_copy(bias_h[:, lh:lh + 1], ps_bb[:, 0:1]),
             waits=[(sNm, sNm.n)], inc=sBh)
    BH_DONE = sBh.n
    P1_SPILLS = sSp.n
    p.op("gpsimd", lambda e: e.dma_start(out=x16md[:, :], in_=x32m[:, :]),
         waits=[(sCast, 16 * NTT)], inc=sCastM, inc_amt=16)
    for c in range(8):
        p.op("gpsimd", lambda e, c=c: e.dma_start(out=w16d[c * 512:(c + 1) * 512, :],
                                                  in_=Wt[c * 512:(c + 1) * 512, :]),
             inc=sCastW, inc_amt=16)

    ph1.close()
    ph2 = contextlib.ExitStack()
    kT_sb = [ph2.enter_context(nc.sbuf_tensor(f"kT{i}", [128, S], F16)) for i in range(2)]
    qT_sb = [ph2.enter_context(nc.sbuf_tensor(f"qT{i}", [128, S], F16)) for i in range(2)]
    v_sb = [ph2.enter_context(nc.sbuf_tensor(f"v{i}", [128, KCH, 128], BF16)) for i in range(2)]
    pt_sb = [ph2.enter_context(nc.sbuf_tensor(f"pt{i}", [128, 2 * TQ], BF16)) for i in range(3)]
    tree_bufs = [ph2.enter_context(nc.sbuf_tensor(f"tr{i}", [128, TQ], BF16)) for i in range(16)]
    rrow = ph2.enter_context(nc.sbuf_tensor("rrow", [1, 2 * TQ], F32))
    rb_sb = ph2.enter_context(nc.sbuf_tensor("rb_sb", [128, 2 * TQ], F32))
    ctxn = [ph2.enter_context(nc.sbuf_tensor(f"ctxn{i}", [128, TQ], F16)) for i in range(2)]
    ps_st = [ph2.enter_context(nc.psum_tensor(f"ps_stt{i}", [128, 2 * TQ], F32))
             for i in range(3)]
    ps_ctx2 = ph2.enter_context(nc.psum_tensor("ps_ctx2", [128, 2 * TQ], F32))

    st_free = [0, 0, 0]
    pt_free_pv = [0, 0, 0]
    pt_free_trA = [0, 0, 0]
    pt_free_trB = [0, 0, 0]
    ctxn_dma_val = [0, 0]
    head_pv_done = {}

    def ap_of(tok):
        if tok[0] == "pt":
            return pt_sb[tok[1]][:, tok[2] * TQ:(tok[2] + 1) * TQ]
        return tok[1][:, :]

    class Tree:

        def __init__(self, pool):
            self.free = list(pool)
            self.occ = {}

        def insert_pair_ops(self, tokx, toky):
            ops = []
            outb = self.free.pop()
            ops.append(lambda e, a=tokx, bb=toky, outb=outb: e.tensor_add(
                outb[:, :], ap_of(a), ap_of(bb)))
            cur = ("buf", outb)
            lvl = 0
            while lvl in self.occ:
                prev = self.occ.pop(lvl)
                outb = prev[1]
                ops.append(lambda e, a=cur, bb=prev, outb=outb: e.tensor_add(
                    outb[:, :], ap_of(a), ap_of(bb)))
                if cur[1] is not outb:
                    self.free.append(cur[1])
                cur = ("buf", outb)
                lvl += 1
            self.occ[lvl] = cur
            return ops

        def root(self):
            assert len(self.occ) == 1
            return self.occ.popitem()[1][1]

    for lh in range(NHL):
        b = lh % 2
        waits = [(sSp, P1_SPILLS), (sBh, BH_DONE)]
        if lh >= 2:
            waits.append((sPV, head_pv_done[lh - 2]))
        p.op("sync", lambda e, b=b, lh=lh: e.dma_start(
            out=kT_sb[b][:, :], in_=kTd[lh * 128:(lh + 1) * 128, :]),
            waits=waits, inc=sLd, inc_amt=16)
        p.op("sync", lambda e, b=b, lh=lh: e.dma_start(
            out=qT_sb[b][:, :], in_=qTd[lh * 128:(lh + 1) * 128, :]),
            inc=sLd, inc_amt=16)
        p.op("sync", lambda e, b=b, lh=lh: e.dma_start(
            out=v_sb[b][:, :, :],
            in_=vd[:, lh * 128:(lh + 1) * 128].rearrange("(j p) d -> p j d", p=128)),
            inc=sLd, inc_amt=16)
        ld_val = sLd.n

        for qp in range(NQT // 2):
            qA, qB = 2 * qp, 2 * qp + 1
            ctx_free_val = sEp.n
            treeA = Tree(tree_bufs[0:8])
            treeB = Tree(tree_bufs[8:16])
            exp_vals = {}

            def emit_pv(kc, ctx_free_val=None):
                par = kc % 3
                ptb = pt_sb[par]
                w2 = [(sEX, exp_vals[kc])]
                if kc == 0:
                    w2.append((sEp, ctx_free_val))
                p.op("tensor",
                     lambda e, b=b, kc=kc, ptb=ptb: e.matmul(
                         ps_ctx2[:, 0:TQ], v_sb[b][:, kc, :], ptb[:, 0:TQ],
                         start=(kc == 0), stop=(kc == KCH - 1)),
                     waits=w2, inc=None)
                p.op("tensor",
                     lambda e, b=b, kc=kc, ptb=ptb: e.matmul(
                         ps_ctx2[:, TQ:2 * TQ], v_sb[b][:, kc, :], ptb[:, TQ:2 * TQ],
                         start=(kc == 0), stop=(kc == KCH - 1)),
                     waits=(), inc=sPV)
                pt_free_pv[par] = sPV.n

            for kc in range(KCH):
                par = kc % 3
                stps = ps_st[par]
                ptb = pt_sb[par]
                waits = []
                if kc == 0 and qp == 0:
                    waits.append((sLd, ld_val))
                    if lh == 0:
                        waits.append((sBh, BH_DONE))
                if kc == 0:
                    waits.append((sEp, ctx_free_val))
                if st_free[par] > 0:
                    waits.append((sEX, st_free[par]))
                p.op("tensor",
                     lambda e, b=b, kc=kc, qA=qA, stps=stps: e.matmul(
                         stps[:, 0:TQ],
                         kT_sb[b][:, kc * 128:(kc + 1) * 128],
                         qT_sb[b][:, qA * TQ:(qA + 1) * TQ],
                         start=True, stop=True),
                     waits=waits, inc=None)
                p.op("tensor",
                     lambda e, b=b, kc=kc, qB=qB, stps=stps: e.matmul(
                         stps[:, TQ:2 * TQ],
                         kT_sb[b][:, kc * 128:(kc + 1) * 128],
                         qT_sb[b][:, qB * TQ:(qB + 1) * TQ],
                         start=True, stop=True),
                     waits=(), inc=sST)
                st_val = sST.n
                if kc >= 1:
                    emit_pv(kc - 1, ctx_free_val)

                waits = [(sST, st_val)]
                if pt_free_pv[par] > 0:
                    waits.append((sPV, pt_free_pv[par]))
                    waits.append((sTR, pt_free_trA[par]))
                    waits.append((sTRb, pt_free_trB[par]))
                p.op("scalar",
                     lambda e, ptb=ptb, stps=stps, lh=lh: e.activation(
                         ptb[:, :], stps[:, :], AF.Exp,
                         bias=bias_h[:, lh:lh + 1], scale=SCALE),
                     waits=waits, inc=sEX)
                exp_vals[kc] = sEX.n
                st_free[par] = sEX.n

                if kc % 2 == 1:
                    par0 = (kc - 1) % 3
                    opsA = treeA.insert_pair_ops(("pt", par0, 0), ("pt", par, 0))
                    for oi, fn_ in enumerate(opsA):
                        p.op("vector", fn_,
                             waits=[(sEX, exp_vals[kc])] if oi == 0 else (),
                             inc=sTR if oi == len(opsA) - 1 else None)
                    pt_free_trA[par0] = pt_free_trA[par] = sTR.n
                    opsB = treeB.insert_pair_ops(("pt", par0, 1), ("pt", par, 1))
                    for oi, fn_ in enumerate(opsB):
                        p.op("vector", fn_,
                             waits=(),
                             inc=sTRb if oi == len(opsB) - 1 else None)
                    pt_free_trB[par0] = pt_free_trB[par] = sTRb.n
            emit_pv(KCH - 1)
            pv_all = sPV.n

            rootA, rootB = treeA.root(), treeB.root()
            trA_val, trB_val = sTR.n, sTRb.n
            p.op("tensor",
                 lambda e, rootA=rootA: e.matmul(ps_st[0][0:1, 0:TQ], onesbf[:, :],
                                                 rootA[:, :], start=True, stop=True),
                 waits=[(sTR, trA_val), (sEX, exp_vals[KCH - 1]), (sPv, CONST_V)],
                 inc=None)
            p.op("tensor",
                 lambda e, rootB=rootB: e.matmul(ps_st[1][0:1, 0:TQ], onesbf[:, :],
                                                 rootB[:, :], start=True, stop=True),
                 waits=[(sTRb, trB_val)], inc=sNm)
            mm_val = sNm.n
            p.op("vector",
                 lambda e: e.reciprocal(rrow[0:1, 0:TQ], ps_st[0][0:1, 0:TQ]),
                 waits=[(sNm, mm_val)], inc=None)
            p.op("vector",
                 lambda e: e.reciprocal(rrow[0:1, TQ:2 * TQ], ps_st[1][0:1, 0:TQ]),
                 waits=(), inc=sEp)
            rec_val = sEp.n
            p.op("tensor",
                 lambda e: e.matmul(ps_st[0][:, 0:TQ], ones32r[:, :], rrow[0:1, 0:TQ],
                                    start=True, stop=True),
                 waits=[(sEp, rec_val)], inc=None)
            p.op("tensor",
                 lambda e: e.matmul(ps_st[1][:, 0:TQ], ones32r[:, :], rrow[0:1, TQ:2 * TQ],
                                    start=True, stop=True),
                 waits=(), inc=sNm)
            bc_val = sNm.n
            p.op("vector",
                 lambda e: e.tensor_copy(rb_sb[:, 0:TQ], ps_st[0][:, 0:TQ]),
                 waits=[(sNm, bc_val)], inc=None)
            p.op("vector",
                 lambda e: e.tensor_copy(rb_sb[:, TQ:2 * TQ], ps_st[1][:, 0:TQ]),
                 waits=(), inc=sEp)
            rb_val = sEp.n
            waits = [(sPV, pv_all), (sEp, rb_val)]
            if ctxn_dma_val[0] > 0:
                waits.append((sAG, max(ctxn_dma_val[0], ctxn_dma_val[1])))
            p.op("vector",
                 lambda e: e.tensor_tensor(ctxn[0][:, :], ps_ctx2[:, 0:TQ],
                                           rb_sb[:, 0:TQ], op=ALU.mult),
                 waits=waits, inc=sEp)
            evA = sEp.n
            p.op("vector",
                 lambda e: e.tensor_tensor(ctxn[1][:, :], ps_ctx2[:, TQ:2 * TQ],
                                           rb_sb[:, TQ:2 * TQ], op=ALU.mult),
                 waits=(), inc=sEp)
            evB = sEp.n
            p.op("sync",
                 lambda e, lh=lh, qA=qA: e.dma_start(
                     out=ag_in[lh * 128:(lh + 1) * 128, qA * TQ:(qA + 1) * TQ],
                     in_=ctxn[0][:, :]),
                 waits=[(sEp, evA)], inc=sAG, inc_amt=16)
            ctxn_dma_val[0] = sAG.n
            p.op("sync",
                 lambda e, lh=lh, qB=qB: e.dma_start(
                     out=ag_in[lh * 128:(lh + 1) * 128, qB * TQ:(qB + 1) * TQ],
                     in_=ctxn[1][:, :]),
                 waits=[(sEp, evB)], inc=sAG, inc_amt=16)
            ctxn_dma_val[1] = sAG.n
        head_pv_done[lh] = sPV.n

    AG_ALL = sAG.n

    p.op("gpsimd",
         lambda e: e.collective_compute(
             "AllGather", ALU.bypass, replica_groups=[list(range(NCORES))],
             ins=[ag_in.opt()], outs=[ag_out.opt()]),
         waits=[(sAG, AG_ALL)], inc=sCC, inc_amt=1)

    ph2.close()
    ph3 = contextlib.ExitStack()
    y1_16 = ph3.enter_context(nc.sbuf_tensor("y1_16", [128, OCH, TLOC], F16))
    idx_sb = ph3.enter_context(nc.sbuf_tensor("idx_sb", [128, OCH], I32))
    gstg = [ph3.enter_context(nc.sbuf_tensor(f"gstg{i}", [128, TLOC], F16)) for i in range(2)]
    p.op("sync", lambda e: e.dma_start(out=idx_sb[:, :], in_=gidx[:, :]),
         waits=[(sAG, AG_ALL)], inc=sIdx, inc_amt=16)
    gstg_out_val = [0, 0]
    for c in range(OCH):
        waits = [(sIdx, 16), (sCC, 1)] if c == 0 else []
        if gstg_out_val[c % 2] > 0:
            waits.append((sGa, gstg_out_val[c % 2]))
        p.op("gpsimd",
             lambda e, c=c: e.indirect_dma_start(
                 gstg[c % 2][:, :], None, ag_out_flat[0:128, :],
                 IndirectOffsetOnAxis(idx_sb[:, c:c + 1], axis=0)),
             waits=waits, inc=sGa, inc_amt=16)
        gv = sGa.n
        p.op("gpsimd",
             lambda e, c=c: e.dma_start(
                 out=ctx_mine_d[c * 128:(c + 1) * 128, :], in_=gstg[c % 2][:, :]),
             waits=[(sGa, gv)], inc=sGa, inc_amt=16)
        gstg_out_val[c % 2] = sGa.n
    GATHER_ALL = sGa.n

    pA = contextlib.ExitStack()
    ctxp = pA.enter_context(nc.sbuf_tensor("ctxp", [128, OCH, SUB], F16))
    owp = [pA.enter_context(nc.sbuf_tensor(f"owp{i}", [128, ICH, 128], F16)) for i in range(2)]
    xtp = [pA.enter_context(nc.sbuf_tensor(f"xtp{i}", [128, SUB], F16)) for i in range(2)]
    ysb = pA.enter_context(nc.sbuf_tensor("ysb", [128, OCH, SUB], F32))
    y16t = pA.enter_context(nc.sbuf_tensor("y16t", [128, SUB], F16))
    sq16t = pA.enter_context(nc.sbuf_tensor("sq16t", [128, SUB], F16))
    obt_sb = pA.enter_context(nc.sbuf_tensor("obt_sb", [128, OCH], F32))
    mrow = pA.enter_context(nc.sbuf_tensor("mrow", [1, 2 * SUB], F32))
    vrow = pA.enter_context(nc.sbuf_tensor("vrow", [1, 2 * SUB], F32))
    mb_sb = pA.enter_context(nc.sbuf_tensor("mb_sb", [128, SUB], F32))
    rstdb = pA.enter_context(nc.sbuf_tensor("rstdb", [128, SUB], F32))
    tnorm = [pA.enter_context(nc.sbuf_tensor(f"tnorm{i}", [128, SUB], F32)) for i in range(2)]
    ps_op = pA.enter_context(nc.psum_tensor("ps_op", [128, 2 * SUB], F32))
    ps_st1 = pA.enter_context(nc.psum_tensor("ps_st1", [128, 2 * SUB], F32))
    ps_bc = pA.enter_context(nc.psum_tensor("ps_bc", [128, SUB], F32))

    p.op("sync", lambda e: e.dma_start(out=obt_sb[:, :], in_=obt[:, :]),
         waits=[(sAG, AG_ALL)], inc=sCst, inc_amt=16)
    OBT_LD = sCst.n

    ow_war = {}
    xt_war = {}
    evac_war = {}
    tnorm_war = [0, 0]
    ga_war = 0
    for sub in range(NSUB):
        waits = [(sGa, GATHER_ALL)]
        if sub > 0:
            waits.append((sPm, ga_war))
        p.op("sync",
             lambda e, sub=sub: e.dma_start(
                 out=ctxp[:, :, :],
                 in_=ctx_mine_d[:, sub * SUB:(sub + 1) * SUB]
                 .rearrange("(c p) t -> p c t", p=128)),
             waits=waits, inc=sGa, inc_amt=16)
        ga_val = sGa.n

        for c in range(OCH):
            ci = sub * OCH + c
            ob_ = owp[ci % 2]
            waits = []
            if ci >= 2:
                waits.append((sPm, ow_war[ci - 2]))
            p.op("gpsimd",
                 lambda e, ob_=ob_, c=c: e.dma_start(
                     out=ob_[:, :, :],
                     in_=ow[:, c * 128:(c + 1) * 128].rearrange("(i p) c -> p i c", p=128)),
                 waits=waits, inc=sOwl, inc_amt=16)
            owl_val = sOwl.n
            xb = xtp[ci % 2]
            waits = [(sCastM, 16)]
            if ci >= 2:
                waits.append((sPv, xt_war[ci - 2]))
            p.op("sync",
                 lambda e, xb=xb, c=c, sub=sub: e.dma_start(
                     out=xb[:, :],
                     in_=x16md[sub * SUB:(sub + 1) * SUB, c * 128:(c + 1) * 128],
                     transpose=True),
                 waits=waits, inc=sXtp, inc_amt=16)
            xtp_val = sXtp.n
            bank = ci % 2
            waits = [(sOwl, owl_val), (sGa, ga_val)]
            if ci >= 2:
                waits.append((sPe, evac_war[ci - 2]))
            for i in range(ICH):
                p.op("tensor",
                     lambda e, ob_=ob_, i=i, bank=bank: e.matmul(
                         ps_op[:, bank * SUB:(bank + 1) * SUB],
                         ob_[:, i, :], ctxp[:, i, :],
                         start=(i == 0), stop=(i == ICH - 1)),
                     waits=waits if i == 0 else (),
                     inc=sPm if i == ICH - 1 else None)
            ow_war[ci] = sPm.n
            mm_val = sPm.n
            tn = tnorm[ci % 2]
            waits = [(sPm, mm_val), (sCst, OBT_LD)]
            if tnorm_war[ci % 2] > 0:
                waits.append((sPv, tnorm_war[ci % 2]))
            p.op("scalar",
                 lambda e, tn=tn, bank=bank, c=c: e.activation(
                     tn[:, :], ps_op[:, bank * SUB:(bank + 1) * SUB],
                     AF.Identity, bias=obt_sb[:, c:c + 1], scale=1.0),
                 waits=waits, inc=sPe)
            evac_war[ci] = sPe.n
            p.op("vector",
                 lambda e, c=c, xb=xb, tn=tn: e.tensor_add(ysb[:, c, :], tn[:, :], xb[:, :]),
                 waits=[(sPe, sPe.n), (sXtp, xtp_val)], inc=sPv)
            tnorm_war[ci % 2] = sPv.n
            xt_war[ci] = sPv.n
            p.op("vector",
                 lambda e, c=c: e.tensor_copy(y16t[:, :], ysb[:, c, :]),
                 waits=(), inc=sPv)
            p.op("vector",
                 lambda e: e.tensor_mul(sq16t[:, :], y16t[:, :], y16t[:, :]),
                 waits=(), inc=sPv)
            sqv = sPv.n
            p.op("tensor",
                 lambda e, c=c: e.matmul(ps_st1[0:1, 0:SUB], ones16[:, :], y16t[:, :],
                                         start=(c == 0), stop=(c == OCH - 1)),
                 waits=[(sPv, sqv)],
                 inc=None)
            p.op("tensor",
                 lambda e, c=c: e.matmul(ps_st1[0:1, SUB:2 * SUB], ones16[:, :], sq16t[:, :],
                                         start=(c == 0), stop=(c == OCH - 1)),
                 waits=(),
                 inc=sPm if c == OCH - 1 else None)
        ga_war = sPm.n
        st_done = sPm.n
        p.op("vector",
             lambda e: e.tensor_single_scalar(mrow[0:1, 0:SUB], ps_st1[0:1, 0:SUB],
                                              1.0 / D, ALU.mult),
             waits=[(sPm, st_done)], inc=sPv)
        p.op("vector",
             lambda e: e.tensor_single_scalar(mrow[0:1, SUB:2 * SUB], ps_st1[0:1, SUB:2 * SUB],
                                              1.0 / D, ALU.mult),
             waits=(), inc=sPv)
        p.op("vector",
             lambda e: e.tensor_mul(vrow[0:1, 0:SUB], mrow[0:1, 0:SUB], mrow[0:1, 0:SUB]),
             waits=(), inc=sPv)
        p.op("vector",
             lambda e: e.tensor_sub(vrow[0:1, 0:SUB], mrow[0:1, SUB:2 * SUB],
                                    vrow[0:1, 0:SUB]),
             waits=(), inc=sPv)
        p.op("vector",
             lambda e: e.tensor_single_scalar(vrow[0:1, 0:SUB], vrow[0:1, 0:SUB],
                                              EPS, ALU.add),
             waits=(), inc=sPv)
        varv = sPv.n
        p.op("scalar",
             lambda e: e.activation(vrow[0:1, SUB:2 * SUB], vrow[0:1, 0:SUB], AF.Ln),
             waits=[(sPv, varv)], inc=sPs)
        p.op("scalar",
             lambda e: e.activation(vrow[0:1, 0:SUB], vrow[0:1, SUB:2 * SUB], AF.Exp,
                                    scale=-0.5),
             waits=(), inc=None)
        p.op("scalar",
             lambda e: e.activation(vrow[0:1, SUB:2 * SUB], vrow[0:1, 0:SUB], AF.Identity),
             waits=(), inc=sPs)
        rstd_val = sPs.n
        p.op("tensor",
             lambda e: e.matmul(ps_bc[:, :], ones32r[:, :], mrow[0:1, 0:SUB],
                                start=True, stop=True),
             waits=[(sPs, rstd_val)], inc=sPm)
        p.op("vector",
             lambda e: e.tensor_copy(mb_sb[:, :], ps_bc[:, :]),
             waits=[(sPm, sPm.n)], inc=sPv)
        p.op("tensor",
             lambda e: e.matmul(ps_bc[:, :], ones32r[:, :], vrow[0:1, 0:SUB],
                                start=True, stop=True),
             waits=[(sPv, sPv.n)], inc=sPm)
        p.op("vector",
             lambda e: e.tensor_copy(rstdb[:, :], ps_bc[:, :]),
             waits=[(sPm, sPm.n)], inc=sPv)
        bc_done = sPv.n
        for c in range(OCH):
            tn = tnorm[c % 2]
            waits = [(sPv, bc_done)]
            p.op("vector",
                 lambda e, c=c, tn=tn: e.tensor_sub(tn[:, :], ysb[:, c, :], mb_sb[:, :]),
                 waits=waits, inc=None)
            p.op("vector",
                 lambda e, c=c, sub=sub, tn=tn: e.tensor_mul(
                     y1_16[:, c, sub * SUB:(sub + 1) * SUB], tn[:, :], rstdb[:, :]),
                 waits=(), inc=sPv)
        tnorm_war = [sPv.n, sPv.n]
    Y1_DONE = sPv.n
    pA.close()

    pBs = contextlib.ExitStack()
    bm16 = pBs.enter_context(nc.sbuf_tensor("bm16", [32, D], F16))
    pr16 = pBs.enter_context(nc.sbuf_tensor("pr16", [32, NSUB, SUB], F16))
    pB = contextlib.ExitStack()
    a16 = pB.enter_context(nc.sbuf_tensor("a16", [32, D], F16))
    aT16 = pB.enter_context(nc.sbuf_tensor("aT16", [128, ICH, 32], F16))
    wr16 = pB.enter_context(nc.sbuf_tensor("wr16", [4, D], F16))
    wrT16 = pB.enter_context(nc.sbuf_tensor("wrT16", [128, ICH, 4], F16))
    sel16 = pB.enter_context(nc.sbuf_tensor("sel16", [4, 32], F16))
    gl32 = pB.enter_context(nc.sbuf_tensor("gl32", [4, SUB], F32))
    ge32 = pB.enter_context(nc.sbuf_tensor("ge32", [4, SUB], F32))
    rb4 = pB.enter_context(nc.sbuf_tensor("rb4", [4, SUB], F32))
    grow2 = pB.enter_context(nc.sbuf_tensor("grow2", [1, SUB], F32))
    ones4 = pB.enter_context(nc.sbuf_tensor("ones4", [4, 1], F32))
    gem16 = pB.enter_context(nc.sbuf_tensor("gem16", [4, SUB], F16))
    grep_sb = pB.enter_context(nc.sbuf_tensor("grep_sb", [32, SUB], F32))
    ps_g = pB.enter_context(nc.psum_tensor("ps_g", [128, 512], F32))
    ps_g2 = pB.enter_context(nc.psum_tensor("ps_g2", [128, 512], F32))
    ps_tA = pB.enter_context(nc.psum_tensor("ps_tA", [128, 64], F16))

    p.op("vector", lambda e: e.memset(ones4[:, :], 1.0),
         waits=[(sPv, Y1_DONE)], inc=sPv)
    CONST_V = sPv.n
    p.op("gpsimd", lambda e: e.dma_start(out=a16[:, :], in_=A[:, :]),
         waits=[(sPv, Y1_DONE)], inc=sGa, inc_amt=16)
    p.op("gpsimd", lambda e: e.dma_start(out=wr16[:, :], in_=Wr[:, :]), inc=sGa, inc_amt=16)
    p.op("gpsimd", lambda e: e.dma_start(out=bm16[:, :], in_=BmT[:, :]), inc=sGa, inc_amt=16)
    p.op("sync", lambda e: e.dma_start(out=sel16[:, :], in_=sel[:, :]),
         waits=[(sPv, Y1_DONE)], inc=sCst, inc_amt=16)
    SMALL_G = sGa.n
    SMALL_C = sCst.n
    for i in range(ICH):
        p.op("tensor",
             lambda e, i=i: e.transpose(ps_tA[:, 0:32],
                                        a16[:, i * 128:(i + 1) * 128], ident16[0:32, 0:32]),
             waits=[(sGa, SMALL_G), (sCst, CONST_LD), (sPv, sPv.n)], inc=sPm)
        p.op("vector",
             lambda e, i=i: e.tensor_copy(aT16[:, i, :], ps_tA[:, 0:32]),
             waits=[(sPm, sPm.n)], inc=sPv)
        p.op("tensor",
             lambda e, i=i: e.transpose(ps_tA[:, 32:36],
                                        wr16[:, i * 128:(i + 1) * 128], ident16[0:4, 0:4]),
             waits=[(sPv, sPv.n)], inc=sPm)
        p.op("vector",
             lambda e, i=i: e.tensor_copy(wrT16[:, i, :], ps_tA[:, 32:36]),
             waits=[(sPm, sPm.n)], inc=sPv)
    TRS_V = sPv.n

    for sub in range(NSUB):
        for i in range(ICH):
            p.op("tensor",
                 lambda e, i=i, sub=sub: e.matmul(
                     ps_g[0:4, 0:SUB], wrT16[:, i, :],
                     y1_16[:, i, sub * SUB:(sub + 1) * SUB],
                     start=(i == 0), stop=(i == ICH - 1)),
                 waits=[(sPv, TRS_V)] if i == 0 else (),
                 inc=sPm if i == ICH - 1 else None)
        p.op("vector",
             lambda e: e.tensor_copy(gl32[:, :], ps_g[0:4, 0:SUB]),
             waits=[(sPm, sPm.n)], inc=sPv)
        glv = sPv.n
        p.op("scalar",
             lambda e: e.activation(ge32[:, :], gl32[:, :], AF.Exp),
             waits=[(sPv, glv)], inc=None)
        p.op("scalar",
             lambda e: e.activation(rb4[:, :], ge32[:, :], AF.Identity),
             waits=(), inc=sPs)
        exp_drain = sPs.n
        p.op("tensor",
             lambda e: e.matmul(ps_g2[0:1, 0:SUB], ones4[:, :], ge32[:, :],
                                start=True, stop=True),
             waits=[(sPs, exp_drain), (sPv, CONST_V)], inc=sPm)
        p.op("vector",
             lambda e: e.reciprocal(grow2[0:1, :], ps_g2[0:1, 0:SUB]),
             waits=[(sPm, sPm.n)], inc=sPv)
        p.op("tensor",
             lambda e: e.matmul(ps_g2[0:4, 0:SUB], ones32r[0:1, 0:4], grow2[0:1, :],
                                start=True, stop=True),
             waits=[(sPv, sPv.n)], inc=sPm)
        p.op("vector",
             lambda e: e.tensor_copy(rb4[:, :], ps_g2[0:4, 0:SUB]),
             waits=[(sPm, sPm.n)], inc=sPv)
        p.op("vector",
             lambda e: e.tensor_tensor(gem16[:, :], ge32[:, :], rb4[:, :], op=ALU.mult),
             waits=(), inc=sPv)
        gem_done = sPv.n
        p.op("tensor",
             lambda e: e.matmul(ps_g[0:32, 0:SUB], sel16[:, :], gem16[:, :],
                                start=True, stop=True),
             waits=[(sPv, gem_done), (sCst, SMALL_C)], inc=sPm)
        p.op("vector",
             lambda e: e.tensor_copy(grep_sb[:, :], ps_g[0:32, 0:SUB]),
             waits=[(sPm, sPm.n)], inc=sPv)
        for i in range(ICH):
            p.op("tensor",
                 lambda e, i=i, sub=sub: e.matmul(
                     ps_g2[0:32, 0:SUB], aT16[:, i, :],
                     y1_16[:, i, sub * SUB:(sub + 1) * SUB],
                     start=(i == 0), stop=(i == ICH - 1)),
                 waits=[(sPv, sPv.n)] if i == 0 else (),
                 inc=sPm if i == ICH - 1 else None)
        p.op("vector",
             lambda e, sub=sub: e.tensor_tensor(pr16[:, sub, :], ps_g2[0:32, 0:SUB],
                                                grep_sb[:, :], op=ALU.mult),
             waits=[(sPm, sPm.n)], inc=sPv)
    PR_DONE = sPv.n
    pB.close()

    pC = contextlib.ExitStack()
    y2_16 = pC.enter_context(nc.sbuf_tensor("y2_16", [128, OCH, TLOC], F16, side="right"))
    ps_st2 = pC.enter_context(nc.psum_tensor("ps_st2", [128, 2 * NSUB * SUB], F32, side="right"))
    pW = contextlib.ExitStack()
    OSUP = 256
    wtp = [pW.enter_context(nc.sbuf_tensor(f"wtp{i}", [128, ICH, OSUP], F16)) for i in range(2)]
    ftmp = [pW.enter_context(nc.sbuf_tensor(f"ftmp{i}", [128, SUB], F32)) for i in range(2)]
    fsq = pW.enter_context(nc.sbuf_tensor("fsq", [128, SUB], F16))
    bt_sb = pW.enter_context(nc.sbuf_tensor("bt_sb", [128, OCH], F32))
    ps_f = pW.enter_context(nc.psum_tensor("ps_f", [128, 2 * SUB], F32))

    p.op("sync", lambda e: e.dma_start(out=bt_sb[:, :], in_=bt[:, :]),
         waits=[(sPv, PR_DONE)], inc=sCst, inc_amt=16)
    BT_LD = sCst.n

    NOS = D // OSUP
    OCPS = OSUP // 128
    wt_war = {}
    f_evac_war = {}
    ftmp_war = [0, 0]
    for osup in range(NOS):
        wb = wtp[osup % 2]
        waits = [(sCastW, 16 * 8)]
        if osup == 0:
            waits.append((sPv, PR_DONE))
        if osup >= 2:
            waits.append((sPm, wt_war[osup - 2]))
        for i in range(ICH):
            p.op("sync",
                 lambda e, wb=wb, osup=osup, i=i: e.dma_start(
                     out=wb[:, i, :],
                     in_=w16d[osup * OSUP:(osup + 1) * OSUP, i * 128:(i + 1) * 128],
                     transpose=True),
                 waits=waits if i == 0 else (), inc=sWtl, inc_amt=16)
        wt_val = sWtl.n
        for sub in range(NSUB):
            for oc in range(OCPS):
                og = osup * OCPS + oc
                fi = (osup * NSUB + sub) * OCPS + oc
                bank = fi % 2
                waits = [(sWtl, wt_val), (sPv, PR_DONE)]
                if fi >= 2:
                    waits.append((sPe, f_evac_war[fi - 2]))
                for i in range(ICH):
                    p.op("tensor",
                         lambda e, wb=wb, i=i, oc=oc, sub=sub, bank=bank: e.matmul(
                             ps_f[:, bank * SUB:(bank + 1) * SUB],
                             wb[:, i, oc * 128:(oc + 1) * 128],
                             y1_16[:, i, sub * SUB:(sub + 1) * SUB],
                             start=(i == 0), stop=False),
                         waits=waits if i == 0 else (), inc=None)
                p.op("tensor",
                     lambda e, og=og, sub=sub, bank=bank: e.matmul(
                         ps_f[:, bank * SUB:(bank + 1) * SUB],
                         bm16[:, og * 128:(og + 1) * 128],
                         pr16[:, sub, :],
                         start=False, stop=True),
                     waits=(), inc=sPm)
                mm_val = sPm.n
                ft = ftmp[fi % 2]
                waits = [(sPm, mm_val), (sCst, BT_LD)]
                if ftmp_war[fi % 2] > 0:
                    waits.append((sPv, ftmp_war[fi % 2]))
                p.op("scalar",
                     lambda e, ft=ft, bank=bank, og=og: e.activation(
                         ft[:, :], ps_f[:, bank * SUB:(bank + 1) * SUB],
                         AF.Identity, bias=bt_sb[:, og:og + 1], scale=1.0),
                     waits=waits, inc=sPe)
                f_evac_war[fi] = sPe.n
                p.op("vector",
                     lambda e, og=og, sub=sub, ft=ft: e.tensor_add(
                         y2_16[:, og, sub * SUB:(sub + 1) * SUB],
                         ft[:, :],
                         y1_16[:, og, sub * SUB:(sub + 1) * SUB]),
                     waits=[(sPe, sPe.n)], inc=sPv)
                ftmp_war[fi % 2] = sPv.n
                p.op("vector",
                     lambda e, og=og, sub=sub: e.tensor_mul(
                         fsq[:, :],
                         y2_16[:, og, sub * SUB:(sub + 1) * SUB],
                         y2_16[:, og, sub * SUB:(sub + 1) * SUB]),
                     waits=(), inc=sPv)
                sq_val = sPv.n
                first = (osup == 0 and oc == 0)
                last = (osup == NOS - 1 and oc == OCPS - 1)
                p.op("tensor",
                     lambda e, og=og, sub=sub, first=first, last=last: e.matmul(
                         ps_st2[0:1, (2 * sub) * SUB:(2 * sub + 1) * SUB], ones16[:, :],
                         y2_16[:, og, sub * SUB:(sub + 1) * SUB],
                         start=first, stop=last),
                     waits=[(sPv, sq_val)], inc=None)
                p.op("tensor",
                     lambda e, sub=sub, first=first, last=last: e.matmul(
                         ps_st2[0:1, (2 * sub + 1) * SUB:(2 * sub + 2) * SUB],
                         ones16[:, :], fsq[:, :],
                         start=first, stop=last),
                     waits=(), inc=sPm if last else None)
        wt_war[osup] = sPm.n
    FFN_DONE = sPm.n
    pW.close()
    pBs.close()
    ph3.close()

    pD = contextlib.ExitStack()
    m2row = pD.enter_context(nc.sbuf_tensor("m2row", [1, 2 * SUB], F32))
    v2row = pD.enter_context(nc.sbuf_tensor("v2row", [1, 2 * SUB], F32))
    mtok = pD.enter_context(nc.sbuf_tensor("mtok", [128, 2], F32))
    y2tok = pD.enter_context(nc.sbuf_tensor("y2tok", [128, OCH * 128], F16))
    fin32 = [pD.enter_context(nc.sbuf_tensor(f"fin{i}", [128, D], F32)) for i in range(2)]
    ps_t = pD.enter_context(nc.psum_tensor("ps_t", [128, 1024], F32))
    ps_t16 = ps_t.ap().bitcast(F16)

    fin_dma_val = [0, 0]
    for sub in range(NSUB):
        p.op("vector",
             lambda e, sub=sub: e.tensor_single_scalar(
                 m2row[0:1, 0:SUB], ps_st2[0:1, (2 * sub) * SUB:(2 * sub + 1) * SUB],
                 1.0 / D, ALU.mult),
             waits=[(sPm, FFN_DONE)], inc=sPv)
        p.op("vector",
             lambda e, sub=sub: e.tensor_single_scalar(
                 m2row[0:1, SUB:2 * SUB],
                 ps_st2[0:1, (2 * sub + 1) * SUB:(2 * sub + 2) * SUB],
                 1.0 / D, ALU.mult),
             waits=(), inc=sPv)
        p.op("vector",
             lambda e: e.tensor_mul(v2row[0:1, 0:SUB], m2row[0:1, 0:SUB],
                                    m2row[0:1, 0:SUB]),
             waits=(), inc=sPv)
        p.op("vector",
             lambda e: e.tensor_sub(v2row[0:1, 0:SUB], m2row[0:1, SUB:2 * SUB],
                                    v2row[0:1, 0:SUB]),
             waits=(), inc=sPv)
        p.op("vector",
             lambda e: e.tensor_single_scalar(v2row[0:1, 0:SUB], v2row[0:1, 0:SUB],
                                              EPS, ALU.add),
             waits=(), inc=sPv)
        varv = sPv.n
        p.op("scalar",
             lambda e: e.activation(v2row[0:1, SUB:2 * SUB], v2row[0:1, 0:SUB], AF.Ln),
             waits=[(sPv, varv)], inc=sPs)
        p.op("scalar",
             lambda e: e.activation(v2row[0:1, 0:SUB], v2row[0:1, SUB:2 * SUB], AF.Exp,
                                    scale=-0.5),
             waits=(), inc=None)
        p.op("scalar",
             lambda e: e.activation(v2row[0:1, SUB:2 * SUB], v2row[0:1, 0:SUB], AF.Identity),
             waits=(), inc=sPs)
        row_val = sPs.n
        for j in range(SUB // 128):
            p.op("tensor",
                 lambda e, j=j: e.transpose(ps_t.ap()[:, 0:1],
                                            m2row[0:1, j * 128:(j + 1) * 128],
                                            ident32[0:1, 0:1]),
                 waits=[(sPs, row_val), (sPv, sPv.n)], inc=sPm)
            p.op("vector",
                 lambda e: e.tensor_copy(mtok[:, 0:1], ps_t.ap()[:, 0:1]),
                 waits=[(sPm, sPm.n)], inc=sPv)
            p.op("tensor",
                 lambda e, j=j: e.transpose(ps_t.ap()[:, 0:1],
                                            v2row[0:1, j * 128:(j + 1) * 128],
                                            ident32[0:1, 0:1]),
                 waits=[(sPv, sPv.n)], inc=sPm)
            p.op("vector",
                 lambda e: e.tensor_copy(mtok[:, 1:2], ps_t.ap()[:, 0:1]),
                 waits=[(sPm, sPm.n)], inc=sPv)
            tr_copy_val = [sPv.n, sPv.n]
            for oc in range(OCH):
                par = oc % 2
                reg = ps_t16[:, 0:128] if par == 0 else ps_t16[:, 1024:1152]
                p.op("tensor",
                     lambda e, oc=oc, sub=sub, j=j, reg=reg: e.transpose(
                         reg,
                         y2_16[:, oc, sub * SUB + j * 128: sub * SUB + (j + 1) * 128],
                         ident16[:, :]),
                     waits=[(sPv, tr_copy_val[par])], inc=sPm)
                p.op("vector",
                     lambda e, oc=oc, reg=reg: e.tensor_copy(
                         y2tok[:, oc * 128:(oc + 1) * 128], reg),
                     waits=[(sPm, sPm.n)], inc=sPv)
                tr_copy_val[par] = sPv.n
            fb = fin32[j % 2]
            waits = [(sPv, sPv.n)]
            if fin_dma_val[j % 2] > 0:
                waits.append((sOut, fin_dma_val[j % 2]))
            p.op("vector",
                 lambda e, fb=fb: e.tensor_scalar(
                     fb[:, :], y2tok[:, :],
                     mtok[:, 0:1], mtok[:, 1:2],
                     op0=ALU.subtract, op1=ALU.mult),
                 waits=waits, inc=sPv)
            p.op("gpsimd",
                 lambda e, fb=fb, sub=sub, j=j: e.dma_start(
                     out=y_out[sub * SUB + j * 128: sub * SUB + (j + 1) * 128, :],
                     in_=fb[:, :]),
                 waits=[(sPv, sPv.n)], inc=sOut, inc_amt=16)
            fin_dma_val[j % 2] = sOut.n

    p.op("gpsimd", lambda e: None, waits=[(sOut, sOut.n)])

    with nc.Block() as block:
        @block.sync
        def _(e):
            for f in p.q["sync"]:
                f(e)

        @block.tensor
        def _(e):
            for f in p.q["tensor"]:
                f(e)

        @block.vector
        def _(e):
            for f in p.q["vector"]:
                f(e)

        @block.scalar
        def _(e):
            for f in p.q["scalar"]:
                f(e)

        @block.gpsimd
        def _(e):
            for f in p.q["gpsimd"]:
                f(e)

    return nc


def kernel(L0_attn_x, L0_attn_attn_qkvw, L0_attn_attn_qkvb, L0_attn_attn_ow,
           L0_attn_attn_ob, L0_ffn_W, L0_ffn_b, L0_ffn_A, L0_ffn_Bm, L0_ffn_Wr):
    import ml_dtypes
    x = np.ascontiguousarray(np.asarray(L0_attn_x, dtype=np.float32).reshape(S, D))
    qkvw = np.asarray(L0_attn_attn_qkvw, dtype=np.float32)
    qkvb = np.asarray(L0_attn_attn_qkvb, dtype=np.float32)
    ow = np.ascontiguousarray(np.asarray(L0_attn_attn_ow, dtype=np.float32))
    ob = np.asarray(L0_attn_attn_ob, dtype=np.float32)
    W = np.ascontiguousarray(np.asarray(L0_ffn_W, dtype=np.float32))
    b = np.asarray(L0_ffn_b, dtype=np.float32)
    A = np.ascontiguousarray(np.asarray(L0_ffn_A, dtype=np.float32).reshape(32, D))
    Bm = np.asarray(L0_ffn_Bm, dtype=np.float32)
    Wr = np.ascontiguousarray(np.asarray(L0_ffn_Wr, dtype=np.float32))

    BmT = np.ascontiguousarray(Bm.transpose(0, 2, 1).reshape(32, D))
    sel = np.zeros((4, 32), dtype=np.float16)
    for e_ in range(4):
        sel[e_, e_ * 8:(e_ + 1) * 8] = 2.0
    id16 = np.eye(128, dtype=np.float16)
    idbf = np.eye(128).astype(ml_dtypes.bfloat16).view(np.uint16)
    id32 = np.eye(128, dtype=np.float32)
    obt = np.ascontiguousarray(ob.reshape(OCH, 128).T)
    btt = np.ascontiguousarray(b.reshape(OCH, 128).T)

    nc = build_nc()

    in_maps = []
    for r in range(NCORES):
        heads = range(NHL * r, NHL * (r + 1))
        cols = []
        bcols = []
        for h in heads:
            for k in range(3):
                cols.append(qkvw[:, k * D + h * 128:k * D + (h + 1) * 128])
                bcols.append(qkvb[k * D + h * 128:k * D + (h + 1) * 128])
        qkvw_r = np.ascontiguousarray(np.concatenate(cols, axis=1))
        qkvb_r = np.ascontiguousarray(np.stack(bcols, axis=1))
        gi = np.zeros((128, OCH), dtype=np.int32)
        for c in range(OCH):
            f = c * 128 + np.arange(128)
            gi[:, c] = f * S + r * TLOC
        in_maps.append({
            "x32": x,
            "x32m": np.ascontiguousarray(x[r * TLOC:(r + 1) * TLOC, :]),
            "qkvw": qkvw_r,
            "qkvb": qkvb_r,
            "ow": ow,
            "obt": obt,
            "W": W,
            "bt": btt,
            "A": A,
            "BmT": BmT,
            "Wr": Wr,
            "sel": sel,
            "id16": id16,
            "idbf": idbf,
            "id32": id32,
            "gidx": gi,
        })

    res = run_bass_kernel_spmd(nc, in_maps, core_ids=list(range(NCORES)))
    out = np.concatenate([res.results[r]["y_out"] for r in range(NCORES)], axis=0)
    return out.reshape(1, S, D).astype(np.float32)


# version 13
# speedup vs baseline: 1.1227x; 1.0096x over previous
import contextlib
import math
import numpy as np

import concourse.bass as bass
import concourse.mybir as mybir
from concourse.bass import IndirectOffsetOnAxis
from concourse.bass_utils import run_bass_kernel_spmd

F32 = mybir.dt.float32
F16 = mybir.dt.float16
BF16 = mybir.dt.bfloat16
I32 = mybir.dt.int32
AF = mybir.ActivationFunctionType
ALU = mybir.AluOpType
AXX = mybir.AxisListType.X

NCORES = 8
S = 8192
D = 4096
HD = 128
NHL = 4
TLOC = S // NCORES
SCALE = 1.0 / math.sqrt(HD)
EPS = 1e-5

CCH = 3 * NHL
ICH = D // 128
TT = 512
NTT = S // TT
TQ = 512
NQT = S // TQ
KCH = S // 128
GS = 3
SUB = 512
NSUB = TLOC // SUB
OCH = D // 128


class Sem:
    def __init__(self, handle, name):
        self.h = handle
        self.name = name
        self.n = 0


class Prog:
    QUEUES = ("sync", "tensor", "vector", "scalar", "gpsimd")

    def __init__(self):
        self.q = {k: [] for k in self.QUEUES}

    def op(self, qn, fn, waits=(), inc=None, inc_amt=1, defer=None):
        for s, v in waits:
            assert v <= s.n, f"wait on {s.name} for {v} but only {s.n} recorded"
        wl = [(s.h, v) for s, v in waits if v > 0]
        if inc is not None:
            inc.n += inc_amt
            ih, ia = inc.h, inc_amt
        else:
            ih, ia = None, 0

        def run(eng, wl=wl, fn=fn, ih=ih, ia=ia):
            for h, v in wl:
                eng.wait_ge(h, v)
            ins = fn(eng)
            if ih is not None:
                if ins is None:
                    ins = eng.nop()
                ins.then_inc(ih, ia)
        if defer is not None:
            defer.append((qn, run))
        else:
            self.q[qn].append(run)

    def flush(self, defer):
        for qn, run in defer:
            self.q[qn].append(run)
        defer.clear()


def build_nc():
    nc = bass.Bass("TRN2", target_bir_lowering=False)

    x32 = nc.dram_tensor("x32", [S, D], F32, kind="ExternalInput").ap()
    x32m = nc.dram_tensor("x32m", [TLOC, D], F32, kind="ExternalInput").ap()
    qkvw = nc.dram_tensor("qkvw", [D, CCH * 128], F32, kind="ExternalInput").ap()
    qkvb = nc.dram_tensor("qkvb", [128, CCH], F32, kind="ExternalInput").ap()
    ow = nc.dram_tensor("ow", [D, D], F32, kind="ExternalInput").ap()
    obt = nc.dram_tensor("obt", [128, OCH], F32, kind="ExternalInput").ap()
    Wt = nc.dram_tensor("W", [D, D], F32, kind="ExternalInput").ap()
    bt = nc.dram_tensor("bt", [128, OCH], F32, kind="ExternalInput").ap()
    A = nc.dram_tensor("A", [32, D], F32, kind="ExternalInput").ap()
    BmT = nc.dram_tensor("BmT", [32, D], F32, kind="ExternalInput").ap()
    Wr = nc.dram_tensor("Wr", [4, D], F32, kind="ExternalInput").ap()
    sel = nc.dram_tensor("sel", [4, 32], F16, kind="ExternalInput").ap()
    id16 = nc.dram_tensor("id16", [128, 128], F16, kind="ExternalInput").ap()
    idbf = nc.dram_tensor("idbf", [128, 128], BF16, kind="ExternalInput").ap()
    id32 = nc.dram_tensor("id32", [128, 128], F32, kind="ExternalInput").ap()
    gidx = nc.dram_tensor("gidx", [128, OCH], I32, kind="ExternalInput").ap()

    y_out = nc.dram_tensor("y_out", [TLOC, D], F32, kind="ExternalOutput").ap()
    x16d = nc.dram_tensor("x16d", [S, D], F16).ap()
    x16md = nc.dram_tensor("x16md", [TLOC, D], F16).ap()
    w16d = nc.dram_tensor("w16d", [D, D], F16).ap()
    qTd = nc.dram_tensor("qTd", [NHL * 128, S], F16).ap()
    kTd = nc.dram_tensor("kTd", [NHL * 128, S], F16).ap()
    vd = nc.dram_tensor("vd", [S, NHL * 128], BF16).ap()
    ag_in = nc.dram_tensor("ag_in", [NHL * 128, S], F16).ap()
    ag_out = nc.dram_tensor("ag_out", [NHL * 1024, S], F16, addr_space="Shared").ap()
    ctx_mine_d = nc.dram_tensor("ctx_mine_d", [D, TLOC], F16).ap()
    ag_out_flat = ag_out.rearrange("a b -> (a b)").rearrange("(n one) -> n one", one=1)

    p = Prog()
    stack = contextlib.ExitStack()
    nc._kernel_stack = stack

    def sem(name):
        return Sem(stack.enter_context(nc.semaphore(name)), name)

    sCast = sem("sCast")
    sCastM = sem("sCastM")
    sCastW = sem("sCastW")
    sWld = sem("sWld")
    sXT = sem("sXT")
    sMM = sem("sMM")
    sEv = sem("sEv")
    sVt = sem("sVt")
    sVe = sem("sVe")
    sSp = sem("sSp")
    sNq = sem("sNq")
    sNm = sem("sNm")
    sBh = sem("sBh")
    sLd = sem("sLd")
    sST = sem("sST")
    sEX = sem("sEX")
    sPV = sem("sPV")
    sTR = sem("sTR")
    sTRb = sem("sTRb")
    sEp = sem("sEp")
    sAG = sem("sAG")
    sCC = sem("sCC")
    sGa = sem("sGa")
    sOwl = sem("sOwl")
    sXtp = sem("sXtp")
    sPm = sem("sPm")
    sPe = sem("sPe")
    sPv = sem("sPv")
    sWtl = sem("sWtl")
    sPs = sem("sPs")
    sOut = sem("sOut")
    sIdx = sem("sIdx")
    sCst = sem("sCst")

    ident16 = stack.enter_context(nc.sbuf_tensor("ident16", [128, 128], F16))
    identbf = stack.enter_context(nc.sbuf_tensor("identbf", [128, 128], BF16))
    ident32 = stack.enter_context(nc.sbuf_tensor("ident32", [128, 128], F32))
    ones16 = stack.enter_context(nc.sbuf_tensor("ones16", [128, 1], F16))
    onesbf = stack.enter_context(nc.sbuf_tensor("onesbf", [128, 1], BF16))
    ones32r = stack.enter_context(nc.sbuf_tensor("ones32r", [1, 128], F32))
    qkvb_sb = stack.enter_context(nc.sbuf_tensor("qkvb_sb", [128, CCH], F32))
    bias_h = stack.enter_context(nc.sbuf_tensor("bias_h", [128, NHL], F32))
    norm_acc = stack.enter_context(nc.sbuf_tensor("norm_acc", [1, 2 * NHL], F32))
    rowtmp = stack.enter_context(nc.sbuf_tensor("rowtmp", [1, 8], F32))

    p.op("sync", lambda e: e.dma_start(out=ident16[:, :], in_=id16[:, :]), inc=sCst, inc_amt=16)
    p.op("sync", lambda e: e.dma_start(out=identbf[:, :], in_=idbf[:, :]), inc=sCst, inc_amt=16)
    p.op("sync", lambda e: e.dma_start(out=ident32[:, :], in_=id32[:, :]), inc=sCst, inc_amt=16)
    p.op("sync", lambda e: e.dma_start(out=qkvb_sb[:, :], in_=qkvb[:, :]), inc=sCst, inc_amt=16)
    p.op("vector", lambda e: e.memset(ones16[:, :], 1.0), inc=sPv)
    p.op("vector", lambda e: e.memset(onesbf[:, :], 1.0), inc=sPv)
    p.op("vector", lambda e: e.memset(ones32r[:, :], 1.0), inc=sPv)
    p.op("vector", lambda e: e.memset(norm_acc[:, :], 0.0), inc=sPv)
    CONST_LD = sCst.n
    CONST_V = sPv.n

    ph1 = contextlib.ExitStack()
    qkvw_sb = ph1.enter_context(nc.sbuf_tensor("qkvw_sb", [128, CCH * ICH, 128], F16))
    xt_sb = [ph1.enter_context(nc.sbuf_tensor(f"xt{i}", [128, ICH, TT], F16)) for i in range(2)]
    stg = [ph1.enter_context(nc.sbuf_tensor(f"stg{i}", [128, TT], F16)) for i in range(3)]
    stgv = [ph1.enter_context(nc.sbuf_tensor(f"stgv{i}", [128, TT], BF16)) for i in range(2)]
    vstg = [ph1.enter_context(nc.sbuf_tensor(f"vstg{i}", [128, TT], BF16)) for i in range(2)]
    sq_sb = [ph1.enter_context(nc.sbuf_tensor(f"sq_sb{i}", [128, TT], F16)) for i in range(2)]
    ps_qkv = ph1.enter_context(nc.psum_tensor("ps_qkv", [128, 3 * TT], F32))
    ps_vt = ph1.enter_context(nc.psum_tensor("ps_vt", [128, TT], BF16))
    ps_qn = ph1.enter_context(nc.psum_tensor("ps_qn", [128, 512], F32))
    ps_bb = ph1.enter_context(nc.psum_tensor("ps_bb", [128, 512], F32))

    for c in range(CCH):
        p.op("gpsimd",
             lambda e, c=c: e.dma_start(
                 out=qkvw_sb[:, c * ICH:(c + 1) * ICH, :],
                 in_=qkvw[:, c * 128:(c + 1) * 128].rearrange("(i p) c -> p i c", p=128)),
             inc=sWld, inc_amt=16)
    QKVW_LD = sWld.n

    panel_mm_done = {}
    qk_spill_val = [0, 0, 0]
    v_spill_val = [0, 0]
    vstg_evac_val = [0, 0]
    sq_mm_val = [0, 0]
    qk_ctr = 0
    v_ctr = 0
    norm_defer = []
    for t in range(NTT):
        cwaits = []
        if t >= 2:
            cwaits.append((sXT, 16 * 32 * (t - 1)))
        p.op("gpsimd", lambda e, t=t: e.dma_start(out=x16d[t * TT:(t + 1) * TT, :],
                                                  in_=x32[t * TT:(t + 1) * TT, :]),
             waits=cwaits, inc=sCast, inc_amt=16)
        waits = [(sCast, 16 * (t + 1))]
        if t >= 2:
            waits.append((sMM, panel_mm_done[t - 2]))
        for i in range(ICH):
            p.op("sync",
                 lambda e, t=t, i=i: e.dma_start(
                     out=xt_sb[t % 2][:, i, :],
                     in_=x16d[t * TT:(t + 1) * TT, i * 128:(i + 1) * 128],
                     transpose=True),
                 waits=waits if i == 0 else (), inc=sXT, inc_amt=16)

        for c in range(CCH):
            tile_idx = t * CCH + c
            bank = tile_idx % 3
            waits = [(sXT, 16 * ICH * (t + 1))]
            if tile_idx == 0:
                waits.append((sWld, QKVW_LD))
                waits.append((sCst, CONST_LD))
                waits.append((sPv, CONST_V))
            if tile_idx >= 3:
                waits.append((sEv, tile_idx - 3 + 1))
            for i in range(ICH):
                p.op("tensor",
                     lambda e, c=c, t=t, i=i, bank=bank: e.matmul(
                         ps_qkv[:, bank * TT:(bank + 1) * TT],
                         qkvw_sb[:, c * ICH + i, :],
                         xt_sb[t % 2][:, i, :],
                         start=(i == 0), stop=(i == ICH - 1)),
                     waits=waits if i == 0 else (),
                     inc=sMM if i == ICH - 1 else None)
            mm_val = sMM.n
            p.flush(norm_defer)

            lh, kind = c // 3, c % 3
            if kind in (0, 1):
                bidx = qk_ctr % 3
                sqb = sq_sb[qk_ctr % 2]
                sq_war = sq_mm_val[qk_ctr % 2]
                qk_ctr += 1
                sb = stg[bidx]
                waits = [(sMM, mm_val), (sCst, CONST_LD)]
                if qk_spill_val[bidx] > 0:
                    waits.append((sSp, qk_spill_val[bidx]))
                p.op("scalar",
                     lambda e, sb=sb, bank=bank, c=c: e.activation(
                         sb[:, :], ps_qkv[:, bank * TT:(bank + 1) * TT],
                         AF.Identity, bias=qkvb_sb[:, c:c + 1], scale=1.0),
                     waits=waits, inc=sEv)
                ev_val = sEv.n
                dst = qTd if kind == 0 else kTd
                p.op("sync",
                     lambda e, sb=sb, dst=dst, lh=lh, t=t: e.dma_start(
                         out=dst[lh * 128:(lh + 1) * 128, t * TT:(t + 1) * TT],
                         in_=sb[:, :]),
                     waits=[(sEv, ev_val)], inc=sSp, inc_amt=16)
                qk_spill_val[bidx] = sSp.n
                waits = [(sEv, ev_val)]
                if sq_war > 0:
                    waits.append((sNm, sq_war))
                p.op("vector",
                     lambda e, sb=sb, sqb=sqb: e.tensor_mul(sqb[:, :], sb[:, :], sb[:, :]),
                     waits=waits, inc=sNq)
                p.op("tensor",
                     lambda e, sqb=sqb: e.matmul(ps_qn[0:1, 0:TT], ones16[:, :], sqb[:, :],
                                                 start=True, stop=True),
                     waits=[(sNq, sNq.n), (sNq, sNq.n)], inc=sNm, defer=norm_defer)
                sq_mm_val[(qk_ctr - 1) % 2] = sNm.n
                acc_col = lh * 2 + kind
                p.op("vector",
                     lambda e: e.reduce_max(rowtmp[0:1, 0:1], ps_qn[0:1, 0:TT], axis=AXX),
                     waits=[(sNm, sNm.n)], inc=sNq)
                p.op("vector",
                     lambda e, acc_col=acc_col: e.tensor_max(
                         norm_acc[0:1, acc_col:acc_col + 1],
                         norm_acc[0:1, acc_col:acc_col + 1],
                         rowtmp[0:1, 0:1]),
                     waits=(), inc=sNq)
            else:
                sbv = stgv[v_ctr % 2]
                vb = vstg[v_ctr % 2]
                waits = [(sMM, mm_val), (sCst, CONST_LD)]
                if vstg_evac_val[v_ctr % 2] > 0:
                    waits.append((sVt, vstg_evac_val[v_ctr % 2]))
                p.op("scalar",
                     lambda e, sbv=sbv, bank=bank, c=c: e.activation(
                         sbv[:, :], ps_qkv[:, bank * TT:(bank + 1) * TT],
                         AF.Identity, bias=qkvb_sb[:, c:c + 1], scale=1.0),
                     waits=waits, inc=sEv)
                ev_val = sEv.n
                for j in range(4):
                    p.op("tensor",
                         lambda e, sbv=sbv, j=j: e.transpose(
                             ps_vt[:, j * 128:(j + 1) * 128],
                             sbv[:, j * 128:(j + 1) * 128], identbf[:, :]),
                         waits=([(sEv, ev_val), (sVe, sVe.n)] if j == 0 else ()),
                         inc=sVt if j == 3 else None)
                vstg_evac_val[v_ctr % 2] = sVt.n
                vt_val = sVt.n
                waits = [(sVt, vt_val)]
                if v_spill_val[v_ctr % 2] > 0:
                    waits.append((sSp, v_spill_val[v_ctr % 2]))
                p.op("vector",
                     lambda e, vb=vb: e.tensor_copy(vb[:, :], ps_vt[:, :]),
                     waits=waits, inc=sVe)
                ve_val = sVe.n
                p.op("sync",
                     lambda e, vb=vb, lh=lh, t=t: e.dma_start(
                         out=vd[t * TT:(t + 1) * TT, lh * 128:(lh + 1) * 128]
                         .rearrange("(j p) d -> p j d", p=128),
                         in_=vb[:, :].rearrange("p (j d) -> p j d", j=4)),
                     waits=[(sVe, ve_val)], inc=sSp, inc_amt=16)
                v_spill_val[v_ctr % 2] = sSp.n
                v_ctr += 1
        panel_mm_done[t] = sMM.n
    p.flush(norm_defer)

    nq0 = sNq.n
    for lh in range(NHL):
        p.op("vector",
             lambda e, lh=lh: e.tensor_mul(rowtmp[0:1, 1:2],
                                           norm_acc[0:1, 2 * lh:2 * lh + 1],
                                           norm_acc[0:1, 2 * lh + 1:2 * lh + 2]),
             waits=[(sNq, nq0)], inc=sNq)
        p.op("scalar",
             lambda e: e.activation(rowtmp[0:1, 2:3], rowtmp[0:1, 1:2], AF.Ln),
             waits=[(sNq, sNq.n)], inc=sBh)
        p.op("scalar",
             lambda e: e.activation(rowtmp[0:1, 3:4], rowtmp[0:1, 2:3], AF.Exp, scale=0.5),
             waits=(), inc=None)
        p.op("scalar",
             lambda e: e.activation(rowtmp[0:1, 5:6], rowtmp[0:1, 3:4], AF.Identity),
             waits=(), inc=sBh)
        p.op("vector",
             lambda e: e.tensor_single_scalar(rowtmp[0:1, 4:5], rowtmp[0:1, 3:4],
                                              -SCALE, ALU.mult),
             waits=[(sBh, sBh.n)], inc=sNq)
        p.op("tensor",
             lambda e: e.matmul(ps_bb[:, 0:1], ones32r[:, :], rowtmp[0:1, 4:5],
                                start=True, stop=True),
             waits=[(sNq, sNq.n), (sBh, sBh.n)], inc=sNm)
        p.op("vector",
             lambda e, lh=lh: e.tensor_copy(bias_h[:, lh:lh + 1], ps_bb[:, 0:1]),
             waits=[(sNm, sNm.n)], inc=sBh)
    BH_DONE = sBh.n
    P1_SPILLS = sSp.n
    p.op("gpsimd", lambda e: e.dma_start(out=x16md[:, :], in_=x32m[:, :]),
         waits=[(sCast, 16 * NTT)], inc=sCastM, inc_amt=16)
    for c in range(8):
        p.op("gpsimd", lambda e, c=c: e.dma_start(out=w16d[c * 512:(c + 1) * 512, :],
                                                  in_=Wt[c * 512:(c + 1) * 512, :]),
             inc=sCastW, inc_amt=16)

    ph1.close()
    ph2 = contextlib.ExitStack()
    kT_sb = [ph2.enter_context(nc.sbuf_tensor(f"kT{i}", [128, S], F16)) for i in range(2)]
    qT_sb = [ph2.enter_context(nc.sbuf_tensor(f"qT{i}", [128, S], F16)) for i in range(2)]
    v_sb = [ph2.enter_context(nc.sbuf_tensor(f"v{i}", [128, KCH, 128], BF16)) for i in range(2)]
    pt_sb = [ph2.enter_context(nc.sbuf_tensor(f"pt{i}", [128, 2 * TQ], BF16)) for i in range(3)]
    tree_bufs = [ph2.enter_context(nc.sbuf_tensor(f"tr{i}", [128, TQ], BF16)) for i in range(16)]
    rrow = ph2.enter_context(nc.sbuf_tensor("rrow", [1, 2 * TQ], F32))
    rb_sb = ph2.enter_context(nc.sbuf_tensor("rb_sb", [128, 2 * TQ], F32))
    ctxn = [ph2.enter_context(nc.sbuf_tensor(f"ctxn{i}", [128, TQ], F16)) for i in range(2)]
    ps_st = [ph2.enter_context(nc.psum_tensor(f"ps_stt{i}", [128, 2 * TQ], F32))
             for i in range(3)]
    ps_ctx2 = ph2.enter_context(nc.psum_tensor("ps_ctx2", [128, 2 * TQ], F32))

    st_free = [0, 0, 0]
    pt_free_pv = [0, 0, 0]
    pt_free_trA = [0, 0, 0]
    pt_free_trB = [0, 0, 0]
    ctxn_dma_val = [0, 0]
    head_pv_done = {}

    def ap_of(tok):
        if tok[0] == "pt":
            return pt_sb[tok[1]][:, tok[2] * TQ:(tok[2] + 1) * TQ]
        return tok[1][:, :]

    class Tree:

        def __init__(self, pool):
            self.free = list(pool)
            self.occ = {}

        def insert_pair_ops(self, tokx, toky):
            ops = []
            outb = self.free.pop()
            ops.append(lambda e, a=tokx, bb=toky, outb=outb: e.tensor_add(
                outb[:, :], ap_of(a), ap_of(bb)))
            cur = ("buf", outb)
            lvl = 0
            while lvl in self.occ:
                prev = self.occ.pop(lvl)
                outb = prev[1]
                ops.append(lambda e, a=cur, bb=prev, outb=outb: e.tensor_add(
                    outb[:, :], ap_of(a), ap_of(bb)))
                if cur[1] is not outb:
                    self.free.append(cur[1])
                cur = ("buf", outb)
                lvl += 1
            self.occ[lvl] = cur
            return ops

        def root(self):
            assert len(self.occ) == 1
            return self.occ.popitem()[1][1]

    for lh in range(NHL):
        b = lh % 2
        waits = [(sSp, P1_SPILLS), (sBh, BH_DONE)]
        if lh >= 2:
            waits.append((sPV, head_pv_done[lh - 2]))
        p.op("sync", lambda e, b=b, lh=lh: e.dma_start(
            out=kT_sb[b][:, :], in_=kTd[lh * 128:(lh + 1) * 128, :]),
            waits=waits, inc=sLd, inc_amt=16)
        p.op("sync", lambda e, b=b, lh=lh: e.dma_start(
            out=qT_sb[b][:, :], in_=qTd[lh * 128:(lh + 1) * 128, :]),
            inc=sLd, inc_amt=16)
        p.op("sync", lambda e, b=b, lh=lh: e.dma_start(
            out=v_sb[b][:, :, :],
            in_=vd[:, lh * 128:(lh + 1) * 128].rearrange("(j p) d -> p j d", p=128)),
            inc=sLd, inc_amt=16)
        ld_val = sLd.n

        for qp in range(NQT // 2):
            qA, qB = 2 * qp, 2 * qp + 1
            ctx_free_val = sEp.n
            treeA = Tree(tree_bufs[0:8])
            treeB = Tree(tree_bufs[8:16])
            exp_vals = {}

            def emit_pv(kc, ctx_free_val=None):
                par = kc % 3
                ptb = pt_sb[par]
                w2 = [(sEX, exp_vals[kc])]
                if kc == 0:
                    w2.append((sEp, ctx_free_val))
                p.op("tensor",
                     lambda e, b=b, kc=kc, ptb=ptb: e.matmul(
                         ps_ctx2[:, 0:TQ], v_sb[b][:, kc, :], ptb[:, 0:TQ],
                         start=(kc == 0), stop=(kc == KCH - 1)),
                     waits=w2, inc=None)
                p.op("tensor",
                     lambda e, b=b, kc=kc, ptb=ptb: e.matmul(
                         ps_ctx2[:, TQ:2 * TQ], v_sb[b][:, kc, :], ptb[:, TQ:2 * TQ],
                         start=(kc == 0), stop=(kc == KCH - 1)),
                     waits=(), inc=sPV)
                pt_free_pv[par] = sPV.n

            for kc in range(KCH):
                par = kc % 3
                stps = ps_st[par]
                ptb = pt_sb[par]
                waits = []
                if kc == 0 and qp == 0:
                    waits.append((sLd, ld_val))
                    if lh == 0:
                        waits.append((sBh, BH_DONE))
                if kc == 0:
                    waits.append((sEp, ctx_free_val))
                if st_free[par] > 0:
                    waits.append((sEX, st_free[par]))
                p.op("tensor",
                     lambda e, b=b, kc=kc, qA=qA, stps=stps: e.matmul(
                         stps[:, 0:TQ],
                         kT_sb[b][:, kc * 128:(kc + 1) * 128],
                         qT_sb[b][:, qA * TQ:(qA + 1) * TQ],
                         start=True, stop=True),
                     waits=waits, inc=None)
                p.op("tensor",
                     lambda e, b=b, kc=kc, qB=qB, stps=stps: e.matmul(
                         stps[:, TQ:2 * TQ],
                         kT_sb[b][:, kc * 128:(kc + 1) * 128],
                         qT_sb[b][:, qB * TQ:(qB + 1) * TQ],
                         start=True, stop=True),
                     waits=(), inc=sST)
                st_val = sST.n
                if kc >= 1:
                    emit_pv(kc - 1, ctx_free_val)

                waits = [(sST, st_val)]
                if pt_free_pv[par] > 0:
                    waits.append((sPV, pt_free_pv[par]))
                    waits.append((sTR, pt_free_trA[par]))
                    waits.append((sTRb, pt_free_trB[par]))
                p.op("scalar",
                     lambda e, ptb=ptb, stps=stps, lh=lh: e.activation(
                         ptb[:, :], stps[:, :], AF.Exp,
                         bias=bias_h[:, lh:lh + 1], scale=SCALE),
                     waits=waits, inc=sEX)
                exp_vals[kc] = sEX.n
                st_free[par] = sEX.n

                if kc % 2 == 1:
                    par0 = (kc - 1) % 3
                    opsA = treeA.insert_pair_ops(("pt", par0, 0), ("pt", par, 0))
                    for oi, fn_ in enumerate(opsA):
                        p.op("vector", fn_,
                             waits=[(sEX, exp_vals[kc])] if oi == 0 else (),
                             inc=sTR if oi == len(opsA) - 1 else None)
                    pt_free_trA[par0] = pt_free_trA[par] = sTR.n
                    opsB = treeB.insert_pair_ops(("pt", par0, 1), ("pt", par, 1))
                    for oi, fn_ in enumerate(opsB):
                        p.op("vector", fn_,
                             waits=(),
                             inc=sTRb if oi == len(opsB) - 1 else None)
                    pt_free_trB[par0] = pt_free_trB[par] = sTRb.n
            emit_pv(KCH - 1)
            pv_all = sPV.n

            rootA, rootB = treeA.root(), treeB.root()
            trA_val, trB_val = sTR.n, sTRb.n
            p.op("tensor",
                 lambda e, rootA=rootA: e.matmul(ps_st[0][0:1, 0:TQ], onesbf[:, :],
                                                 rootA[:, :], start=True, stop=True),
                 waits=[(sTR, trA_val), (sEX, exp_vals[KCH - 1]), (sPv, CONST_V)],
                 inc=None)
            p.op("tensor",
                 lambda e, rootB=rootB: e.matmul(ps_st[1][0:1, 0:TQ], onesbf[:, :],
                                                 rootB[:, :], start=True, stop=True),
                 waits=[(sTRb, trB_val)], inc=sNm)
            mm_val = sNm.n
            p.op("vector",
                 lambda e: e.reciprocal(rrow[0:1, 0:TQ], ps_st[0][0:1, 0:TQ]),
                 waits=[(sNm, mm_val)], inc=None)
            p.op("vector",
                 lambda e: e.reciprocal(rrow[0:1, TQ:2 * TQ], ps_st[1][0:1, 0:TQ]),
                 waits=(), inc=sEp)
            rec_val = sEp.n
            p.op("tensor",
                 lambda e: e.matmul(ps_st[0][:, 0:TQ], ones32r[:, :], rrow[0:1, 0:TQ],
                                    start=True, stop=True),
                 waits=[(sEp, rec_val)], inc=None)
            p.op("tensor",
                 lambda e: e.matmul(ps_st[1][:, 0:TQ], ones32r[:, :], rrow[0:1, TQ:2 * TQ],
                                    start=True, stop=True),
                 waits=(), inc=sNm)
            bc_val = sNm.n
            p.op("vector",
                 lambda e: e.tensor_copy(rb_sb[:, 0:TQ], ps_st[0][:, 0:TQ]),
                 waits=[(sNm, bc_val)], inc=None)
            p.op("vector",
                 lambda e: e.tensor_copy(rb_sb[:, TQ:2 * TQ], ps_st[1][:, 0:TQ]),
                 waits=(), inc=sEp)
            rb_val = sEp.n
            waits = [(sPV, pv_all), (sEp, rb_val)]
            if ctxn_dma_val[0] > 0:
                waits.append((sAG, max(ctxn_dma_val[0], ctxn_dma_val[1])))
            p.op("vector",
                 lambda e: e.tensor_tensor(ctxn[0][:, :], ps_ctx2[:, 0:TQ],
                                           rb_sb[:, 0:TQ], op=ALU.mult),
                 waits=waits, inc=sEp)
            evA = sEp.n
            p.op("vector",
                 lambda e: e.tensor_tensor(ctxn[1][:, :], ps_ctx2[:, TQ:2 * TQ],
                                           rb_sb[:, TQ:2 * TQ], op=ALU.mult),
                 waits=(), inc=sEp)
            evB = sEp.n
            p.op("sync",
                 lambda e, lh=lh, qA=qA: e.dma_start(
                     out=ag_in[lh * 128:(lh + 1) * 128, qA * TQ:(qA + 1) * TQ],
                     in_=ctxn[0][:, :]),
                 waits=[(sEp, evA)], inc=sAG, inc_amt=16)
            ctxn_dma_val[0] = sAG.n
            p.op("sync",
                 lambda e, lh=lh, qB=qB: e.dma_start(
                     out=ag_in[lh * 128:(lh + 1) * 128, qB * TQ:(qB + 1) * TQ],
                     in_=ctxn[1][:, :]),
                 waits=[(sEp, evB)], inc=sAG, inc_amt=16)
            ctxn_dma_val[1] = sAG.n
        head_pv_done[lh] = sPV.n
        ag_head_val = sAG.n
        p.op("gpsimd",
             lambda e, lh=lh: e.collective_compute(
                 "AllGather", ALU.bypass, replica_groups=[list(range(NCORES))],
                 ins=[ag_in[lh * 128:(lh + 1) * 128, :].opt()],
                 outs=[ag_out[lh * 1024:(lh + 1) * 1024, :].opt()]),
             waits=[(sAG, ag_head_val)], inc=sCC, inc_amt=1)

    AG_ALL = sAG.n

    ph2.close()
    ph3 = contextlib.ExitStack()
    y1_16 = ph3.enter_context(nc.sbuf_tensor("y1_16", [128, OCH, TLOC], F16))
    idx_sb = ph3.enter_context(nc.sbuf_tensor("idx_sb", [128, OCH], I32))
    gstg = [ph3.enter_context(nc.sbuf_tensor(f"gstg{i}", [128, TLOC], F16)) for i in range(2)]
    p.op("sync", lambda e: e.dma_start(out=idx_sb[:, :], in_=gidx[:, :]),
         waits=[(sAG, AG_ALL)], inc=sIdx, inc_amt=16)
    gstg_out_val = [0, 0]
    for c in range(OCH):
        waits = [(sIdx, 16), (sCC, NHL)] if c == 0 else []
        if gstg_out_val[c % 2] > 0:
            waits.append((sGa, gstg_out_val[c % 2]))
        p.op("gpsimd",
             lambda e, c=c: e.indirect_dma_start(
                 gstg[c % 2][:, :], None, ag_out_flat[0:128, :],
                 IndirectOffsetOnAxis(idx_sb[:, c:c + 1], axis=0)),
             waits=waits, inc=sGa, inc_amt=16)
        gv = sGa.n
        p.op("gpsimd",
             lambda e, c=c: e.dma_start(
                 out=ctx_mine_d[c * 128:(c + 1) * 128, :], in_=gstg[c % 2][:, :]),
             waits=[(sGa, gv)], inc=sGa, inc_amt=16)
        gstg_out_val[c % 2] = sGa.n
    GATHER_ALL = sGa.n

    pA = contextlib.ExitStack()
    ctxp = pA.enter_context(nc.sbuf_tensor("ctxp", [128, OCH, SUB], F16))
    owp = [pA.enter_context(nc.sbuf_tensor(f"owp{i}", [128, ICH, 128], F16)) for i in range(2)]
    xtp = [pA.enter_context(nc.sbuf_tensor(f"xtp{i}", [128, SUB], F16)) for i in range(2)]
    ysb = pA.enter_context(nc.sbuf_tensor("ysb", [128, OCH, SUB], F32))
    y16t = pA.enter_context(nc.sbuf_tensor("y16t", [128, SUB], F16))
    sq16t = pA.enter_context(nc.sbuf_tensor("sq16t", [128, SUB], F16))
    obt_sb = pA.enter_context(nc.sbuf_tensor("obt_sb", [128, OCH], F32))
    mrow = pA.enter_context(nc.sbuf_tensor("mrow", [1, 2 * SUB], F32))
    vrow = pA.enter_context(nc.sbuf_tensor("vrow", [1, 2 * SUB], F32))
    mb_sb = pA.enter_context(nc.sbuf_tensor("mb_sb", [128, SUB], F32))
    rstdb = pA.enter_context(nc.sbuf_tensor("rstdb", [128, SUB], F32))
    tnorm = [pA.enter_context(nc.sbuf_tensor(f"tnorm{i}", [128, SUB], F32)) for i in range(2)]
    ps_op = pA.enter_context(nc.psum_tensor("ps_op", [128, 2 * SUB], F32))
    ps_st1 = pA.enter_context(nc.psum_tensor("ps_st1", [128, 2 * SUB], F32))
    ps_bc = pA.enter_context(nc.psum_tensor("ps_bc", [128, SUB], F32))

    p.op("sync", lambda e: e.dma_start(out=obt_sb[:, :], in_=obt[:, :]),
         waits=[(sAG, AG_ALL)], inc=sCst, inc_amt=16)
    OBT_LD = sCst.n

    ow_war = {}
    xt_war = {}
    evac_war = {}
    tnorm_war = [0, 0]
    ga_war = 0
    for sub in range(NSUB):
        waits = [(sGa, GATHER_ALL)]
        if sub > 0:
            waits.append((sPm, ga_war))
        p.op("sync",
             lambda e, sub=sub: e.dma_start(
                 out=ctxp[:, :, :],
                 in_=ctx_mine_d[:, sub * SUB:(sub + 1) * SUB]
                 .rearrange("(c p) t -> p c t", p=128)),
             waits=waits, inc=sGa, inc_amt=16)
        ga_val = sGa.n

        for c in range(OCH):
            ci = sub * OCH + c
            ob_ = owp[ci % 2]
            waits = []
            if ci >= 2:
                waits.append((sPm, ow_war[ci - 2]))
            p.op("gpsimd",
                 lambda e, ob_=ob_, c=c: e.dma_start(
                     out=ob_[:, :, :],
                     in_=ow[:, c * 128:(c + 1) * 128].rearrange("(i p) c -> p i c", p=128)),
                 waits=waits, inc=sOwl, inc_amt=16)
            owl_val = sOwl.n
            xb = xtp[ci % 2]
            waits = [(sCastM, 16)]
            if ci >= 2:
                waits.append((sPv, xt_war[ci - 2]))
            p.op("sync",
                 lambda e, xb=xb, c=c, sub=sub: e.dma_start(
                     out=xb[:, :],
                     in_=x16md[sub * SUB:(sub + 1) * SUB, c * 128:(c + 1) * 128],
                     transpose=True),
                 waits=waits, inc=sXtp, inc_amt=16)
            xtp_val = sXtp.n
            bank = ci % 2
            waits = [(sOwl, owl_val), (sGa, ga_val)]
            if ci >= 2:
                waits.append((sPe, evac_war[ci - 2]))
            for i in range(ICH):
                p.op("tensor",
                     lambda e, ob_=ob_, i=i, bank=bank: e.matmul(
                         ps_op[:, bank * SUB:(bank + 1) * SUB],
                         ob_[:, i, :], ctxp[:, i, :],
                         start=(i == 0), stop=(i == ICH - 1)),
                     waits=waits if i == 0 else (),
                     inc=sPm if i == ICH - 1 else None)
            ow_war[ci] = sPm.n
            mm_val = sPm.n
            tn = tnorm[ci % 2]
            waits = [(sPm, mm_val), (sCst, OBT_LD)]
            if tnorm_war[ci % 2] > 0:
                waits.append((sPv, tnorm_war[ci % 2]))
            p.op("scalar",
                 lambda e, tn=tn, bank=bank, c=c: e.activation(
                     tn[:, :], ps_op[:, bank * SUB:(bank + 1) * SUB],
                     AF.Identity, bias=obt_sb[:, c:c + 1], scale=1.0),
                 waits=waits, inc=sPe)
            evac_war[ci] = sPe.n
            p.op("vector",
                 lambda e, c=c, xb=xb, tn=tn: e.tensor_add(ysb[:, c, :], tn[:, :], xb[:, :]),
                 waits=[(sPe, sPe.n), (sXtp, xtp_val)], inc=sPv)
            tnorm_war[ci % 2] = sPv.n
            xt_war[ci] = sPv.n
            p.op("vector",
                 lambda e, c=c: e.tensor_copy(y16t[:, :], ysb[:, c, :]),
                 waits=(), inc=sPv)
            p.op("vector",
                 lambda e: e.tensor_mul(sq16t[:, :], y16t[:, :], y16t[:, :]),
                 waits=(), inc=sPv)
            sqv = sPv.n
            p.op("tensor",
                 lambda e, c=c: e.matmul(ps_st1[0:1, 0:SUB], ones16[:, :], y16t[:, :],
                                         start=(c == 0), stop=(c == OCH - 1)),
                 waits=[(sPv, sqv)],
                 inc=None)
            p.op("tensor",
                 lambda e, c=c: e.matmul(ps_st1[0:1, SUB:2 * SUB], ones16[:, :], sq16t[:, :],
                                         start=(c == 0), stop=(c == OCH - 1)),
                 waits=(),
                 inc=sPm if c == OCH - 1 else None)
        ga_war = sPm.n
        st_done = sPm.n
        p.op("vector",
             lambda e: e.tensor_single_scalar(mrow[0:1, 0:SUB], ps_st1[0:1, 0:SUB],
                                              1.0 / D, ALU.mult),
             waits=[(sPm, st_done)], inc=sPv)
        p.op("vector",
             lambda e: e.tensor_single_scalar(mrow[0:1, SUB:2 * SUB], ps_st1[0:1, SUB:2 * SUB],
                                              1.0 / D, ALU.mult),
             waits=(), inc=sPv)
        p.op("vector",
             lambda e: e.tensor_mul(vrow[0:1, 0:SUB], mrow[0:1, 0:SUB], mrow[0:1, 0:SUB]),
             waits=(), inc=sPv)
        p.op("vector",
             lambda e: e.tensor_sub(vrow[0:1, 0:SUB], mrow[0:1, SUB:2 * SUB],
                                    vrow[0:1, 0:SUB]),
             waits=(), inc=sPv)
        p.op("vector",
             lambda e: e.tensor_single_scalar(vrow[0:1, 0:SUB], vrow[0:1, 0:SUB],
                                              EPS, ALU.add),
             waits=(), inc=sPv)
        varv = sPv.n
        p.op("scalar",
             lambda e: e.activation(vrow[0:1, SUB:2 * SUB], vrow[0:1, 0:SUB], AF.Ln),
             waits=[(sPv, varv)], inc=sPs)
        p.op("scalar",
             lambda e: e.activation(vrow[0:1, 0:SUB], vrow[0:1, SUB:2 * SUB], AF.Exp,
                                    scale=-0.5),
             waits=(), inc=None)
        p.op("scalar",
             lambda e: e.activation(vrow[0:1, SUB:2 * SUB], vrow[0:1, 0:SUB], AF.Identity),
             waits=(), inc=sPs)
        rstd_val = sPs.n
        p.op("tensor",
             lambda e: e.matmul(ps_bc[:, :], ones32r[:, :], mrow[0:1, 0:SUB],
                                start=True, stop=True),
             waits=[(sPs, rstd_val)], inc=sPm)
        p.op("vector",
             lambda e: e.tensor_copy(mb_sb[:, :], ps_bc[:, :]),
             waits=[(sPm, sPm.n)], inc=sPv)
        p.op("tensor",
             lambda e: e.matmul(ps_bc[:, :], ones32r[:, :], vrow[0:1, 0:SUB],
                                start=True, stop=True),
             waits=[(sPv, sPv.n)], inc=sPm)
        p.op("vector",
             lambda e: e.tensor_copy(rstdb[:, :], ps_bc[:, :]),
             waits=[(sPm, sPm.n)], inc=sPv)
        bc_done = sPv.n
        for c in range(OCH):
            tn = tnorm[c % 2]
            waits = [(sPv, bc_done)]
            p.op("vector",
                 lambda e, c=c, tn=tn: e.tensor_sub(tn[:, :], ysb[:, c, :], mb_sb[:, :]),
                 waits=waits, inc=None)
            p.op("vector",
                 lambda e, c=c, sub=sub, tn=tn: e.tensor_mul(
                     y1_16[:, c, sub * SUB:(sub + 1) * SUB], tn[:, :], rstdb[:, :]),
                 waits=(), inc=sPv)
        tnorm_war = [sPv.n, sPv.n]
    Y1_DONE = sPv.n
    pA.close()

    pBs = contextlib.ExitStack()
    bm16 = pBs.enter_context(nc.sbuf_tensor("bm16", [32, D], F16))
    pr16 = pBs.enter_context(nc.sbuf_tensor("pr16", [32, NSUB, SUB], F16))
    pB = contextlib.ExitStack()
    a16 = pB.enter_context(nc.sbuf_tensor("a16", [32, D], F16))
    aT16 = pB.enter_context(nc.sbuf_tensor("aT16", [128, ICH, 32], F16))
    wr16 = pB.enter_context(nc.sbuf_tensor("wr16", [4, D], F16))
    wrT16 = pB.enter_context(nc.sbuf_tensor("wrT16", [128, ICH, 4], F16))
    sel16 = pB.enter_context(nc.sbuf_tensor("sel16", [4, 32], F16))
    gl32 = pB.enter_context(nc.sbuf_tensor("gl32", [4, SUB], F32))
    ge32 = pB.enter_context(nc.sbuf_tensor("ge32", [4, SUB], F32))
    rb4 = pB.enter_context(nc.sbuf_tensor("rb4", [4, SUB], F32))
    grow2 = pB.enter_context(nc.sbuf_tensor("grow2", [1, SUB], F32))
    ones4 = pB.enter_context(nc.sbuf_tensor("ones4", [4, 1], F32))
    gem16 = pB.enter_context(nc.sbuf_tensor("gem16", [4, SUB], F16))
    grep_sb = pB.enter_context(nc.sbuf_tensor("grep_sb", [32, SUB], F32))
    ps_g = pB.enter_context(nc.psum_tensor("ps_g", [128, 512], F32))
    ps_g2 = pB.enter_context(nc.psum_tensor("ps_g2", [128, 512], F32))
    ps_tA = pB.enter_context(nc.psum_tensor("ps_tA", [128, 64], F16))

    p.op("vector", lambda e: e.memset(ones4[:, :], 1.0),
         waits=[(sPv, Y1_DONE)], inc=sPv)
    CONST_V = sPv.n
    p.op("gpsimd", lambda e: e.dma_start(out=a16[:, :], in_=A[:, :]),
         waits=[(sPv, Y1_DONE)], inc=sGa, inc_amt=16)
    p.op("gpsimd", lambda e: e.dma_start(out=wr16[:, :], in_=Wr[:, :]), inc=sGa, inc_amt=16)
    p.op("gpsimd", lambda e: e.dma_start(out=bm16[:, :], in_=BmT[:, :]), inc=sGa, inc_amt=16)
    p.op("sync", lambda e: e.dma_start(out=sel16[:, :], in_=sel[:, :]),
         waits=[(sPv, Y1_DONE)], inc=sCst, inc_amt=16)
    SMALL_G = sGa.n
    SMALL_C = sCst.n
    for i in range(ICH):
        p.op("tensor",
             lambda e, i=i: e.transpose(ps_tA[:, 0:32],
                                        a16[:, i * 128:(i + 1) * 128], ident16[0:32, 0:32]),
             waits=[(sGa, SMALL_G), (sCst, CONST_LD), (sPv, sPv.n)], inc=sPm)
        p.op("vector",
             lambda e, i=i: e.tensor_copy(aT16[:, i, :], ps_tA[:, 0:32]),
             waits=[(sPm, sPm.n)], inc=sPv)
        p.op("tensor",
             lambda e, i=i: e.transpose(ps_tA[:, 32:36],
                                        wr16[:, i * 128:(i + 1) * 128], ident16[0:4, 0:4]),
             waits=[(sPv, sPv.n)], inc=sPm)
        p.op("vector",
             lambda e, i=i: e.tensor_copy(wrT16[:, i, :], ps_tA[:, 32:36]),
             waits=[(sPm, sPm.n)], inc=sPv)
    TRS_V = sPv.n

    for sub in range(NSUB):
        for i in range(ICH):
            p.op("tensor",
                 lambda e, i=i, sub=sub: e.matmul(
                     ps_g[0:4, 0:SUB], wrT16[:, i, :],
                     y1_16[:, i, sub * SUB:(sub + 1) * SUB],
                     start=(i == 0), stop=(i == ICH - 1)),
                 waits=[(sPv, TRS_V)] if i == 0 else (),
                 inc=sPm if i == ICH - 1 else None)
        p.op("vector",
             lambda e: e.tensor_copy(gl32[:, :], ps_g[0:4, 0:SUB]),
             waits=[(sPm, sPm.n)], inc=sPv)
        glv = sPv.n
        p.op("scalar",
             lambda e: e.activation(ge32[:, :], gl32[:, :], AF.Exp),
             waits=[(sPv, glv)], inc=None)
        p.op("scalar",
             lambda e: e.activation(rb4[:, :], ge32[:, :], AF.Identity),
             waits=(), inc=sPs)
        exp_drain = sPs.n
        p.op("tensor",
             lambda e: e.matmul(ps_g2[0:1, 0:SUB], ones4[:, :], ge32[:, :],
                                start=True, stop=True),
             waits=[(sPs, exp_drain), (sPv, CONST_V)], inc=sPm)
        p.op("vector",
             lambda e: e.reciprocal(grow2[0:1, :], ps_g2[0:1, 0:SUB]),
             waits=[(sPm, sPm.n)], inc=sPv)
        p.op("tensor",
             lambda e: e.matmul(ps_g2[0:4, 0:SUB], ones32r[0:1, 0:4], grow2[0:1, :],
                                start=True, stop=True),
             waits=[(sPv, sPv.n)], inc=sPm)
        p.op("vector",
             lambda e: e.tensor_copy(rb4[:, :], ps_g2[0:4, 0:SUB]),
             waits=[(sPm, sPm.n)], inc=sPv)
        p.op("vector",
             lambda e: e.tensor_tensor(gem16[:, :], ge32[:, :], rb4[:, :], op=ALU.mult),
             waits=(), inc=sPv)
        gem_done = sPv.n
        p.op("tensor",
             lambda e: e.matmul(ps_g[0:32, 0:SUB], sel16[:, :], gem16[:, :],
                                start=True, stop=True),
             waits=[(sPv, gem_done), (sCst, SMALL_C)], inc=sPm)
        p.op("vector",
             lambda e: e.tensor_copy(grep_sb[:, :], ps_g[0:32, 0:SUB]),
             waits=[(sPm, sPm.n)], inc=sPv)
        for i in range(ICH):
            p.op("tensor",
                 lambda e, i=i, sub=sub: e.matmul(
                     ps_g2[0:32, 0:SUB], aT16[:, i, :],
                     y1_16[:, i, sub * SUB:(sub + 1) * SUB],
                     start=(i == 0), stop=(i == ICH - 1)),
                 waits=[(sPv, sPv.n)] if i == 0 else (),
                 inc=sPm if i == ICH - 1 else None)
        p.op("vector",
             lambda e, sub=sub: e.tensor_tensor(pr16[:, sub, :], ps_g2[0:32, 0:SUB],
                                                grep_sb[:, :], op=ALU.mult),
             waits=[(sPm, sPm.n)], inc=sPv)
    PR_DONE = sPv.n
    pB.close()

    pC = contextlib.ExitStack()
    y2_16 = pC.enter_context(nc.sbuf_tensor("y2_16", [128, OCH, TLOC], F16, side="right"))
    ps_st2 = pC.enter_context(nc.psum_tensor("ps_st2", [128, 2 * NSUB * SUB], F32, side="right"))
    pW = contextlib.ExitStack()
    OSUP = 256
    wtp = [pW.enter_context(nc.sbuf_tensor(f"wtp{i}", [128, ICH, OSUP], F16)) for i in range(2)]
    ftmp = [pW.enter_context(nc.sbuf_tensor(f"ftmp{i}", [128, SUB], F32)) for i in range(2)]
    fsq = pW.enter_context(nc.sbuf_tensor("fsq", [128, SUB], F16))
    bt_sb = pW.enter_context(nc.sbuf_tensor("bt_sb", [128, OCH], F32))
    ps_f = pW.enter_context(nc.psum_tensor("ps_f", [128, 2 * SUB], F32))

    p.op("sync", lambda e: e.dma_start(out=bt_sb[:, :], in_=bt[:, :]),
         waits=[(sPv, PR_DONE)], inc=sCst, inc_amt=16)
    BT_LD = sCst.n

    NOS = D // OSUP
    OCPS = OSUP // 128
    wt_war = {}
    f_evac_war = {}
    ftmp_war = [0, 0]
    for osup in range(NOS):
        wb = wtp[osup % 2]
        waits = [(sCastW, 16 * 8)]
        if osup == 0:
            waits.append((sPv, PR_DONE))
        if osup >= 2:
            waits.append((sPm, wt_war[osup - 2]))
        for i in range(ICH):
            p.op("sync",
                 lambda e, wb=wb, osup=osup, i=i: e.dma_start(
                     out=wb[:, i, :],
                     in_=w16d[osup * OSUP:(osup + 1) * OSUP, i * 128:(i + 1) * 128],
                     transpose=True),
                 waits=waits if i == 0 else (), inc=sWtl, inc_amt=16)
        wt_val = sWtl.n
        for sub in range(NSUB):
            for oc in range(OCPS):
                og = osup * OCPS + oc
                fi = (osup * NSUB + sub) * OCPS + oc
                bank = fi % 2
                waits = [(sWtl, wt_val), (sPv, PR_DONE)]
                if fi >= 2:
                    waits.append((sPe, f_evac_war[fi - 2]))
                for i in range(ICH):
                    p.op("tensor",
                         lambda e, wb=wb, i=i, oc=oc, sub=sub, bank=bank: e.matmul(
                             ps_f[:, bank * SUB:(bank + 1) * SUB],
                             wb[:, i, oc * 128:(oc + 1) * 128],
                             y1_16[:, i, sub * SUB:(sub + 1) * SUB],
                             start=(i == 0), stop=False),
                         waits=waits if i == 0 else (), inc=None)
                p.op("tensor",
                     lambda e, og=og, sub=sub, bank=bank: e.matmul(
                         ps_f[:, bank * SUB:(bank + 1) * SUB],
                         bm16[:, og * 128:(og + 1) * 128],
                         pr16[:, sub, :],
                         start=False, stop=True),
                     waits=(), inc=sPm)
                mm_val = sPm.n
                ft = ftmp[fi % 2]
                waits = [(sPm, mm_val), (sCst, BT_LD)]
                if ftmp_war[fi % 2] > 0:
                    waits.append((sPv, ftmp_war[fi % 2]))
                p.op("scalar",
                     lambda e, ft=ft, bank=bank, og=og: e.activation(
                         ft[:, :], ps_f[:, bank * SUB:(bank + 1) * SUB],
                         AF.Identity, bias=bt_sb[:, og:og + 1], scale=1.0),
                     waits=waits, inc=sPe)
                f_evac_war[fi] = sPe.n
                p.op("vector",
                     lambda e, og=og, sub=sub, ft=ft: e.tensor_add(
                         y2_16[:, og, sub * SUB:(sub + 1) * SUB],
                         ft[:, :],
                         y1_16[:, og, sub * SUB:(sub + 1) * SUB]),
                     waits=[(sPe, sPe.n)], inc=sPv)
                ftmp_war[fi % 2] = sPv.n
                p.op("vector",
                     lambda e, og=og, sub=sub: e.tensor_mul(
                         fsq[:, :],
                         y2_16[:, og, sub * SUB:(sub + 1) * SUB],
                         y2_16[:, og, sub * SUB:(sub + 1) * SUB]),
                     waits=(), inc=sPv)
                sq_val = sPv.n
                first = (osup == 0 and oc == 0)
                last = (osup == NOS - 1 and oc == OCPS - 1)
                p.op("tensor",
                     lambda e, og=og, sub=sub, first=first, last=last: e.matmul(
                         ps_st2[0:1, (2 * sub) * SUB:(2 * sub + 1) * SUB], ones16[:, :],
                         y2_16[:, og, sub * SUB:(sub + 1) * SUB],
                         start=first, stop=last),
                     waits=[(sPv, sq_val)], inc=None)
                p.op("tensor",
                     lambda e, sub=sub, first=first, last=last: e.matmul(
                         ps_st2[0:1, (2 * sub + 1) * SUB:(2 * sub + 2) * SUB],
                         ones16[:, :], fsq[:, :],
                         start=first, stop=last),
                     waits=(), inc=sPm if last else None)
        wt_war[osup] = sPm.n
    FFN_DONE = sPm.n
    pW.close()
    pBs.close()
    ph3.close()

    pD = contextlib.ExitStack()
    m2row = pD.enter_context(nc.sbuf_tensor("m2row", [1, 2 * SUB], F32))
    v2row = pD.enter_context(nc.sbuf_tensor("v2row", [1, 2 * SUB], F32))
    mtok = pD.enter_context(nc.sbuf_tensor("mtok", [128, 2], F32))
    y2tok = pD.enter_context(nc.sbuf_tensor("y2tok", [128, OCH * 128], F16))
    fin32 = [pD.enter_context(nc.sbuf_tensor(f"fin{i}", [128, D], F32)) for i in range(2)]
    ps_t = pD.enter_context(nc.psum_tensor("ps_t", [128, 1024], F32))
    ps_t16 = ps_t.ap().bitcast(F16)

    fin_dma_val = [0, 0]
    for sub in range(NSUB):
        p.op("vector",
             lambda e, sub=sub: e.tensor_single_scalar(
                 m2row[0:1, 0:SUB], ps_st2[0:1, (2 * sub) * SUB:(2 * sub + 1) * SUB],
                 1.0 / D, ALU.mult),
             waits=[(sPm, FFN_DONE)], inc=sPv)
        p.op("vector",
             lambda e, sub=sub: e.tensor_single_scalar(
                 m2row[0:1, SUB:2 * SUB],
                 ps_st2[0:1, (2 * sub + 1) * SUB:(2 * sub + 2) * SUB],
                 1.0 / D, ALU.mult),
             waits=(), inc=sPv)
        p.op("vector",
             lambda e: e.tensor_mul(v2row[0:1, 0:SUB], m2row[0:1, 0:SUB],
                                    m2row[0:1, 0:SUB]),
             waits=(), inc=sPv)
        p.op("vector",
             lambda e: e.tensor_sub(v2row[0:1, 0:SUB], m2row[0:1, SUB:2 * SUB],
                                    v2row[0:1, 0:SUB]),
             waits=(), inc=sPv)
        p.op("vector",
             lambda e: e.tensor_single_scalar(v2row[0:1, 0:SUB], v2row[0:1, 0:SUB],
                                              EPS, ALU.add),
             waits=(), inc=sPv)
        varv = sPv.n
        p.op("scalar",
             lambda e: e.activation(v2row[0:1, SUB:2 * SUB], v2row[0:1, 0:SUB], AF.Ln),
             waits=[(sPv, varv)], inc=sPs)
        p.op("scalar",
             lambda e: e.activation(v2row[0:1, 0:SUB], v2row[0:1, SUB:2 * SUB], AF.Exp,
                                    scale=-0.5),
             waits=(), inc=None)
        p.op("scalar",
             lambda e: e.activation(v2row[0:1, SUB:2 * SUB], v2row[0:1, 0:SUB], AF.Identity),
             waits=(), inc=sPs)
        row_val = sPs.n
        for j in range(SUB // 128):
            p.op("tensor",
                 lambda e, j=j: e.transpose(ps_t.ap()[:, 0:1],
                                            m2row[0:1, j * 128:(j + 1) * 128],
                                            ident32[0:1, 0:1]),
                 waits=[(sPs, row_val), (sPv, sPv.n)], inc=sPm)
            p.op("vector",
                 lambda e: e.tensor_copy(mtok[:, 0:1], ps_t.ap()[:, 0:1]),
                 waits=[(sPm, sPm.n)], inc=sPv)
            p.op("tensor",
                 lambda e, j=j: e.transpose(ps_t.ap()[:, 0:1],
                                            v2row[0:1, j * 128:(j + 1) * 128],
                                            ident32[0:1, 0:1]),
                 waits=[(sPv, sPv.n)], inc=sPm)
            p.op("vector",
                 lambda e: e.tensor_copy(mtok[:, 1:2], ps_t.ap()[:, 0:1]),
                 waits=[(sPm, sPm.n)], inc=sPv)
            tr_copy_val = [sPv.n, sPv.n]
            for oc in range(OCH):
                par = oc % 2
                reg = ps_t16[:, 0:128] if par == 0 else ps_t16[:, 1024:1152]
                p.op("tensor",
                     lambda e, oc=oc, sub=sub, j=j, reg=reg: e.transpose(
                         reg,
                         y2_16[:, oc, sub * SUB + j * 128: sub * SUB + (j + 1) * 128],
                         ident16[:, :]),
                     waits=[(sPv, tr_copy_val[par])], inc=sPm)
                p.op("vector",
                     lambda e, oc=oc, reg=reg: e.tensor_copy(
                         y2tok[:, oc * 128:(oc + 1) * 128], reg),
                     waits=[(sPm, sPm.n)], inc=sPv)
                tr_copy_val[par] = sPv.n
            fb = fin32[j % 2]
            waits = [(sPv, sPv.n)]
            if fin_dma_val[j % 2] > 0:
                waits.append((sOut, fin_dma_val[j % 2]))
            p.op("vector",
                 lambda e, fb=fb: e.tensor_scalar(
                     fb[:, :], y2tok[:, :],
                     mtok[:, 0:1], mtok[:, 1:2],
                     op0=ALU.subtract, op1=ALU.mult),
                 waits=waits, inc=sPv)
            p.op("gpsimd",
                 lambda e, fb=fb, sub=sub, j=j: e.dma_start(
                     out=y_out[sub * SUB + j * 128: sub * SUB + (j + 1) * 128, :],
                     in_=fb[:, :]),
                 waits=[(sPv, sPv.n)], inc=sOut, inc_amt=16)
            fin_dma_val[j % 2] = sOut.n

    p.op("gpsimd", lambda e: None, waits=[(sOut, sOut.n)])

    with nc.Block() as block:
        @block.sync
        def _(e):
            for f in p.q["sync"]:
                f(e)

        @block.tensor
        def _(e):
            for f in p.q["tensor"]:
                f(e)

        @block.vector
        def _(e):
            for f in p.q["vector"]:
                f(e)

        @block.scalar
        def _(e):
            for f in p.q["scalar"]:
                f(e)

        @block.gpsimd
        def _(e):
            for f in p.q["gpsimd"]:
                f(e)

    return nc


def kernel(L0_attn_x, L0_attn_attn_qkvw, L0_attn_attn_qkvb, L0_attn_attn_ow,
           L0_attn_attn_ob, L0_ffn_W, L0_ffn_b, L0_ffn_A, L0_ffn_Bm, L0_ffn_Wr):
    import ml_dtypes
    x = np.ascontiguousarray(np.asarray(L0_attn_x, dtype=np.float32).reshape(S, D))
    qkvw = np.asarray(L0_attn_attn_qkvw, dtype=np.float32)
    qkvb = np.asarray(L0_attn_attn_qkvb, dtype=np.float32)
    ow = np.ascontiguousarray(np.asarray(L0_attn_attn_ow, dtype=np.float32))
    ob = np.asarray(L0_attn_attn_ob, dtype=np.float32)
    W = np.ascontiguousarray(np.asarray(L0_ffn_W, dtype=np.float32))
    b = np.asarray(L0_ffn_b, dtype=np.float32)
    A = np.ascontiguousarray(np.asarray(L0_ffn_A, dtype=np.float32).reshape(32, D))
    Bm = np.asarray(L0_ffn_Bm, dtype=np.float32)
    Wr = np.ascontiguousarray(np.asarray(L0_ffn_Wr, dtype=np.float32))

    BmT = np.ascontiguousarray(Bm.transpose(0, 2, 1).reshape(32, D))
    sel = np.zeros((4, 32), dtype=np.float16)
    for e_ in range(4):
        sel[e_, e_ * 8:(e_ + 1) * 8] = 2.0
    id16 = np.eye(128, dtype=np.float16)
    idbf = np.eye(128).astype(ml_dtypes.bfloat16).view(np.uint16)
    id32 = np.eye(128, dtype=np.float32)
    obt = np.ascontiguousarray(ob.reshape(OCH, 128).T)
    btt = np.ascontiguousarray(b.reshape(OCH, 128).T)

    nc = build_nc()

    in_maps = []
    for r in range(NCORES):
        heads = range(NHL * r, NHL * (r + 1))
        cols = []
        bcols = []
        for h in heads:
            for k in range(3):
                cols.append(qkvw[:, k * D + h * 128:k * D + (h + 1) * 128])
                bcols.append(qkvb[k * D + h * 128:k * D + (h + 1) * 128])
        qkvw_r = np.ascontiguousarray(np.concatenate(cols, axis=1))
        qkvb_r = np.ascontiguousarray(np.stack(bcols, axis=1))
        gi = np.zeros((128, OCH), dtype=np.int32)
        for c in range(OCH):
            lh = c % NHL
            rp = c // NHL
            gi[:, c] = (lh * 1024 + rp * 128 + np.arange(128)) * S + r * TLOC
        in_maps.append({
            "x32": x,
            "x32m": np.ascontiguousarray(x[r * TLOC:(r + 1) * TLOC, :]),
            "qkvw": qkvw_r,
            "qkvb": qkvb_r,
            "ow": ow,
            "obt": obt,
            "W": W,
            "bt": btt,
            "A": A,
            "BmT": BmT,
            "Wr": Wr,
            "sel": sel,
            "id16": id16,
            "idbf": idbf,
            "id32": id32,
            "gidx": gi,
        })

    res = run_bass_kernel_spmd(nc, in_maps, core_ids=list(range(NCORES)))
    out = np.concatenate([res.results[r]["y_out"] for r in range(NCORES)], axis=0)
    return out.reshape(1, S, D).astype(np.float32)


# version 15
# speedup vs baseline: 1.1459x; 1.0206x over previous
import contextlib
import math
import numpy as np

import concourse.bass as bass
import concourse.mybir as mybir
from concourse.bass import IndirectOffsetOnAxis
from concourse.bass_utils import run_bass_kernel_spmd

F32 = mybir.dt.float32
F16 = mybir.dt.float16
BF16 = mybir.dt.bfloat16
I32 = mybir.dt.int32
AF = mybir.ActivationFunctionType
ALU = mybir.AluOpType
AXX = mybir.AxisListType.X

NCORES = 8
S = 8192
D = 4096
HD = 128
NHL = 4
TLOC = S // NCORES
SCALE = 1.0 / math.sqrt(HD)
EPS = 1e-5

CCH = 3 * NHL
ICH = D // 128
TT = 512
NTT = S // TT
TQ = 512
NQT = S // TQ
KCH = S // 128
GS = 3
SUB = 512
NSUB = TLOC // SUB
OCH = D // 128


class Sem:
    def __init__(self, handle, name):
        self.h = handle
        self.name = name
        self.n = 0


class Prog:
    QUEUES = ("sync", "tensor", "vector", "scalar", "gpsimd")

    def __init__(self):
        self.q = {k: [] for k in self.QUEUES}

    def op(self, qn, fn, waits=(), inc=None, inc_amt=1, defer=None):
        for s, v in waits:
            assert v <= s.n, f"wait on {s.name} for {v} but only {s.n} recorded"
        wl = [(s.h, v) for s, v in waits if v > 0]
        if inc is not None:
            inc.n += inc_amt
            ih, ia = inc.h, inc_amt
        else:
            ih, ia = None, 0

        def run(eng, wl=wl, fn=fn, ih=ih, ia=ia):
            for h, v in wl:
                eng.wait_ge(h, v)
            ins = fn(eng)
            if ih is not None:
                if ins is None:
                    ins = eng.nop()
                ins.then_inc(ih, ia)
        if defer is not None:
            defer.append((qn, run))
        else:
            self.q[qn].append(run)

    def flush(self, defer):
        for qn, run in defer:
            self.q[qn].append(run)
        defer.clear()


def build_nc():
    nc = bass.Bass("TRN2", target_bir_lowering=False)

    x32 = nc.dram_tensor("x32", [S, D], F32, kind="ExternalInput").ap()
    x32m = nc.dram_tensor("x32m", [TLOC, D], F32, kind="ExternalInput").ap()
    qkvw = nc.dram_tensor("qkvw", [D, CCH * 128], F32, kind="ExternalInput").ap()
    qkvb = nc.dram_tensor("qkvb", [128, CCH], F32, kind="ExternalInput").ap()
    ow = nc.dram_tensor("ow", [D, D], F32, kind="ExternalInput").ap()
    obt = nc.dram_tensor("obt", [128, OCH], F32, kind="ExternalInput").ap()
    Wt = nc.dram_tensor("W", [D, D], F32, kind="ExternalInput").ap()
    bt = nc.dram_tensor("bt", [128, OCH], F32, kind="ExternalInput").ap()
    A = nc.dram_tensor("A", [32, D], F32, kind="ExternalInput").ap()
    BmT = nc.dram_tensor("BmT", [32, D], F32, kind="ExternalInput").ap()
    Wr = nc.dram_tensor("Wr", [4, D], F32, kind="ExternalInput").ap()
    sel = nc.dram_tensor("sel", [4, 32], F16, kind="ExternalInput").ap()
    id16 = nc.dram_tensor("id16", [128, 128], F16, kind="ExternalInput").ap()
    idbf = nc.dram_tensor("idbf", [128, 128], BF16, kind="ExternalInput").ap()
    id32 = nc.dram_tensor("id32", [128, 128], F32, kind="ExternalInput").ap()
    gidx = nc.dram_tensor("gidx", [128, OCH], I32, kind="ExternalInput").ap()

    y_out = nc.dram_tensor("y_out", [TLOC, D], F32, kind="ExternalOutput").ap()
    x16d = nc.dram_tensor("x16d", [S, D], F16).ap()
    x16md = nc.dram_tensor("x16md", [TLOC, D], F16).ap()
    w16d = nc.dram_tensor("w16d", [D, D], F16).ap()
    qTd = nc.dram_tensor("qTd", [NHL * 128, S], F16).ap()
    kTd = nc.dram_tensor("kTd", [NHL * 128, S], F16).ap()
    vd = nc.dram_tensor("vd", [S, NHL * 128], BF16).ap()
    ag_in = nc.dram_tensor("ag_in", [NHL * 128, S], F16).ap()
    ag_out = nc.dram_tensor("ag_out", [NHL * 1024, S], F16, addr_space="Shared").ap()
    ctx_mine_d = nc.dram_tensor("ctx_mine_d", [D, TLOC], F16).ap()
    ag_out_flat = ag_out.rearrange("a b -> (a b)").rearrange("(n one) -> n one", one=1)

    p = Prog()
    stack = contextlib.ExitStack()
    nc._kernel_stack = stack

    def sem(name):
        return Sem(stack.enter_context(nc.semaphore(name)), name)

    sCast = sem("sCast")
    sCastM = sem("sCastM")
    sCastW = sem("sCastW")
    sWld = sem("sWld")
    sXT = sem("sXT")
    sMM = sem("sMM")
    sEv = sem("sEv")
    sVt = sem("sVt")
    sVe = sem("sVe")
    sSp = sem("sSp")
    sNq = sem("sNq")
    sNm = sem("sNm")
    sBh = sem("sBh")
    sLd = sem("sLd")
    sST = sem("sST")
    sEX = sem("sEX")
    sPV = sem("sPV")
    sTR = sem("sTR")
    sTRb = sem("sTRb")
    sEp = sem("sEp")
    sAG = sem("sAG")
    sCC = sem("sCC")
    sGa = sem("sGa")
    sOwl = sem("sOwl")
    sXtp = sem("sXtp")
    sPm = sem("sPm")
    sPe = sem("sPe")
    sPv = sem("sPv")
    sWtl = sem("sWtl")
    sPs = sem("sPs")
    sOut = sem("sOut")
    sIdx = sem("sIdx")
    sCst = sem("sCst")

    ident16 = stack.enter_context(nc.sbuf_tensor("ident16", [128, 128], F16))
    identbf = stack.enter_context(nc.sbuf_tensor("identbf", [128, 128], BF16))
    ident32 = stack.enter_context(nc.sbuf_tensor("ident32", [128, 128], F32))
    ones16 = stack.enter_context(nc.sbuf_tensor("ones16", [128, 1], F16))
    onesbf = stack.enter_context(nc.sbuf_tensor("onesbf", [128, 1], BF16))
    ones32r = stack.enter_context(nc.sbuf_tensor("ones32r", [1, 128], F32))
    qkvb_sb = stack.enter_context(nc.sbuf_tensor("qkvb_sb", [128, CCH], F32))
    bias_h = stack.enter_context(nc.sbuf_tensor("bias_h", [128, NHL], F32))
    norm_acc = stack.enter_context(nc.sbuf_tensor("norm_acc", [1, 2 * NHL], F32))
    rowtmp = stack.enter_context(nc.sbuf_tensor("rowtmp", [1, 8], F32))

    p.op("sync", lambda e: e.dma_start(out=ident16[:, :], in_=id16[:, :]), inc=sCst, inc_amt=16)
    p.op("sync", lambda e: e.dma_start(out=identbf[:, :], in_=idbf[:, :]), inc=sCst, inc_amt=16)
    p.op("sync", lambda e: e.dma_start(out=ident32[:, :], in_=id32[:, :]), inc=sCst, inc_amt=16)
    p.op("sync", lambda e: e.dma_start(out=qkvb_sb[:, :], in_=qkvb[:, :]), inc=sCst, inc_amt=16)
    p.op("vector", lambda e: e.memset(ones16[:, :], 1.0), inc=sPv)
    p.op("vector", lambda e: e.memset(onesbf[:, :], 1.0), inc=sPv)
    p.op("vector", lambda e: e.memset(ones32r[:, :], 1.0), inc=sPv)
    p.op("vector", lambda e: e.memset(norm_acc[:, :], 0.0), inc=sPv)
    CONST_LD = sCst.n
    CONST_V = sPv.n

    ph1 = contextlib.ExitStack()
    qkvw_sb = ph1.enter_context(nc.sbuf_tensor("qkvw_sb", [128, CCH * ICH, 128], F16))
    xt_sb = [ph1.enter_context(nc.sbuf_tensor(f"xt{i}", [128, ICH, TT], F16)) for i in range(2)]
    stg = [ph1.enter_context(nc.sbuf_tensor(f"stg{i}", [128, TT], F16)) for i in range(3)]
    stgv = [ph1.enter_context(nc.sbuf_tensor(f"stgv{i}", [128, TT], BF16)) for i in range(2)]
    vstg = [ph1.enter_context(nc.sbuf_tensor(f"vstg{i}", [128, TT], BF16)) for i in range(2)]
    sq_sb = [ph1.enter_context(nc.sbuf_tensor(f"sq_sb{i}", [128, TT], F16)) for i in range(2)]
    ps_qkv = ph1.enter_context(nc.psum_tensor("ps_qkv", [128, 3 * TT], F32))
    ps_vt = ph1.enter_context(nc.psum_tensor("ps_vt", [128, TT], BF16))
    ps_qn = ph1.enter_context(nc.psum_tensor("ps_qn", [128, 512], F32))
    ps_bb = ph1.enter_context(nc.psum_tensor("ps_bb", [128, 512], F32))

    for c in range(CCH):
        p.op("gpsimd",
             lambda e, c=c: e.dma_start(
                 out=qkvw_sb[:, c * ICH:(c + 1) * ICH, :],
                 in_=qkvw[:, c * 128:(c + 1) * 128].rearrange("(i p) c -> p i c", p=128)),
             inc=sWld, inc_amt=16)
    QKVW_LD = sWld.n

    panel_mm_done = {}
    qk_spill_val = [0, 0, 0]
    v_spill_val = [0, 0]
    vstg_evac_val = [0, 0]
    sq_mm_val = [0, 0]
    qk_ctr = 0
    v_ctr = 0
    norm_defer = []
    for t in range(NTT):
        cwaits = []
        if t >= 2:
            cwaits.append((sXT, 16 * 32 * (t - 1)))
        p.op("gpsimd", lambda e, t=t: e.dma_start(out=x16d[t * TT:(t + 1) * TT, :],
                                                  in_=x32[t * TT:(t + 1) * TT, :]),
             waits=cwaits, inc=sCast, inc_amt=16)
        waits = [(sCast, 16 * (t + 1))]
        if t >= 2:
            waits.append((sMM, panel_mm_done[t - 2]))
        for i in range(ICH):
            p.op("sync",
                 lambda e, t=t, i=i: e.dma_start(
                     out=xt_sb[t % 2][:, i, :],
                     in_=x16d[t * TT:(t + 1) * TT, i * 128:(i + 1) * 128],
                     transpose=True),
                 waits=waits if i == 0 else (), inc=sXT, inc_amt=16)

        for c in range(CCH):
            tile_idx = t * CCH + c
            bank = tile_idx % 3
            waits = [(sXT, 16 * ICH * (t + 1))]
            if tile_idx == 0:
                waits.append((sWld, QKVW_LD))
                waits.append((sCst, CONST_LD))
                waits.append((sPv, CONST_V))
            if tile_idx >= 3:
                waits.append((sEv, tile_idx - 3 + 1))
            for i in range(ICH):
                p.op("tensor",
                     lambda e, c=c, t=t, i=i, bank=bank: e.matmul(
                         ps_qkv[:, bank * TT:(bank + 1) * TT],
                         qkvw_sb[:, c * ICH + i, :],
                         xt_sb[t % 2][:, i, :],
                         start=(i == 0), stop=(i == ICH - 1)),
                     waits=waits if i == 0 else (),
                     inc=sMM if i == ICH - 1 else None)
            mm_val = sMM.n
            p.flush(norm_defer)

            lh, kind = c // 3, c % 3
            if kind in (0, 1):
                bidx = qk_ctr % 3
                sqb = sq_sb[qk_ctr % 2]
                sq_war = sq_mm_val[qk_ctr % 2]
                qk_ctr += 1
                sb = stg[bidx]
                waits = [(sMM, mm_val), (sCst, CONST_LD)]
                if qk_spill_val[bidx] > 0:
                    waits.append((sSp, qk_spill_val[bidx]))
                p.op("scalar",
                     lambda e, sb=sb, bank=bank, c=c: e.activation(
                         sb[:, :], ps_qkv[:, bank * TT:(bank + 1) * TT],
                         AF.Identity, bias=qkvb_sb[:, c:c + 1], scale=1.0),
                     waits=waits, inc=sEv)
                ev_val = sEv.n
                dst = qTd if kind == 0 else kTd
                p.op("sync",
                     lambda e, sb=sb, dst=dst, lh=lh, t=t: e.dma_start(
                         out=dst[lh * 128:(lh + 1) * 128, t * TT:(t + 1) * TT],
                         in_=sb[:, :]),
                     waits=[(sEv, ev_val)], inc=sSp, inc_amt=16)
                qk_spill_val[bidx] = sSp.n
                waits = [(sEv, ev_val)]
                if sq_war > 0:
                    waits.append((sNm, sq_war))
                p.op("vector",
                     lambda e, sb=sb, sqb=sqb: e.tensor_mul(sqb[:, :], sb[:, :], sb[:, :]),
                     waits=waits, inc=sNq)
                p.op("tensor",
                     lambda e, sqb=sqb: e.matmul(ps_qn[0:1, 0:TT], ones16[:, :], sqb[:, :],
                                                 start=True, stop=True),
                     waits=[(sNq, sNq.n), (sNq, sNq.n)], inc=sNm, defer=norm_defer)
                sq_mm_val[(qk_ctr - 1) % 2] = sNm.n
                acc_col = lh * 2 + kind
                p.op("vector",
                     lambda e: e.reduce_max(rowtmp[0:1, 0:1], ps_qn[0:1, 0:TT], axis=AXX),
                     waits=[(sNm, sNm.n)], inc=sNq)
                p.op("vector",
                     lambda e, acc_col=acc_col: e.tensor_max(
                         norm_acc[0:1, acc_col:acc_col + 1],
                         norm_acc[0:1, acc_col:acc_col + 1],
                         rowtmp[0:1, 0:1]),
                     waits=(), inc=sNq)
            else:
                sbv = stgv[v_ctr % 2]
                vb = vstg[v_ctr % 2]
                waits = [(sMM, mm_val), (sCst, CONST_LD)]
                if vstg_evac_val[v_ctr % 2] > 0:
                    waits.append((sVt, vstg_evac_val[v_ctr % 2]))
                p.op("scalar",
                     lambda e, sbv=sbv, bank=bank, c=c: e.activation(
                         sbv[:, :], ps_qkv[:, bank * TT:(bank + 1) * TT],
                         AF.Identity, bias=qkvb_sb[:, c:c + 1], scale=1.0),
                     waits=waits, inc=sEv)
                ev_val = sEv.n
                for j in range(4):
                    p.op("tensor",
                         lambda e, sbv=sbv, j=j: e.transpose(
                             ps_vt[:, j * 128:(j + 1) * 128],
                             sbv[:, j * 128:(j + 1) * 128], identbf[:, :]),
                         waits=([(sEv, ev_val), (sVe, sVe.n)] if j == 0 else ()),
                         inc=sVt if j == 3 else None)
                vstg_evac_val[v_ctr % 2] = sVt.n
                vt_val = sVt.n
                waits = [(sVt, vt_val)]
                if v_spill_val[v_ctr % 2] > 0:
                    waits.append((sSp, v_spill_val[v_ctr % 2]))
                p.op("vector",
                     lambda e, vb=vb: e.tensor_copy(vb[:, :], ps_vt[:, :]),
                     waits=waits, inc=sVe)
                ve_val = sVe.n
                p.op("sync",
                     lambda e, vb=vb, lh=lh, t=t: e.dma_start(
                         out=vd[t * TT:(t + 1) * TT, lh * 128:(lh + 1) * 128]
                         .rearrange("(j p) d -> p j d", p=128),
                         in_=vb[:, :].rearrange("p (j d) -> p j d", j=4)),
                     waits=[(sVe, ve_val)], inc=sSp, inc_amt=16)
                v_spill_val[v_ctr % 2] = sSp.n
                v_ctr += 1
        panel_mm_done[t] = sMM.n
    p.flush(norm_defer)

    nq0 = sNq.n
    for lh in range(NHL):
        p.op("vector",
             lambda e, lh=lh: e.tensor_mul(rowtmp[0:1, 1:2],
                                           norm_acc[0:1, 2 * lh:2 * lh + 1],
                                           norm_acc[0:1, 2 * lh + 1:2 * lh + 2]),
             waits=[(sNq, nq0)], inc=sNq)
        p.op("scalar",
             lambda e: e.activation(rowtmp[0:1, 2:3], rowtmp[0:1, 1:2], AF.Ln),
             waits=[(sNq, sNq.n)], inc=sBh)
        p.op("scalar",
             lambda e: e.activation(rowtmp[0:1, 3:4], rowtmp[0:1, 2:3], AF.Exp, scale=0.5),
             waits=(), inc=None)
        p.op("scalar",
             lambda e: e.activation(rowtmp[0:1, 5:6], rowtmp[0:1, 3:4], AF.Identity),
             waits=(), inc=sBh)
        p.op("vector",
             lambda e: e.tensor_single_scalar(rowtmp[0:1, 4:5], rowtmp[0:1, 3:4],
                                              -SCALE, ALU.mult),
             waits=[(sBh, sBh.n)], inc=sNq)
        p.op("tensor",
             lambda e: e.matmul(ps_bb[:, 0:1], ones32r[:, :], rowtmp[0:1, 4:5],
                                start=True, stop=True),
             waits=[(sNq, sNq.n), (sBh, sBh.n)], inc=sNm)
        p.op("vector",
             lambda e, lh=lh: e.tensor_copy(bias_h[:, lh:lh + 1], ps_bb[:, 0:1]),
             waits=[(sNm, sNm.n)], inc=sBh)
    BH_DONE = sBh.n
    P1_SPILLS = sSp.n
    p.op("gpsimd", lambda e: e.dma_start(out=x16md[:, :], in_=x32m[:, :]),
         waits=[(sCast, 16 * NTT)], inc=sCastM, inc_amt=16)
    for c in range(8):
        p.op("gpsimd", lambda e, c=c: e.dma_start(out=w16d[c * 512:(c + 1) * 512, :],
                                                  in_=Wt[c * 512:(c + 1) * 512, :]),
             inc=sCastW, inc_amt=16)

    ph1.close()
    ph2 = contextlib.ExitStack()
    kT_sb = [ph2.enter_context(nc.sbuf_tensor(f"kT{i}", [128, S], F16)) for i in range(2)]
    qT_sb = [ph2.enter_context(nc.sbuf_tensor(f"qT{i}", [128, S], F16)) for i in range(2)]
    v_sb = [ph2.enter_context(nc.sbuf_tensor(f"v{i}", [128, KCH, 128], BF16)) for i in range(2)]
    pt_sb = [ph2.enter_context(nc.sbuf_tensor(f"pt{i}", [128, 2 * TQ], BF16)) for i in range(3)]
    tree_bufs = [ph2.enter_context(nc.sbuf_tensor(f"tr{i}", [128, TQ], BF16)) for i in range(16)]
    rrow = ph2.enter_context(nc.sbuf_tensor("rrow", [1, 2 * TQ], F32))
    rb_sb = ph2.enter_context(nc.sbuf_tensor("rb_sb", [128, 2 * TQ], F32))
    ctxn = [ph2.enter_context(nc.sbuf_tensor(f"ctxn{i}", [128, TQ], F16)) for i in range(2)]
    ps_st = [ph2.enter_context(nc.psum_tensor(f"ps_stt{i}", [128, 2 * TQ], F32))
             for i in range(3)]
    ps_ctx2 = ph2.enter_context(nc.psum_tensor("ps_ctx2", [128, 2 * TQ], F32))

    st_free = [0, 0, 0]
    pt_free_pv = [0, 0, 0]
    pt_free_trA = [0, 0, 0]
    pt_free_trB = [0, 0, 0]
    ctxn_dma_val = [0, 0]
    head_pv_done = {}

    def ap_of(tok):
        if tok[0] == "pt":
            return pt_sb[tok[1]][:, tok[2] * TQ:(tok[2] + 1) * TQ]
        return tok[1][:, :]

    class Tree:

        def __init__(self, pool):
            self.free = list(pool)
            self.occ = {}

        def insert_pair_ops(self, tokx, toky):
            ops = []
            outb = self.free.pop()
            ops.append(lambda e, a=tokx, bb=toky, outb=outb: e.tensor_add(
                outb[:, :], ap_of(a), ap_of(bb)))
            cur = ("buf", outb)
            lvl = 0
            while lvl in self.occ:
                prev = self.occ.pop(lvl)
                outb = prev[1]
                ops.append(lambda e, a=cur, bb=prev, outb=outb: e.tensor_add(
                    outb[:, :], ap_of(a), ap_of(bb)))
                if cur[1] is not outb:
                    self.free.append(cur[1])
                cur = ("buf", outb)
                lvl += 1
            self.occ[lvl] = cur
            return ops

        def root(self):
            assert len(self.occ) == 1
            return self.occ.popitem()[1][1]

    for lh in range(NHL):
        b = lh % 2
        waits = [(sSp, P1_SPILLS), (sBh, BH_DONE)]
        if lh >= 2:
            waits.append((sPV, head_pv_done[lh - 2]))
        p.op("sync", lambda e, b=b, lh=lh: e.dma_start(
            out=kT_sb[b][:, :], in_=kTd[lh * 128:(lh + 1) * 128, :]),
            waits=waits, inc=sLd, inc_amt=16)
        p.op("sync", lambda e, b=b, lh=lh: e.dma_start(
            out=qT_sb[b][:, :], in_=qTd[lh * 128:(lh + 1) * 128, :]),
            inc=sLd, inc_amt=16)
        p.op("sync", lambda e, b=b, lh=lh: e.dma_start(
            out=v_sb[b][:, :, :],
            in_=vd[:, lh * 128:(lh + 1) * 128].rearrange("(j p) d -> p j d", p=128)),
            inc=sLd, inc_amt=16)
        ld_val = sLd.n

        for qp in range(NQT // 2):
            qA, qB = 2 * qp, 2 * qp + 1
            ctx_free_val = sEp.n
            treeA = Tree(tree_bufs[0:8])
            treeB = Tree(tree_bufs[8:16])
            exp_vals = {}

            def emit_pv(kc, ctx_free_val=None):
                par = kc % 3
                ptb = pt_sb[par]
                w2 = [(sEX, exp_vals[kc])]
                if kc == 0:
                    w2.append((sEp, ctx_free_val))
                p.op("tensor",
                     lambda e, b=b, kc=kc, ptb=ptb: e.matmul(
                         ps_ctx2[:, 0:TQ], v_sb[b][:, kc, :], ptb[:, 0:TQ],
                         start=(kc == 0), stop=(kc == KCH - 1)),
                     waits=w2, inc=None)
                p.op("tensor",
                     lambda e, b=b, kc=kc, ptb=ptb: e.matmul(
                         ps_ctx2[:, TQ:2 * TQ], v_sb[b][:, kc, :], ptb[:, TQ:2 * TQ],
                         start=(kc == 0), stop=(kc == KCH - 1)),
                     waits=(), inc=sPV)
                pt_free_pv[par] = sPV.n

            for kc in range(KCH):
                par = kc % 3
                stps = ps_st[par]
                ptb = pt_sb[par]
                waits = []
                if kc == 0 and qp == 0:
                    waits.append((sLd, ld_val))
                    if lh == 0:
                        waits.append((sBh, BH_DONE))
                if kc == 0:
                    waits.append((sEp, ctx_free_val))
                if st_free[par] > 0:
                    waits.append((sEX, st_free[par]))
                p.op("tensor",
                     lambda e, b=b, kc=kc, qA=qA, stps=stps: e.matmul(
                         stps[:, 0:TQ],
                         kT_sb[b][:, kc * 128:(kc + 1) * 128],
                         qT_sb[b][:, qA * TQ:(qA + 1) * TQ],
                         start=True, stop=True),
                     waits=waits, inc=None)
                p.op("tensor",
                     lambda e, b=b, kc=kc, qB=qB, stps=stps: e.matmul(
                         stps[:, TQ:2 * TQ],
                         kT_sb[b][:, kc * 128:(kc + 1) * 128],
                         qT_sb[b][:, qB * TQ:(qB + 1) * TQ],
                         start=True, stop=True),
                     waits=(), inc=sST)
                st_val = sST.n
                if kc >= 1:
                    emit_pv(kc - 1, ctx_free_val)

                waits = [(sST, st_val)]
                if pt_free_pv[par] > 0:
                    waits.append((sPV, pt_free_pv[par]))
                    waits.append((sTR, pt_free_trA[par]))
                    waits.append((sTRb, pt_free_trB[par]))
                p.op("scalar",
                     lambda e, ptb=ptb, stps=stps, lh=lh: e.activation(
                         ptb[:, :], stps[:, :], AF.Exp,
                         bias=bias_h[:, lh:lh + 1], scale=SCALE),
                     waits=waits, inc=sEX)
                exp_vals[kc] = sEX.n
                st_free[par] = sEX.n

                if kc % 2 == 1:
                    par0 = (kc - 1) % 3
                    opsA = treeA.insert_pair_ops(("pt", par0, 0), ("pt", par, 0))
                    for oi, fn_ in enumerate(opsA):
                        p.op("vector", fn_,
                             waits=[(sEX, exp_vals[kc])] if oi == 0 else (),
                             inc=sTR if oi == len(opsA) - 1 else None)
                    pt_free_trA[par0] = pt_free_trA[par] = sTR.n
                    opsB = treeB.insert_pair_ops(("pt", par0, 1), ("pt", par, 1))
                    for oi, fn_ in enumerate(opsB):
                        p.op("vector", fn_,
                             waits=(),
                             inc=sTRb if oi == len(opsB) - 1 else None)
                    pt_free_trB[par0] = pt_free_trB[par] = sTRb.n
            emit_pv(KCH - 1)
            pv_all = sPV.n

            rootA, rootB = treeA.root(), treeB.root()
            trA_val, trB_val = sTR.n, sTRb.n
            p.op("tensor",
                 lambda e, rootA=rootA: e.matmul(ps_st[0][0:1, 0:TQ], onesbf[:, :],
                                                 rootA[:, :], start=True, stop=True),
                 waits=[(sTR, trA_val), (sEX, exp_vals[KCH - 1]), (sPv, CONST_V)],
                 inc=None)
            p.op("tensor",
                 lambda e, rootB=rootB: e.matmul(ps_st[1][0:1, 0:TQ], onesbf[:, :],
                                                 rootB[:, :], start=True, stop=True),
                 waits=[(sTRb, trB_val)], inc=sNm)
            mm_val = sNm.n
            p.op("vector",
                 lambda e: e.reciprocal(rrow[0:1, 0:TQ], ps_st[0][0:1, 0:TQ]),
                 waits=[(sNm, mm_val)], inc=None)
            p.op("vector",
                 lambda e: e.reciprocal(rrow[0:1, TQ:2 * TQ], ps_st[1][0:1, 0:TQ]),
                 waits=(), inc=sEp)
            rec_val = sEp.n
            p.op("tensor",
                 lambda e: e.matmul(ps_st[0][:, 0:TQ], ones32r[:, :], rrow[0:1, 0:TQ],
                                    start=True, stop=True),
                 waits=[(sEp, rec_val)], inc=None)
            p.op("tensor",
                 lambda e: e.matmul(ps_st[1][:, 0:TQ], ones32r[:, :], rrow[0:1, TQ:2 * TQ],
                                    start=True, stop=True),
                 waits=(), inc=sNm)
            bc_val = sNm.n
            p.op("vector",
                 lambda e: e.tensor_copy(rb_sb[:, 0:TQ], ps_st[0][:, 0:TQ]),
                 waits=[(sNm, bc_val)], inc=None)
            p.op("vector",
                 lambda e: e.tensor_copy(rb_sb[:, TQ:2 * TQ], ps_st[1][:, 0:TQ]),
                 waits=(), inc=sEp)
            rb_val = sEp.n
            waits = [(sPV, pv_all), (sEp, rb_val)]
            if ctxn_dma_val[0] > 0:
                waits.append((sAG, max(ctxn_dma_val[0], ctxn_dma_val[1])))
            p.op("vector",
                 lambda e: e.tensor_tensor(ctxn[0][:, :], ps_ctx2[:, 0:TQ],
                                           rb_sb[:, 0:TQ], op=ALU.mult),
                 waits=waits, inc=sEp)
            evA = sEp.n
            p.op("vector",
                 lambda e: e.tensor_tensor(ctxn[1][:, :], ps_ctx2[:, TQ:2 * TQ],
                                           rb_sb[:, TQ:2 * TQ], op=ALU.mult),
                 waits=(), inc=sEp)
            evB = sEp.n
            p.op("sync",
                 lambda e, lh=lh, qA=qA: e.dma_start(
                     out=ag_in[lh * 128:(lh + 1) * 128, qA * TQ:(qA + 1) * TQ],
                     in_=ctxn[0][:, :]),
                 waits=[(sEp, evA)], inc=sAG, inc_amt=16)
            ctxn_dma_val[0] = sAG.n
            p.op("sync",
                 lambda e, lh=lh, qB=qB: e.dma_start(
                     out=ag_in[lh * 128:(lh + 1) * 128, qB * TQ:(qB + 1) * TQ],
                     in_=ctxn[1][:, :]),
                 waits=[(sEp, evB)], inc=sAG, inc_amt=16)
            ctxn_dma_val[1] = sAG.n
        head_pv_done[lh] = sPV.n
        ag_head_val = sAG.n
        p.op("gpsimd",
             lambda e, lh=lh: e.collective_compute(
                 "AllGather", ALU.bypass, replica_groups=[list(range(NCORES))],
                 ins=[ag_in[lh * 128:(lh + 1) * 128, :].opt()],
                 outs=[ag_out[lh * 1024:(lh + 1) * 1024, :].opt()]),
             waits=[(sAG, ag_head_val)], inc=sCC, inc_amt=1)

    AG_ALL = sAG.n

    ph2.close()
    ph3 = contextlib.ExitStack()
    y1_16 = ph3.enter_context(nc.sbuf_tensor("y1_16", [128, OCH, TLOC], F16))
    idx_sb = ph3.enter_context(nc.sbuf_tensor("idx_sb", [128, OCH], I32))
    gstg = [ph3.enter_context(nc.sbuf_tensor(f"gstg{i}", [128, TLOC], F16)) for i in range(2)]
    p.op("sync", lambda e: e.dma_start(out=idx_sb[:, :], in_=gidx[:, :]),
         waits=[(sAG, AG_ALL)], inc=sIdx, inc_amt=16)
    gstg_out_val = [0, 0]
    for c in range(OCH):
        waits = [(sIdx, 16), (sCC, NHL)] if c == 0 else []
        if gstg_out_val[c % 2] > 0:
            waits.append((sGa, gstg_out_val[c % 2]))
        p.op("gpsimd",
             lambda e, c=c: e.indirect_dma_start(
                 gstg[c % 2][:, :], None, ag_out_flat[0:128, :],
                 IndirectOffsetOnAxis(idx_sb[:, c:c + 1], axis=0)),
             waits=waits, inc=sGa, inc_amt=16)
        gv = sGa.n
        p.op("gpsimd",
             lambda e, c=c: e.dma_start(
                 out=ctx_mine_d[c * 128:(c + 1) * 128, :], in_=gstg[c % 2][:, :]),
             waits=[(sGa, gv)], inc=sGa, inc_amt=16)
        gstg_out_val[c % 2] = sGa.n
    GATHER_ALL = sGa.n

    pA = contextlib.ExitStack()
    ctxp = pA.enter_context(nc.sbuf_tensor("ctxp", [128, OCH, SUB], F16))
    owp = [pA.enter_context(nc.sbuf_tensor(f"owp{i}", [128, ICH, 128], F16)) for i in range(2)]
    xtp = [pA.enter_context(nc.sbuf_tensor(f"xtp{i}", [128, SUB], F16)) for i in range(4)]
    ysb = pA.enter_context(nc.sbuf_tensor("ysb", [128, OCH, SUB], F32))
    y16t = pA.enter_context(nc.sbuf_tensor("y16t", [128, SUB], F16))
    sq16t = pA.enter_context(nc.sbuf_tensor("sq16t", [128, SUB], F16))
    obt_sb = pA.enter_context(nc.sbuf_tensor("obt_sb", [128, OCH], F32))
    mrow = pA.enter_context(nc.sbuf_tensor("mrow", [1, 2 * SUB], F32))
    vrow = pA.enter_context(nc.sbuf_tensor("vrow", [1, 2 * SUB], F32))
    mb_sb = pA.enter_context(nc.sbuf_tensor("mb_sb", [128, SUB], F32))
    rstdb = pA.enter_context(nc.sbuf_tensor("rstdb", [128, SUB], F32))
    tnorm = [pA.enter_context(nc.sbuf_tensor(f"tnorm{i}", [128, SUB], F32)) for i in range(3)]
    ps_op = pA.enter_context(nc.psum_tensor("ps_op", [128, 4 * SUB], F32))
    ps_st1 = pA.enter_context(nc.psum_tensor("ps_st1", [128, 2 * SUB], F32))
    ps_bc = pA.enter_context(nc.psum_tensor("ps_bc", [128, SUB], F32))

    p.op("sync", lambda e: e.dma_start(out=obt_sb[:, :], in_=obt[:, :]),
         waits=[(sAG, AG_ALL)], inc=sCst, inc_amt=16)
    OBT_LD = sCst.n

    ow_war = {}
    xt_war = {}
    evac_war = {}
    tnorm_war = [0, 0, 0]
    ga_war = 0
    for sub in range(NSUB):
        waits = [(sGa, GATHER_ALL)]
        if sub > 0:
            waits.append((sPm, ga_war))
        p.op("sync",
             lambda e, sub=sub: e.dma_start(
                 out=ctxp[:, :, :],
                 in_=ctx_mine_d[:, sub * SUB:(sub + 1) * SUB]
                 .rearrange("(c p) t -> p c t", p=128)),
             waits=waits, inc=sGa, inc_amt=16)
        ga_val = sGa.n

        for c in range(OCH):
            ci = sub * OCH + c
            ob_ = owp[ci % 2]
            waits = []
            if ci >= 2:
                waits.append((sPm, ow_war[ci - 2]))
            p.op("gpsimd",
                 lambda e, ob_=ob_, c=c: e.dma_start(
                     out=ob_[:, :, :],
                     in_=ow[:, c * 128:(c + 1) * 128].rearrange("(i p) c -> p i c", p=128)),
                 waits=waits, inc=sOwl, inc_amt=16)
            owl_val = sOwl.n
            xb = xtp[ci % 4]
            waits = [(sCastM, 16)]
            if ci >= 4:
                waits.append((sPv, xt_war[ci - 4]))
            p.op("sync",
                 lambda e, xb=xb, c=c, sub=sub: e.dma_start(
                     out=xb[:, :],
                     in_=x16md[sub * SUB:(sub + 1) * SUB, c * 128:(c + 1) * 128],
                     transpose=True),
                 waits=waits, inc=sXtp, inc_amt=16)
            xtp_val = sXtp.n
            bank = ci % 4
            waits = [(sOwl, owl_val), (sGa, ga_val)]
            if ci >= 4:
                waits.append((sPe, evac_war[ci - 4]))
            for i in range(ICH):
                p.op("tensor",
                     lambda e, ob_=ob_, i=i, bank=bank: e.matmul(
                         ps_op[:, bank * SUB:(bank + 1) * SUB],
                         ob_[:, i, :], ctxp[:, i, :],
                         start=(i == 0), stop=(i == ICH - 1)),
                     waits=waits if i == 0 else (),
                     inc=sPm if i == ICH - 1 else None)
            ow_war[ci] = sPm.n
            mm_val = sPm.n
            tn = tnorm[ci % 3]
            waits = [(sPm, mm_val), (sCst, OBT_LD)]
            if tnorm_war[ci % 3] > 0:
                waits.append((sPv, tnorm_war[ci % 3]))
            p.op("scalar",
                 lambda e, tn=tn, bank=bank, c=c: e.activation(
                     tn[:, :], ps_op[:, bank * SUB:(bank + 1) * SUB],
                     AF.Identity, bias=obt_sb[:, c:c + 1], scale=1.0),
                 waits=waits, inc=sPe)
            evac_war[ci] = sPe.n
            p.op("vector",
                 lambda e, c=c, xb=xb, tn=tn: e.tensor_add(ysb[:, c, :], tn[:, :], xb[:, :]),
                 waits=[(sPe, sPe.n), (sXtp, xtp_val)], inc=sPv)
            tnorm_war[ci % 3] = sPv.n
            xt_war[ci] = sPv.n
            p.op("vector",
                 lambda e, c=c: e.tensor_copy(y16t[:, :], ysb[:, c, :]),
                 waits=(), inc=sPv)
            p.op("vector",
                 lambda e: e.tensor_mul(sq16t[:, :], y16t[:, :], y16t[:, :]),
                 waits=(), inc=sPv)
            sqv = sPv.n
            p.op("tensor",
                 lambda e, c=c: e.matmul(ps_st1[0:1, 0:SUB], ones16[:, :], y16t[:, :],
                                         start=(c == 0), stop=(c == OCH - 1)),
                 waits=[(sPv, sqv)],
                 inc=None)
            p.op("tensor",
                 lambda e, c=c: e.matmul(ps_st1[0:1, SUB:2 * SUB], ones16[:, :], sq16t[:, :],
                                         start=(c == 0), stop=(c == OCH - 1)),
                 waits=(),
                 inc=sPm if c == OCH - 1 else None)
        ga_war = sPm.n
        st_done = sPm.n
        p.op("vector",
             lambda e: e.tensor_single_scalar(mrow[0:1, 0:SUB], ps_st1[0:1, 0:SUB],
                                              1.0 / D, ALU.mult),
             waits=[(sPm, st_done)], inc=sPv)
        p.op("vector",
             lambda e: e.tensor_single_scalar(mrow[0:1, SUB:2 * SUB], ps_st1[0:1, SUB:2 * SUB],
                                              1.0 / D, ALU.mult),
             waits=(), inc=sPv)
        p.op("vector",
             lambda e: e.tensor_mul(vrow[0:1, 0:SUB], mrow[0:1, 0:SUB], mrow[0:1, 0:SUB]),
             waits=(), inc=sPv)
        p.op("vector",
             lambda e: e.tensor_sub(vrow[0:1, 0:SUB], mrow[0:1, SUB:2 * SUB],
                                    vrow[0:1, 0:SUB]),
             waits=(), inc=sPv)
        p.op("vector",
             lambda e: e.tensor_single_scalar(vrow[0:1, 0:SUB], vrow[0:1, 0:SUB],
                                              EPS, ALU.add),
             waits=(), inc=sPv)
        varv = sPv.n
        p.op("scalar",
             lambda e: e.activation(vrow[0:1, SUB:2 * SUB], vrow[0:1, 0:SUB], AF.Ln),
             waits=[(sPv, varv)], inc=sPs)
        p.op("scalar",
             lambda e: e.activation(vrow[0:1, 0:SUB], vrow[0:1, SUB:2 * SUB], AF.Exp,
                                    scale=-0.5),
             waits=(), inc=None)
        p.op("scalar",
             lambda e: e.activation(vrow[0:1, SUB:2 * SUB], vrow[0:1, 0:SUB], AF.Identity),
             waits=(), inc=sPs)
        rstd_val = sPs.n
        p.op("tensor",
             lambda e: e.matmul(ps_bc[:, :], ones32r[:, :], mrow[0:1, 0:SUB],
                                start=True, stop=True),
             waits=[(sPs, rstd_val)], inc=sPm)
        p.op("vector",
             lambda e: e.tensor_copy(mb_sb[:, :], ps_bc[:, :]),
             waits=[(sPm, sPm.n)], inc=sPv)
        p.op("tensor",
             lambda e: e.matmul(ps_bc[:, :], ones32r[:, :], vrow[0:1, 0:SUB],
                                start=True, stop=True),
             waits=[(sPv, sPv.n)], inc=sPm)
        p.op("vector",
             lambda e: e.tensor_copy(rstdb[:, :], ps_bc[:, :]),
             waits=[(sPm, sPm.n)], inc=sPv)
        bc_done = sPv.n
        for c in range(OCH):
            tn = tnorm[c % 3]
            waits = [(sPv, bc_done)]
            p.op("vector",
                 lambda e, c=c, tn=tn: e.tensor_sub(tn[:, :], ysb[:, c, :], mb_sb[:, :]),
                 waits=waits, inc=None)
            p.op("vector",
                 lambda e, c=c, sub=sub, tn=tn: e.tensor_mul(
                     y1_16[:, c, sub * SUB:(sub + 1) * SUB], tn[:, :], rstdb[:, :]),
                 waits=(), inc=sPv)
        tnorm_war = [sPv.n] * 3
    Y1_DONE = sPv.n
    pA.close()

    pBs = contextlib.ExitStack()
    bm16 = pBs.enter_context(nc.sbuf_tensor("bm16", [32, D], F16))
    pr16 = pBs.enter_context(nc.sbuf_tensor("pr16", [32, NSUB, SUB], F16))
    pB = contextlib.ExitStack()
    a16 = pB.enter_context(nc.sbuf_tensor("a16", [32, D], F16))
    aT16 = pB.enter_context(nc.sbuf_tensor("aT16", [128, ICH, 32], F16))
    wr16 = pB.enter_context(nc.sbuf_tensor("wr16", [4, D], F16))
    wrT16 = pB.enter_context(nc.sbuf_tensor("wrT16", [128, ICH, 4], F16))
    sel16 = pB.enter_context(nc.sbuf_tensor("sel16", [4, 32], F16))
    gl32 = pB.enter_context(nc.sbuf_tensor("gl32", [4, SUB], F32))
    ge32 = pB.enter_context(nc.sbuf_tensor("ge32", [4, SUB], F32))
    rb4 = pB.enter_context(nc.sbuf_tensor("rb4", [4, SUB], F32))
    grow2 = pB.enter_context(nc.sbuf_tensor("grow2", [1, SUB], F32))
    ones4 = pB.enter_context(nc.sbuf_tensor("ones4", [4, 1], F32))
    gem16 = pB.enter_context(nc.sbuf_tensor("gem16", [4, SUB], F16))
    grep_sb = pB.enter_context(nc.sbuf_tensor("grep_sb", [32, SUB], F32))
    ps_g = pB.enter_context(nc.psum_tensor("ps_g", [128, 512], F32))
    ps_g2 = pB.enter_context(nc.psum_tensor("ps_g2", [128, 512], F32))
    ps_tA = pB.enter_context(nc.psum_tensor("ps_tA", [128, 64], F16))

    p.op("vector", lambda e: e.memset(ones4[:, :], 1.0),
         waits=[(sPv, Y1_DONE)], inc=sPv)
    CONST_V = sPv.n
    p.op("gpsimd", lambda e: e.dma_start(out=a16[:, :], in_=A[:, :]),
         waits=[(sPv, Y1_DONE)], inc=sGa, inc_amt=16)
    p.op("gpsimd", lambda e: e.dma_start(out=wr16[:, :], in_=Wr[:, :]), inc=sGa, inc_amt=16)
    p.op("gpsimd", lambda e: e.dma_start(out=bm16[:, :], in_=BmT[:, :]), inc=sGa, inc_amt=16)
    p.op("sync", lambda e: e.dma_start(out=sel16[:, :], in_=sel[:, :]),
         waits=[(sPv, Y1_DONE)], inc=sCst, inc_amt=16)
    SMALL_G = sGa.n
    SMALL_C = sCst.n
    for i in range(ICH):
        p.op("tensor",
             lambda e, i=i: e.transpose(ps_tA[:, 0:32],
                                        a16[:, i * 128:(i + 1) * 128], ident16[0:32, 0:32]),
             waits=[(sGa, SMALL_G), (sCst, CONST_LD), (sPv, sPv.n)], inc=sPm)
        p.op("vector",
             lambda e, i=i: e.tensor_copy(aT16[:, i, :], ps_tA[:, 0:32]),
             waits=[(sPm, sPm.n)], inc=sPv)
        p.op("tensor",
             lambda e, i=i: e.transpose(ps_tA[:, 32:36],
                                        wr16[:, i * 128:(i + 1) * 128], ident16[0:4, 0:4]),
             waits=[(sPv, sPv.n)], inc=sPm)
        p.op("vector",
             lambda e, i=i: e.tensor_copy(wrT16[:, i, :], ps_tA[:, 32:36]),
             waits=[(sPm, sPm.n)], inc=sPv)
    TRS_V = sPv.n

    for sub in range(NSUB):
        for i in range(ICH):
            p.op("tensor",
                 lambda e, i=i, sub=sub: e.matmul(
                     ps_g[0:4, 0:SUB], wrT16[:, i, :],
                     y1_16[:, i, sub * SUB:(sub + 1) * SUB],
                     start=(i == 0), stop=(i == ICH - 1)),
                 waits=[(sPv, TRS_V)] if i == 0 else (),
                 inc=sPm if i == ICH - 1 else None)
        p.op("vector",
             lambda e: e.tensor_copy(gl32[:, :], ps_g[0:4, 0:SUB]),
             waits=[(sPm, sPm.n)], inc=sPv)
        glv = sPv.n
        p.op("scalar",
             lambda e: e.activation(ge32[:, :], gl32[:, :], AF.Exp),
             waits=[(sPv, glv)], inc=None)
        p.op("scalar",
             lambda e: e.activation(rb4[:, :], ge32[:, :], AF.Identity),
             waits=(), inc=sPs)
        exp_drain = sPs.n
        p.op("tensor",
             lambda e: e.matmul(ps_g2[0:1, 0:SUB], ones4[:, :], ge32[:, :],
                                start=True, stop=True),
             waits=[(sPs, exp_drain), (sPv, CONST_V)], inc=sPm)
        p.op("vector",
             lambda e: e.reciprocal(grow2[0:1, :], ps_g2[0:1, 0:SUB]),
             waits=[(sPm, sPm.n)], inc=sPv)
        p.op("tensor",
             lambda e: e.matmul(ps_g2[0:4, 0:SUB], ones32r[0:1, 0:4], grow2[0:1, :],
                                start=True, stop=True),
             waits=[(sPv, sPv.n)], inc=sPm)
        p.op("vector",
             lambda e: e.tensor_copy(rb4[:, :], ps_g2[0:4, 0:SUB]),
             waits=[(sPm, sPm.n)], inc=sPv)
        p.op("vector",
             lambda e: e.tensor_tensor(gem16[:, :], ge32[:, :], rb4[:, :], op=ALU.mult),
             waits=(), inc=sPv)
        gem_done = sPv.n
        p.op("tensor",
             lambda e: e.matmul(ps_g[0:32, 0:SUB], sel16[:, :], gem16[:, :],
                                start=True, stop=True),
             waits=[(sPv, gem_done), (sCst, SMALL_C)], inc=sPm)
        p.op("vector",
             lambda e: e.tensor_copy(grep_sb[:, :], ps_g[0:32, 0:SUB]),
             waits=[(sPm, sPm.n)], inc=sPv)
        for i in range(ICH):
            p.op("tensor",
                 lambda e, i=i, sub=sub: e.matmul(
                     ps_g2[0:32, 0:SUB], aT16[:, i, :],
                     y1_16[:, i, sub * SUB:(sub + 1) * SUB],
                     start=(i == 0), stop=(i == ICH - 1)),
                 waits=[(sPv, sPv.n)] if i == 0 else (),
                 inc=sPm if i == ICH - 1 else None)
        p.op("vector",
             lambda e, sub=sub: e.tensor_tensor(pr16[:, sub, :], ps_g2[0:32, 0:SUB],
                                                grep_sb[:, :], op=ALU.mult),
             waits=[(sPm, sPm.n)], inc=sPv)
    PR_DONE = sPv.n
    pB.close()

    pC = contextlib.ExitStack()
    y2_16 = pC.enter_context(nc.sbuf_tensor("y2_16", [128, OCH, TLOC], F16, side="right"))
    ps_st2 = pC.enter_context(nc.psum_tensor("ps_st2", [128, 2 * NSUB * SUB], F32, side="right"))
    pW = contextlib.ExitStack()
    OSUP = 256
    wtp = [pW.enter_context(nc.sbuf_tensor(f"wtp{i}", [128, ICH, OSUP], F16)) for i in range(3)]
    ftmp = [pW.enter_context(nc.sbuf_tensor(f"ftmp{i}", [128, SUB], F32)) for i in range(3)]
    fsq = pW.enter_context(nc.sbuf_tensor("fsq", [128, SUB], F16))
    bt_sb = pW.enter_context(nc.sbuf_tensor("bt_sb", [128, OCH], F32))
    ps_f = pW.enter_context(nc.psum_tensor("ps_f", [128, 3 * SUB], F32))

    p.op("sync", lambda e: e.dma_start(out=bt_sb[:, :], in_=bt[:, :]),
         waits=[(sPv, PR_DONE)], inc=sCst, inc_amt=16)
    BT_LD = sCst.n

    NOS = D // OSUP
    OCPS = OSUP // 128
    wt_war = {}
    f_evac_war = {}
    ftmp_war = [0, 0, 0]
    for osup in range(NOS):
        wb = wtp[osup % 3]
        waits = [(sCastW, 16 * 8)]
        if osup == 0:
            waits.append((sPv, PR_DONE))
        if osup >= 3:
            waits.append((sPm, wt_war[osup - 3]))
        for i in range(ICH):
            p.op("sync",
                 lambda e, wb=wb, osup=osup, i=i: e.dma_start(
                     out=wb[:, i, :],
                     in_=w16d[osup * OSUP:(osup + 1) * OSUP, i * 128:(i + 1) * 128],
                     transpose=True),
                 waits=waits if i == 0 else (), inc=sWtl, inc_amt=16)
        wt_val = sWtl.n
        for sub in range(NSUB):
            for oc in range(OCPS):
                og = osup * OCPS + oc
                fi = (osup * NSUB + sub) * OCPS + oc
                bank = fi % 3
                waits = [(sWtl, wt_val), (sPv, PR_DONE)]
                if fi >= 3:
                    waits.append((sPe, f_evac_war[fi - 3]))
                for i in range(ICH):
                    p.op("tensor",
                         lambda e, wb=wb, i=i, oc=oc, sub=sub, bank=bank: e.matmul(
                             ps_f[:, bank * SUB:(bank + 1) * SUB],
                             wb[:, i, oc * 128:(oc + 1) * 128],
                             y1_16[:, i, sub * SUB:(sub + 1) * SUB],
                             start=(i == 0), stop=False),
                         waits=waits if i == 0 else (), inc=None)
                p.op("tensor",
                     lambda e, og=og, sub=sub, bank=bank: e.matmul(
                         ps_f[:, bank * SUB:(bank + 1) * SUB],
                         bm16[:, og * 128:(og + 1) * 128],
                         pr16[:, sub, :],
                         start=False, stop=True),
                     waits=(), inc=sPm)
                mm_val = sPm.n
                ft = ftmp[fi % 3]
                waits = [(sPm, mm_val), (sCst, BT_LD)]
                if ftmp_war[fi % 3] > 0:
                    waits.append((sPv, ftmp_war[fi % 3]))
                p.op("scalar",
                     lambda e, ft=ft, bank=bank, og=og: e.activation(
                         ft[:, :], ps_f[:, bank * SUB:(bank + 1) * SUB],
                         AF.Identity, bias=bt_sb[:, og:og + 1], scale=1.0),
                     waits=waits, inc=sPe)
                f_evac_war[fi] = sPe.n
                p.op("vector",
                     lambda e, og=og, sub=sub, ft=ft: e.tensor_add(
                         y2_16[:, og, sub * SUB:(sub + 1) * SUB],
                         ft[:, :],
                         y1_16[:, og, sub * SUB:(sub + 1) * SUB]),
                     waits=[(sPe, sPe.n)], inc=sPv)
                ftmp_war[fi % 3] = sPv.n
                p.op("vector",
                     lambda e, og=og, sub=sub: e.tensor_mul(
                         fsq[:, :],
                         y2_16[:, og, sub * SUB:(sub + 1) * SUB],
                         y2_16[:, og, sub * SUB:(sub + 1) * SUB]),
                     waits=(), inc=sPv)
                sq_val = sPv.n
                first = (osup == 0 and oc == 0)
                last = (osup == NOS - 1 and oc == OCPS - 1)
                p.op("tensor",
                     lambda e, og=og, sub=sub, first=first, last=last: e.matmul(
                         ps_st2[0:1, (2 * sub) * SUB:(2 * sub + 1) * SUB], ones16[:, :],
                         y2_16[:, og, sub * SUB:(sub + 1) * SUB],
                         start=first, stop=last),
                     waits=[(sPv, sq_val)], inc=None)
                p.op("tensor",
                     lambda e, sub=sub, first=first, last=last: e.matmul(
                         ps_st2[0:1, (2 * sub + 1) * SUB:(2 * sub + 2) * SUB],
                         ones16[:, :], fsq[:, :],
                         start=first, stop=last),
                     waits=(), inc=sPm if last else None)
        wt_war[osup] = sPm.n
    FFN_DONE = sPm.n
    pW.close()
    pBs.close()
    ph3.close()

    pD = contextlib.ExitStack()
    m2row = pD.enter_context(nc.sbuf_tensor("m2row", [1, 2 * SUB], F32))
    v2row = pD.enter_context(nc.sbuf_tensor("v2row", [1, 2 * SUB], F32))
    mtok = pD.enter_context(nc.sbuf_tensor("mtok", [128, 2], F32))
    y2tok = pD.enter_context(nc.sbuf_tensor("y2tok", [128, OCH * 128], F16))
    fin32 = [pD.enter_context(nc.sbuf_tensor(f"fin{i}", [128, D], F32)) for i in range(2)]
    ps_t = pD.enter_context(nc.psum_tensor("ps_t", [128, 1024], F32))
    ps_t16 = ps_t.ap().bitcast(F16)

    fin_dma_val = [0, 0]
    for sub in range(NSUB):
        p.op("vector",
             lambda e, sub=sub: e.tensor_single_scalar(
                 m2row[0:1, 0:SUB], ps_st2[0:1, (2 * sub) * SUB:(2 * sub + 1) * SUB],
                 1.0 / D, ALU.mult),
             waits=[(sPm, FFN_DONE)], inc=sPv)
        p.op("vector",
             lambda e, sub=sub: e.tensor_single_scalar(
                 m2row[0:1, SUB:2 * SUB],
                 ps_st2[0:1, (2 * sub + 1) * SUB:(2 * sub + 2) * SUB],
                 1.0 / D, ALU.mult),
             waits=(), inc=sPv)
        p.op("vector",
             lambda e: e.tensor_mul(v2row[0:1, 0:SUB], m2row[0:1, 0:SUB],
                                    m2row[0:1, 0:SUB]),
             waits=(), inc=sPv)
        p.op("vector",
             lambda e: e.tensor_sub(v2row[0:1, 0:SUB], m2row[0:1, SUB:2 * SUB],
                                    v2row[0:1, 0:SUB]),
             waits=(), inc=sPv)
        p.op("vector",
             lambda e: e.tensor_single_scalar(v2row[0:1, 0:SUB], v2row[0:1, 0:SUB],
                                              EPS, ALU.add),
             waits=(), inc=sPv)
        varv = sPv.n
        p.op("scalar",
             lambda e: e.activation(v2row[0:1, SUB:2 * SUB], v2row[0:1, 0:SUB], AF.Ln),
             waits=[(sPv, varv)], inc=sPs)
        p.op("scalar",
             lambda e: e.activation(v2row[0:1, 0:SUB], v2row[0:1, SUB:2 * SUB], AF.Exp,
                                    scale=-0.5),
             waits=(), inc=None)
        p.op("scalar",
             lambda e: e.activation(v2row[0:1, SUB:2 * SUB], v2row[0:1, 0:SUB], AF.Identity),
             waits=(), inc=sPs)
        row_val = sPs.n
        for j in range(SUB // 128):
            p.op("tensor",
                 lambda e, j=j: e.transpose(ps_t.ap()[:, 0:1],
                                            m2row[0:1, j * 128:(j + 1) * 128],
                                            ident32[0:1, 0:1]),
                 waits=[(sPs, row_val), (sPv, sPv.n)], inc=sPm)
            p.op("vector",
                 lambda e: e.tensor_copy(mtok[:, 0:1], ps_t.ap()[:, 0:1]),
                 waits=[(sPm, sPm.n)], inc=sPv)
            p.op("tensor",
                 lambda e, j=j: e.transpose(ps_t.ap()[:, 0:1],
                                            v2row[0:1, j * 128:(j + 1) * 128],
                                            ident32[0:1, 0:1]),
                 waits=[(sPv, sPv.n)], inc=sPm)
            p.op("vector",
                 lambda e: e.tensor_copy(mtok[:, 1:2], ps_t.ap()[:, 0:1]),
                 waits=[(sPm, sPm.n)], inc=sPv)
            tr_copy_val = [sPv.n, sPv.n]
            for oc in range(OCH):
                par = oc % 2
                reg = ps_t16[:, 0:128] if par == 0 else ps_t16[:, 1024:1152]
                p.op("tensor",
                     lambda e, oc=oc, sub=sub, j=j, reg=reg: e.transpose(
                         reg,
                         y2_16[:, oc, sub * SUB + j * 128: sub * SUB + (j + 1) * 128],
                         ident16[:, :]),
                     waits=[(sPv, tr_copy_val[par])], inc=sPm)
                p.op("vector",
                     lambda e, oc=oc, reg=reg: e.tensor_copy(
                         y2tok[:, oc * 128:(oc + 1) * 128], reg),
                     waits=[(sPm, sPm.n)], inc=sPv)
                tr_copy_val[par] = sPv.n
            fb = fin32[j % 2]
            waits = [(sPv, sPv.n)]
            if fin_dma_val[j % 2] > 0:
                waits.append((sOut, fin_dma_val[j % 2]))
            p.op("vector",
                 lambda e, fb=fb: e.tensor_scalar(
                     fb[:, :], y2tok[:, :],
                     mtok[:, 0:1], mtok[:, 1:2],
                     op0=ALU.subtract, op1=ALU.mult),
                 waits=waits, inc=sPv)
            p.op("gpsimd",
                 lambda e, fb=fb, sub=sub, j=j: e.dma_start(
                     out=y_out[sub * SUB + j * 128: sub * SUB + (j + 1) * 128, :],
                     in_=fb[:, :]),
                 waits=[(sPv, sPv.n)], inc=sOut, inc_amt=16)
            fin_dma_val[j % 2] = sOut.n

    p.op("gpsimd", lambda e: None, waits=[(sOut, sOut.n)])

    with nc.Block() as block:
        @block.sync
        def _(e):
            for f in p.q["sync"]:
                f(e)

        @block.tensor
        def _(e):
            for f in p.q["tensor"]:
                f(e)

        @block.vector
        def _(e):
            for f in p.q["vector"]:
                f(e)

        @block.scalar
        def _(e):
            for f in p.q["scalar"]:
                f(e)

        @block.gpsimd
        def _(e):
            for f in p.q["gpsimd"]:
                f(e)

    return nc


def kernel(L0_attn_x, L0_attn_attn_qkvw, L0_attn_attn_qkvb, L0_attn_attn_ow,
           L0_attn_attn_ob, L0_ffn_W, L0_ffn_b, L0_ffn_A, L0_ffn_Bm, L0_ffn_Wr):
    import ml_dtypes
    x = np.ascontiguousarray(np.asarray(L0_attn_x, dtype=np.float32).reshape(S, D))
    qkvw = np.asarray(L0_attn_attn_qkvw, dtype=np.float32)
    qkvb = np.asarray(L0_attn_attn_qkvb, dtype=np.float32)
    ow = np.ascontiguousarray(np.asarray(L0_attn_attn_ow, dtype=np.float32))
    ob = np.asarray(L0_attn_attn_ob, dtype=np.float32)
    W = np.ascontiguousarray(np.asarray(L0_ffn_W, dtype=np.float32))
    b = np.asarray(L0_ffn_b, dtype=np.float32)
    A = np.ascontiguousarray(np.asarray(L0_ffn_A, dtype=np.float32).reshape(32, D))
    Bm = np.asarray(L0_ffn_Bm, dtype=np.float32)
    Wr = np.ascontiguousarray(np.asarray(L0_ffn_Wr, dtype=np.float32))

    BmT = np.ascontiguousarray(Bm.transpose(0, 2, 1).reshape(32, D))
    sel = np.zeros((4, 32), dtype=np.float16)
    for e_ in range(4):
        sel[e_, e_ * 8:(e_ + 1) * 8] = 2.0
    id16 = np.eye(128, dtype=np.float16)
    idbf = np.eye(128).astype(ml_dtypes.bfloat16).view(np.uint16)
    id32 = np.eye(128, dtype=np.float32)
    obt = np.ascontiguousarray(ob.reshape(OCH, 128).T)
    btt = np.ascontiguousarray(b.reshape(OCH, 128).T)

    nc = build_nc()

    in_maps = []
    for r in range(NCORES):
        heads = range(NHL * r, NHL * (r + 1))
        cols = []
        bcols = []
        for h in heads:
            for k in range(3):
                cols.append(qkvw[:, k * D + h * 128:k * D + (h + 1) * 128])
                bcols.append(qkvb[k * D + h * 128:k * D + (h + 1) * 128])
        qkvw_r = np.ascontiguousarray(np.concatenate(cols, axis=1))
        qkvb_r = np.ascontiguousarray(np.stack(bcols, axis=1))
        gi = np.zeros((128, OCH), dtype=np.int32)
        for c in range(OCH):
            lh = c % NHL
            rp = c // NHL
            gi[:, c] = (lh * 1024 + rp * 128 + np.arange(128)) * S + r * TLOC
        in_maps.append({
            "x32": x,
            "x32m": np.ascontiguousarray(x[r * TLOC:(r + 1) * TLOC, :]),
            "qkvw": qkvw_r,
            "qkvb": qkvb_r,
            "ow": ow,
            "obt": obt,
            "W": W,
            "bt": btt,
            "A": A,
            "BmT": BmT,
            "Wr": Wr,
            "sel": sel,
            "id16": id16,
            "idbf": idbf,
            "id32": id32,
            "gidx": gi,
        })

    res = run_bass_kernel_spmd(nc, in_maps, core_ids=list(range(NCORES)))
    out = np.concatenate([res.results[r]["y_out"] for r in range(NCORES)], axis=0)
    return out.reshape(1, S, D).astype(np.float32)


# version 16
# speedup vs baseline: 1.1756x; 1.0260x over previous
import contextlib
import math
import numpy as np

import concourse.bass as bass
import concourse.mybir as mybir
from concourse.bass import IndirectOffsetOnAxis
from concourse.bass_utils import run_bass_kernel_spmd

F32 = mybir.dt.float32
F16 = mybir.dt.float16
BF16 = mybir.dt.bfloat16
I32 = mybir.dt.int32
AF = mybir.ActivationFunctionType
ALU = mybir.AluOpType
AXX = mybir.AxisListType.X

NCORES = 8
S = 8192
D = 4096
HD = 128
NHL = 4
TLOC = S // NCORES
SCALE = 1.0 / math.sqrt(HD)
EPS = 1e-5

CCH = 3 * NHL
ICH = D // 128
TT = 512
NTT = S // TT
TQ = 512
NQT = S // TQ
KCH = S // 128
GS = 3
SUB = 512
NSUB = TLOC // SUB
OCH = D // 128


class Sem:
    def __init__(self, handle, name):
        self.h = handle
        self.name = name
        self.n = 0


class Prog:
    QUEUES = ("sync", "tensor", "vector", "scalar", "gpsimd")

    def __init__(self):
        self.q = {k: [] for k in self.QUEUES}

    def op(self, qn, fn, waits=(), inc=None, inc_amt=1, defer=None):
        for s, v in waits:
            assert v <= s.n, f"wait on {s.name} for {v} but only {s.n} recorded"
        wl = [(s.h, v) for s, v in waits if v > 0]
        if inc is not None:
            inc.n += inc_amt
            ih, ia = inc.h, inc_amt
        else:
            ih, ia = None, 0

        def run(eng, wl=wl, fn=fn, ih=ih, ia=ia):
            for h, v in wl:
                eng.wait_ge(h, v)
            ins = fn(eng)
            if ih is not None:
                if ins is None:
                    ins = eng.nop()
                ins.then_inc(ih, ia)
        if defer is not None:
            defer.append((qn, run))
        else:
            self.q[qn].append(run)

    def flush(self, defer):
        for qn, run in defer:
            self.q[qn].append(run)
        defer.clear()


def build_nc():
    nc = bass.Bass("TRN2", target_bir_lowering=False)

    x32 = nc.dram_tensor("x32", [S, D], F32, kind="ExternalInput").ap()
    x32m = nc.dram_tensor("x32m", [TLOC, D], F32, kind="ExternalInput").ap()
    qkvw = nc.dram_tensor("qkvw", [D, CCH * 128], F32, kind="ExternalInput").ap()
    qkvb = nc.dram_tensor("qkvb", [128, CCH], F32, kind="ExternalInput").ap()
    ow = nc.dram_tensor("ow", [D, D], F32, kind="ExternalInput").ap()
    obt = nc.dram_tensor("obt", [128, OCH], F32, kind="ExternalInput").ap()
    Wt = nc.dram_tensor("W", [D, D], F32, kind="ExternalInput").ap()
    bt = nc.dram_tensor("bt", [128, OCH], F32, kind="ExternalInput").ap()
    A = nc.dram_tensor("A", [32, D], F32, kind="ExternalInput").ap()
    BmT = nc.dram_tensor("BmT", [32, D], F32, kind="ExternalInput").ap()
    Wr = nc.dram_tensor("Wr", [4, D], F32, kind="ExternalInput").ap()
    sel = nc.dram_tensor("sel", [4, 32], F16, kind="ExternalInput").ap()
    id16 = nc.dram_tensor("id16", [128, 128], F16, kind="ExternalInput").ap()
    idbf = nc.dram_tensor("idbf", [128, 128], BF16, kind="ExternalInput").ap()
    id32 = nc.dram_tensor("id32", [128, 128], F32, kind="ExternalInput").ap()
    gidx = nc.dram_tensor("gidx", [128, OCH], I32, kind="ExternalInput").ap()

    y_out = nc.dram_tensor("y_out", [TLOC, D], F32, kind="ExternalOutput").ap()
    x16md = nc.dram_tensor("x16md", [TLOC, D], F16).ap()
    w16d = nc.dram_tensor("w16d", [D, D], F16).ap()
    qTd = nc.dram_tensor("qTd", [NHL * 128, S], F16).ap()
    kTd = nc.dram_tensor("kTd", [NHL * 128, S], F16).ap()
    vd = nc.dram_tensor("vd", [S, NHL * 128], BF16).ap()
    ag_in = nc.dram_tensor("ag_in", [NHL * 128, S], F16).ap()
    ag_out = nc.dram_tensor("ag_out", [NHL * 1024, S], F16, addr_space="Shared").ap()
    ctx_mine_d = nc.dram_tensor("ctx_mine_d", [D, TLOC], F16).ap()
    ag_out_flat = ag_out.rearrange("a b -> (a b)").rearrange("(n one) -> n one", one=1)

    p = Prog()
    stack = contextlib.ExitStack()
    nc._kernel_stack = stack

    def sem(name):
        return Sem(stack.enter_context(nc.semaphore(name)), name)

    sCast = sem("sCast")
    sCastM = sem("sCastM")
    sCastW = sem("sCastW")
    sWld = sem("sWld")
    sXT = sem("sXT")
    sTx = sem("sTx")
    sTxe = sem("sTxe")
    sMM = sem("sMM")
    sEv = sem("sEv")
    sVt = sem("sVt")
    sVe = sem("sVe")
    sSp = sem("sSp")
    sNq = sem("sNq")
    sNm = sem("sNm")
    sBh = sem("sBh")
    sLd = sem("sLd")
    sST = sem("sST")
    sEX = sem("sEX")
    sPV = sem("sPV")
    sTR = sem("sTR")
    sTRb = sem("sTRb")
    sEp = sem("sEp")
    sAG = sem("sAG")
    sCC = sem("sCC")
    sGa = sem("sGa")
    sOwl = sem("sOwl")
    sXtp = sem("sXtp")
    sPm = sem("sPm")
    sPe = sem("sPe")
    sPv = sem("sPv")
    sWtl = sem("sWtl")
    sPs = sem("sPs")
    sOut = sem("sOut")
    sIdx = sem("sIdx")
    sCst = sem("sCst")

    ident16 = stack.enter_context(nc.sbuf_tensor("ident16", [128, 128], F16))
    identbf = stack.enter_context(nc.sbuf_tensor("identbf", [128, 128], BF16))
    ident32 = stack.enter_context(nc.sbuf_tensor("ident32", [128, 128], F32))
    ones16 = stack.enter_context(nc.sbuf_tensor("ones16", [128, 1], F16))
    onesbf = stack.enter_context(nc.sbuf_tensor("onesbf", [128, 1], BF16))
    ones32r = stack.enter_context(nc.sbuf_tensor("ones32r", [1, 128], F32))
    qkvb_sb = stack.enter_context(nc.sbuf_tensor("qkvb_sb", [128, CCH], F32))
    bias_h = stack.enter_context(nc.sbuf_tensor("bias_h", [128, NHL], F32))
    norm_acc = stack.enter_context(nc.sbuf_tensor("norm_acc", [1, 2 * NHL], F32))
    rowtmp = stack.enter_context(nc.sbuf_tensor("rowtmp", [1, 8], F32))

    p.op("sync", lambda e: e.dma_start(out=ident16[:, :], in_=id16[:, :]), inc=sCst, inc_amt=16)
    p.op("sync", lambda e: e.dma_start(out=identbf[:, :], in_=idbf[:, :]), inc=sCst, inc_amt=16)
    p.op("sync", lambda e: e.dma_start(out=ident32[:, :], in_=id32[:, :]), inc=sCst, inc_amt=16)
    p.op("sync", lambda e: e.dma_start(out=qkvb_sb[:, :], in_=qkvb[:, :]), inc=sCst, inc_amt=16)
    p.op("vector", lambda e: e.memset(ones16[:, :], 1.0), inc=sPv)
    p.op("vector", lambda e: e.memset(onesbf[:, :], 1.0), inc=sPv)
    p.op("vector", lambda e: e.memset(ones32r[:, :], 1.0), inc=sPv)
    p.op("vector", lambda e: e.memset(norm_acc[:, :], 0.0), inc=sPv)
    CONST_LD = sCst.n
    CONST_V = sPv.n

    ph1 = contextlib.ExitStack()
    qkvw_sb = ph1.enter_context(nc.sbuf_tensor("qkvw_sb", [128, CCH * ICH, 128], F16))
    xt_sb = [ph1.enter_context(nc.sbuf_tensor(f"xt{i}", [128, ICH, TT], F16)) for i in range(2)]
    stg = [ph1.enter_context(nc.sbuf_tensor(f"stg{i}", [128, TT], F16)) for i in range(3)]
    stgv = [ph1.enter_context(nc.sbuf_tensor(f"stgv{i}", [128, TT], BF16)) for i in range(2)]
    vstg = [ph1.enter_context(nc.sbuf_tensor(f"vstg{i}", [128, TT], BF16)) for i in range(2)]
    sq_sb = [ph1.enter_context(nc.sbuf_tensor(f"sq_sb{i}", [128, TT], F16)) for i in range(2)]
    xtok = [ph1.enter_context(nc.sbuf_tensor(f"xtok{i}", [128, D], F16)) for i in range(2)]
    ps_tx = [ph1.enter_context(nc.psum_tensor(f"ps_tx{i}", [128, TT], F16)) for i in range(2)]
    ps_qkv = ph1.enter_context(nc.psum_tensor("ps_qkv", [128, 3 * TT], F32))
    ps_vt = ph1.enter_context(nc.psum_tensor("ps_vt", [128, TT], BF16))
    ps_qn = ph1.enter_context(nc.psum_tensor("ps_qn", [128, 512], F32))
    ps_bb = ph1.enter_context(nc.psum_tensor("ps_bb", [128, 512], F32))

    for c in range(CCH):
        p.op("gpsimd",
             lambda e, c=c: e.dma_start(
                 out=qkvw_sb[:, c * ICH:(c + 1) * ICH, :],
                 in_=qkvw[:, c * 128:(c + 1) * 128].rearrange("(i p) c -> p i c", p=128)),
             inc=sWld, inc_amt=16)
    QKVW_LD = sWld.n

    panel_mm_done = {}
    qk_spill_val = [0, 0, 0]
    v_spill_val = [0, 0]
    vstg_evac_val = [0, 0]
    sq_mm_val = [0, 0]
    qk_ctr = 0
    v_ctr = 0
    norm_defer = []
    panel_mm_done = {}
    tx_copy_done = [0, 0]
    xtok_free = [0, 0]
    for t in range(NTT):
        for q in range(4):
            qidx = 4 * t + q
            xb = xtok[qidx % 2]
            cwaits = []
            if xtok_free[qidx % 2] > 0:
                cwaits.append((sTx, xtok_free[qidx % 2]))
            if q == 0 and t >= 2:
                cwaits.append((sMM, panel_mm_done[t - 2]))
            p.op("gpsimd",
                 lambda e, t=t, q=q, xb=xb: e.dma_start(
                     out=xb[:, :],
                     in_=x32[t * TT + q * 128: t * TT + (q + 1) * 128, :]),
                 waits=cwaits, inc=sCast, inc_amt=16)
            cast_val = sCast.n
            for g4 in range(8):
                bank = (qidx * 8 + g4) % 2
                for k4 in range(4):
                    i = g4 * 4 + k4
                    qidx_l = qidx
                    waits = []
                    if k4 == 0:
                        waits.append((sCast, cast_val))
                        if qidx == 0 and g4 == 0:
                            waits.append((sCst, CONST_LD))
                            waits.append((sPv, CONST_V))
                        if tx_copy_done[bank] > 0:
                            waits.append((sTxe, tx_copy_done[bank]))
                    p.op("tensor",
                         lambda e, xb=xb, i=i, bank=bank, k4=k4: e.transpose(
                             ps_tx[bank][:, k4 * 128:(k4 + 1) * 128],
                             xb[:, i * 128:(i + 1) * 128],
                             ident16[:, :]),
                         waits=waits, inc=sTx if k4 == 3 else None)
                tx_val = sTx.n
                p.op("vector",
                     lambda e, t=t, q=q, g4=g4, bank=bank: e.tensor_copy(
                         xt_sb[t % 2][:, g4 * 4:(g4 + 1) * 4, q * 128:(q + 1) * 128],
                         ps_tx[bank][:, :].rearrange("p (a b) -> p a b", a=4)),
                     waits=[(sTx, tx_val)], inc=sTxe)
                tx_copy_done[bank] = sTxe.n
            xtok_free[qidx % 2] = sTx.n
        panel_ready = sTxe.n

        for c in range(CCH):
            tile_idx = t * CCH + c
            bank = tile_idx % 3
            waits = [(sTxe, panel_ready)]
            if tile_idx == 0:
                waits.append((sWld, QKVW_LD))
                waits.append((sCst, CONST_LD))
                waits.append((sPv, CONST_V))
            if tile_idx >= 3:
                waits.append((sEv, tile_idx - 3 + 1))
            for i in range(ICH):
                p.op("tensor",
                     lambda e, c=c, t=t, i=i, bank=bank: e.matmul(
                         ps_qkv[:, bank * TT:(bank + 1) * TT],
                         qkvw_sb[:, c * ICH + i, :],
                         xt_sb[t % 2][:, i, :],
                         start=(i == 0), stop=(i == ICH - 1)),
                     waits=waits if i == 0 else (),
                     inc=sMM if i == ICH - 1 else None)
            mm_val = sMM.n
            p.flush(norm_defer)

            lh, kind = c // 3, c % 3
            if kind in (0, 1):
                bidx = qk_ctr % 3
                sqb = sq_sb[qk_ctr % 2]
                sq_war = sq_mm_val[qk_ctr % 2]
                qk_ctr += 1
                sb = stg[bidx]
                waits = [(sMM, mm_val), (sCst, CONST_LD)]
                if qk_spill_val[bidx] > 0:
                    waits.append((sSp, qk_spill_val[bidx]))
                p.op("scalar",
                     lambda e, sb=sb, bank=bank, c=c: e.activation(
                         sb[:, :], ps_qkv[:, bank * TT:(bank + 1) * TT],
                         AF.Identity, bias=qkvb_sb[:, c:c + 1], scale=1.0),
                     waits=waits, inc=sEv)
                ev_val = sEv.n
                dst = qTd if kind == 0 else kTd
                p.op("sync",
                     lambda e, sb=sb, dst=dst, lh=lh, t=t: e.dma_start(
                         out=dst[lh * 128:(lh + 1) * 128, t * TT:(t + 1) * TT],
                         in_=sb[:, :]),
                     waits=[(sEv, ev_val)], inc=sSp, inc_amt=16)
                qk_spill_val[bidx] = sSp.n
                waits = [(sEv, ev_val)]
                if sq_war > 0:
                    waits.append((sNm, sq_war))
                p.op("vector",
                     lambda e, sb=sb, sqb=sqb: e.tensor_mul(sqb[:, :], sb[:, :], sb[:, :]),
                     waits=waits, inc=sNq)
                p.op("tensor",
                     lambda e, sqb=sqb: e.matmul(ps_qn[0:1, 0:TT], ones16[:, :], sqb[:, :],
                                                 start=True, stop=True),
                     waits=[(sNq, sNq.n), (sNq, sNq.n)], inc=sNm, defer=norm_defer)
                sq_mm_val[(qk_ctr - 1) % 2] = sNm.n
                acc_col = lh * 2 + kind
                p.op("vector",
                     lambda e: e.reduce_max(rowtmp[0:1, 0:1], ps_qn[0:1, 0:TT], axis=AXX),
                     waits=[(sNm, sNm.n)], inc=sNq)
                p.op("vector",
                     lambda e, acc_col=acc_col: e.tensor_max(
                         norm_acc[0:1, acc_col:acc_col + 1],
                         norm_acc[0:1, acc_col:acc_col + 1],
                         rowtmp[0:1, 0:1]),
                     waits=(), inc=sNq)
            else:
                sbv = stgv[v_ctr % 2]
                vb = vstg[v_ctr % 2]
                waits = [(sMM, mm_val), (sCst, CONST_LD)]
                if vstg_evac_val[v_ctr % 2] > 0:
                    waits.append((sVt, vstg_evac_val[v_ctr % 2]))
                p.op("scalar",
                     lambda e, sbv=sbv, bank=bank, c=c: e.activation(
                         sbv[:, :], ps_qkv[:, bank * TT:(bank + 1) * TT],
                         AF.Identity, bias=qkvb_sb[:, c:c + 1], scale=1.0),
                     waits=waits, inc=sEv)
                ev_val = sEv.n
                for j in range(4):
                    p.op("tensor",
                         lambda e, sbv=sbv, j=j: e.transpose(
                             ps_vt[:, j * 128:(j + 1) * 128],
                             sbv[:, j * 128:(j + 1) * 128], identbf[:, :]),
                         waits=([(sEv, ev_val), (sVe, sVe.n)] if j == 0 else ()),
                         inc=sVt if j == 3 else None)
                vstg_evac_val[v_ctr % 2] = sVt.n
                vt_val = sVt.n
                waits = [(sVt, vt_val)]
                if v_spill_val[v_ctr % 2] > 0:
                    waits.append((sSp, v_spill_val[v_ctr % 2]))
                p.op("vector",
                     lambda e, vb=vb: e.tensor_copy(vb[:, :], ps_vt[:, :]),
                     waits=waits, inc=sVe)
                ve_val = sVe.n
                p.op("sync",
                     lambda e, vb=vb, lh=lh, t=t: e.dma_start(
                         out=vd[t * TT:(t + 1) * TT, lh * 128:(lh + 1) * 128]
                         .rearrange("(j p) d -> p j d", p=128),
                         in_=vb[:, :].rearrange("p (j d) -> p j d", j=4)),
                     waits=[(sVe, ve_val)], inc=sSp, inc_amt=16)
                v_spill_val[v_ctr % 2] = sSp.n
                v_ctr += 1
        panel_mm_done[t] = sMM.n
    p.flush(norm_defer)

    nq0 = sNq.n
    for lh in range(NHL):
        p.op("vector",
             lambda e, lh=lh: e.tensor_mul(rowtmp[0:1, 1:2],
                                           norm_acc[0:1, 2 * lh:2 * lh + 1],
                                           norm_acc[0:1, 2 * lh + 1:2 * lh + 2]),
             waits=[(sNq, nq0)], inc=sNq)
        p.op("scalar",
             lambda e: e.activation(rowtmp[0:1, 2:3], rowtmp[0:1, 1:2], AF.Ln),
             waits=[(sNq, sNq.n)], inc=sBh)
        p.op("scalar",
             lambda e: e.activation(rowtmp[0:1, 3:4], rowtmp[0:1, 2:3], AF.Exp, scale=0.5),
             waits=(), inc=None)
        p.op("scalar",
             lambda e: e.activation(rowtmp[0:1, 5:6], rowtmp[0:1, 3:4], AF.Identity),
             waits=(), inc=sBh)
        p.op("vector",
             lambda e: e.tensor_single_scalar(rowtmp[0:1, 4:5], rowtmp[0:1, 3:4],
                                              -SCALE, ALU.mult),
             waits=[(sBh, sBh.n)], inc=sNq)
        p.op("tensor",
             lambda e: e.matmul(ps_bb[:, 0:1], ones32r[:, :], rowtmp[0:1, 4:5],
                                start=True, stop=True),
             waits=[(sNq, sNq.n), (sBh, sBh.n)], inc=sNm)
        p.op("vector",
             lambda e, lh=lh: e.tensor_copy(bias_h[:, lh:lh + 1], ps_bb[:, 0:1]),
             waits=[(sNm, sNm.n)], inc=sBh)
    BH_DONE = sBh.n
    P1_SPILLS = sSp.n
    p.op("gpsimd", lambda e: e.dma_start(out=x16md[:, :], in_=x32m[:, :]),
         waits=[(sCast, 16 * NTT)], inc=sCastM, inc_amt=16)
    for c in range(8):
        p.op("gpsimd", lambda e, c=c: e.dma_start(out=w16d[c * 512:(c + 1) * 512, :],
                                                  in_=Wt[c * 512:(c + 1) * 512, :]),
             inc=sCastW, inc_amt=16)

    ph1.close()
    ph2 = contextlib.ExitStack()
    kT_sb = [ph2.enter_context(nc.sbuf_tensor(f"kT{i}", [128, S], F16)) for i in range(2)]
    qT_sb = [ph2.enter_context(nc.sbuf_tensor(f"qT{i}", [128, S], F16)) for i in range(2)]
    v_sb = [ph2.enter_context(nc.sbuf_tensor(f"v{i}", [128, KCH, 128], BF16)) for i in range(2)]
    pt_sb = [ph2.enter_context(nc.sbuf_tensor(f"pt{i}", [128, 2 * TQ], BF16)) for i in range(3)]
    tree_bufs = [ph2.enter_context(nc.sbuf_tensor(f"tr{i}", [128, TQ], BF16)) for i in range(16)]
    rrow = ph2.enter_context(nc.sbuf_tensor("rrow", [1, 2 * TQ], F32))
    rb_sb = ph2.enter_context(nc.sbuf_tensor("rb_sb", [128, 2 * TQ], F32))
    ctxn = [ph2.enter_context(nc.sbuf_tensor(f"ctxn{i}", [128, TQ], F16)) for i in range(2)]
    ps_st = [ph2.enter_context(nc.psum_tensor(f"ps_stt{i}", [128, 2 * TQ], F32))
             for i in range(3)]
    ps_ctx2 = ph2.enter_context(nc.psum_tensor("ps_ctx2", [128, 2 * TQ], F32))

    st_free = [0, 0, 0]
    pt_free_pv = [0, 0, 0]
    pt_free_trA = [0, 0, 0]
    pt_free_trB = [0, 0, 0]
    ctxn_dma_val = [0, 0]
    head_pv_done = {}

    def ap_of(tok):
        if tok[0] == "pt":
            return pt_sb[tok[1]][:, tok[2] * TQ:(tok[2] + 1) * TQ]
        return tok[1][:, :]

    class Tree:

        def __init__(self, pool):
            self.free = list(pool)
            self.occ = {}

        def insert_pair_ops(self, tokx, toky):
            ops = []
            outb = self.free.pop()
            ops.append(lambda e, a=tokx, bb=toky, outb=outb: e.tensor_add(
                outb[:, :], ap_of(a), ap_of(bb)))
            cur = ("buf", outb)
            lvl = 0
            while lvl in self.occ:
                prev = self.occ.pop(lvl)
                outb = prev[1]
                ops.append(lambda e, a=cur, bb=prev, outb=outb: e.tensor_add(
                    outb[:, :], ap_of(a), ap_of(bb)))
                if cur[1] is not outb:
                    self.free.append(cur[1])
                cur = ("buf", outb)
                lvl += 1
            self.occ[lvl] = cur
            return ops

        def root(self):
            assert len(self.occ) == 1
            return self.occ.popitem()[1][1]

    for lh in range(NHL):
        b = lh % 2
        waits = [(sSp, P1_SPILLS), (sBh, BH_DONE)]
        if lh >= 2:
            waits.append((sPV, head_pv_done[lh - 2]))
        p.op("sync", lambda e, b=b, lh=lh: e.dma_start(
            out=kT_sb[b][:, :], in_=kTd[lh * 128:(lh + 1) * 128, :]),
            waits=waits, inc=sLd, inc_amt=16)
        p.op("sync", lambda e, b=b, lh=lh: e.dma_start(
            out=qT_sb[b][:, :], in_=qTd[lh * 128:(lh + 1) * 128, :]),
            inc=sLd, inc_amt=16)
        p.op("sync", lambda e, b=b, lh=lh: e.dma_start(
            out=v_sb[b][:, :, :],
            in_=vd[:, lh * 128:(lh + 1) * 128].rearrange("(j p) d -> p j d", p=128)),
            inc=sLd, inc_amt=16)
        ld_val = sLd.n

        for qp in range(NQT // 2):
            qA, qB = 2 * qp, 2 * qp + 1
            ctx_free_val = sEp.n
            treeA = Tree(tree_bufs[0:8])
            treeB = Tree(tree_bufs[8:16])
            exp_vals = {}

            def emit_pv(kc, ctx_free_val=None):
                par = kc % 3
                ptb = pt_sb[par]
                w2 = [(sEX, exp_vals[kc])]
                if kc == 0:
                    w2.append((sEp, ctx_free_val))
                p.op("tensor",
                     lambda e, b=b, kc=kc, ptb=ptb: e.matmul(
                         ps_ctx2[:, 0:TQ], v_sb[b][:, kc, :], ptb[:, 0:TQ],
                         start=(kc == 0), stop=(kc == KCH - 1)),
                     waits=w2, inc=None)
                p.op("tensor",
                     lambda e, b=b, kc=kc, ptb=ptb: e.matmul(
                         ps_ctx2[:, TQ:2 * TQ], v_sb[b][:, kc, :], ptb[:, TQ:2 * TQ],
                         start=(kc == 0), stop=(kc == KCH - 1)),
                     waits=(), inc=sPV)
                pt_free_pv[par] = sPV.n

            for kc in range(KCH):
                par = kc % 3
                stps = ps_st[par]
                ptb = pt_sb[par]
                waits = []
                if kc == 0 and qp == 0:
                    waits.append((sLd, ld_val))
                    if lh == 0:
                        waits.append((sBh, BH_DONE))
                if kc == 0:
                    waits.append((sEp, ctx_free_val))
                if st_free[par] > 0:
                    waits.append((sEX, st_free[par]))
                p.op("tensor",
                     lambda e, b=b, kc=kc, qA=qA, stps=stps: e.matmul(
                         stps[:, 0:TQ],
                         kT_sb[b][:, kc * 128:(kc + 1) * 128],
                         qT_sb[b][:, qA * TQ:(qA + 1) * TQ],
                         start=True, stop=True),
                     waits=waits, inc=None)
                p.op("tensor",
                     lambda e, b=b, kc=kc, qB=qB, stps=stps: e.matmul(
                         stps[:, TQ:2 * TQ],
                         kT_sb[b][:, kc * 128:(kc + 1) * 128],
                         qT_sb[b][:, qB * TQ:(qB + 1) * TQ],
                         start=True, stop=True),
                     waits=(), inc=sST)
                st_val = sST.n
                if kc >= 1:
                    emit_pv(kc - 1, ctx_free_val)

                waits = [(sST, st_val)]
                if pt_free_pv[par] > 0:
                    waits.append((sPV, pt_free_pv[par]))
                    waits.append((sTR, pt_free_trA[par]))
                    waits.append((sTRb, pt_free_trB[par]))
                p.op("scalar",
                     lambda e, ptb=ptb, stps=stps, lh=lh: e.activation(
                         ptb[:, :], stps[:, :], AF.Exp,
                         bias=bias_h[:, lh:lh + 1], scale=SCALE),
                     waits=waits, inc=sEX)
                exp_vals[kc] = sEX.n
                st_free[par] = sEX.n

                if kc % 2 == 1:
                    par0 = (kc - 1) % 3
                    opsA = treeA.insert_pair_ops(("pt", par0, 0), ("pt", par, 0))
                    for oi, fn_ in enumerate(opsA):
                        p.op("vector", fn_,
                             waits=[(sEX, exp_vals[kc])] if oi == 0 else (),
                             inc=sTR if oi == len(opsA) - 1 else None)
                    pt_free_trA[par0] = pt_free_trA[par] = sTR.n
                    opsB = treeB.insert_pair_ops(("pt", par0, 1), ("pt", par, 1))
                    for oi, fn_ in enumerate(opsB):
                        p.op("vector", fn_,
                             waits=(),
                             inc=sTRb if oi == len(opsB) - 1 else None)
                    pt_free_trB[par0] = pt_free_trB[par] = sTRb.n
            emit_pv(KCH - 1)
            pv_all = sPV.n

            rootA, rootB = treeA.root(), treeB.root()
            trA_val, trB_val = sTR.n, sTRb.n
            p.op("tensor",
                 lambda e, rootA=rootA: e.matmul(ps_st[0][0:1, 0:TQ], onesbf[:, :],
                                                 rootA[:, :], start=True, stop=True),
                 waits=[(sTR, trA_val), (sEX, exp_vals[KCH - 1]), (sPv, CONST_V)],
                 inc=None)
            p.op("tensor",
                 lambda e, rootB=rootB: e.matmul(ps_st[1][0:1, 0:TQ], onesbf[:, :],
                                                 rootB[:, :], start=True, stop=True),
                 waits=[(sTRb, trB_val)], inc=sNm)
            mm_val = sNm.n
            p.op("vector",
                 lambda e: e.reciprocal(rrow[0:1, 0:TQ], ps_st[0][0:1, 0:TQ]),
                 waits=[(sNm, mm_val)], inc=None)
            p.op("vector",
                 lambda e: e.reciprocal(rrow[0:1, TQ:2 * TQ], ps_st[1][0:1, 0:TQ]),
                 waits=(), inc=sEp)
            rec_val = sEp.n
            p.op("tensor",
                 lambda e: e.matmul(ps_st[0][:, 0:TQ], ones32r[:, :], rrow[0:1, 0:TQ],
                                    start=True, stop=True),
                 waits=[(sEp, rec_val)], inc=None)
            p.op("tensor",
                 lambda e: e.matmul(ps_st[1][:, 0:TQ], ones32r[:, :], rrow[0:1, TQ:2 * TQ],
                                    start=True, stop=True),
                 waits=(), inc=sNm)
            bc_val = sNm.n
            p.op("vector",
                 lambda e: e.tensor_copy(rb_sb[:, 0:TQ], ps_st[0][:, 0:TQ]),
                 waits=[(sNm, bc_val)], inc=None)
            p.op("vector",
                 lambda e: e.tensor_copy(rb_sb[:, TQ:2 * TQ], ps_st[1][:, 0:TQ]),
                 waits=(), inc=sEp)
            rb_val = sEp.n
            waits = [(sPV, pv_all), (sEp, rb_val)]
            if ctxn_dma_val[0] > 0:
                waits.append((sAG, max(ctxn_dma_val[0], ctxn_dma_val[1])))
            p.op("vector",
                 lambda e: e.tensor_tensor(ctxn[0][:, :], ps_ctx2[:, 0:TQ],
                                           rb_sb[:, 0:TQ], op=ALU.mult),
                 waits=waits, inc=sEp)
            evA = sEp.n
            p.op("vector",
                 lambda e: e.tensor_tensor(ctxn[1][:, :], ps_ctx2[:, TQ:2 * TQ],
                                           rb_sb[:, TQ:2 * TQ], op=ALU.mult),
                 waits=(), inc=sEp)
            evB = sEp.n
            p.op("sync",
                 lambda e, lh=lh, qA=qA: e.dma_start(
                     out=ag_in[lh * 128:(lh + 1) * 128, qA * TQ:(qA + 1) * TQ],
                     in_=ctxn[0][:, :]),
                 waits=[(sEp, evA)], inc=sAG, inc_amt=16)
            ctxn_dma_val[0] = sAG.n
            p.op("sync",
                 lambda e, lh=lh, qB=qB: e.dma_start(
                     out=ag_in[lh * 128:(lh + 1) * 128, qB * TQ:(qB + 1) * TQ],
                     in_=ctxn[1][:, :]),
                 waits=[(sEp, evB)], inc=sAG, inc_amt=16)
            ctxn_dma_val[1] = sAG.n
        head_pv_done[lh] = sPV.n
        ag_head_val = sAG.n
        p.op("gpsimd",
             lambda e, lh=lh: e.collective_compute(
                 "AllGather", ALU.bypass, replica_groups=[list(range(NCORES))],
                 ins=[ag_in[lh * 128:(lh + 1) * 128, :].opt()],
                 outs=[ag_out[lh * 1024:(lh + 1) * 1024, :].opt()]),
             waits=[(sAG, ag_head_val)], inc=sCC, inc_amt=1)

    AG_ALL = sAG.n

    ph2.close()
    ph3 = contextlib.ExitStack()
    y1_16 = ph3.enter_context(nc.sbuf_tensor("y1_16", [128, OCH, TLOC], F16))
    idx_sb = ph3.enter_context(nc.sbuf_tensor("idx_sb", [128, OCH], I32))
    gstg = [ph3.enter_context(nc.sbuf_tensor(f"gstg{i}", [128, TLOC], F16)) for i in range(2)]
    p.op("sync", lambda e: e.dma_start(out=idx_sb[:, :], in_=gidx[:, :]),
         waits=[(sAG, AG_ALL)], inc=sIdx, inc_amt=16)
    gstg_out_val = [0, 0]
    for c in range(OCH):
        waits = [(sIdx, 16), (sCC, NHL)] if c == 0 else []
        if gstg_out_val[c % 2] > 0:
            waits.append((sGa, gstg_out_val[c % 2]))
        p.op("gpsimd",
             lambda e, c=c: e.indirect_dma_start(
                 gstg[c % 2][:, :], None, ag_out_flat[0:128, :],
                 IndirectOffsetOnAxis(idx_sb[:, c:c + 1], axis=0)),
             waits=waits, inc=sGa, inc_amt=16)
        gv = sGa.n
        p.op("gpsimd",
             lambda e, c=c: e.dma_start(
                 out=ctx_mine_d[c * 128:(c + 1) * 128, :], in_=gstg[c % 2][:, :]),
             waits=[(sGa, gv)], inc=sGa, inc_amt=16)
        gstg_out_val[c % 2] = sGa.n
    GATHER_ALL = sGa.n

    pA = contextlib.ExitStack()
    ctxp = pA.enter_context(nc.sbuf_tensor("ctxp", [128, OCH, SUB], F16))
    owp = [pA.enter_context(nc.sbuf_tensor(f"owp{i}", [128, ICH, 128], F16)) for i in range(2)]
    xtp = [pA.enter_context(nc.sbuf_tensor(f"xtp{i}", [128, SUB], F16)) for i in range(4)]
    ysb = pA.enter_context(nc.sbuf_tensor("ysb", [128, OCH, SUB], F32))
    y16t = pA.enter_context(nc.sbuf_tensor("y16t", [128, SUB], F16))
    sq16t = pA.enter_context(nc.sbuf_tensor("sq16t", [128, SUB], F16))
    obt_sb = pA.enter_context(nc.sbuf_tensor("obt_sb", [128, OCH], F32))
    mrow = pA.enter_context(nc.sbuf_tensor("mrow", [1, 2 * SUB], F32))
    vrow = pA.enter_context(nc.sbuf_tensor("vrow", [1, 2 * SUB], F32))
    mb_sb = pA.enter_context(nc.sbuf_tensor("mb_sb", [128, SUB], F32))
    rstdb = pA.enter_context(nc.sbuf_tensor("rstdb", [128, SUB], F32))
    tnorm = [pA.enter_context(nc.sbuf_tensor(f"tnorm{i}", [128, SUB], F32)) for i in range(3)]
    ps_op = pA.enter_context(nc.psum_tensor("ps_op", [128, 4 * SUB], F32))
    ps_st1 = pA.enter_context(nc.psum_tensor("ps_st1", [128, 2 * SUB], F32))
    ps_bc = pA.enter_context(nc.psum_tensor("ps_bc", [128, SUB], F32))

    p.op("sync", lambda e: e.dma_start(out=obt_sb[:, :], in_=obt[:, :]),
         waits=[(sAG, AG_ALL)], inc=sCst, inc_amt=16)
    OBT_LD = sCst.n

    ow_war = {}
    xt_war = {}
    evac_war = {}
    tnorm_war = [0, 0, 0]
    ga_war = 0
    for sub in range(NSUB):
        waits = [(sGa, GATHER_ALL)]
        if sub > 0:
            waits.append((sPm, ga_war))
        p.op("sync",
             lambda e, sub=sub: e.dma_start(
                 out=ctxp[:, :, :],
                 in_=ctx_mine_d[:, sub * SUB:(sub + 1) * SUB]
                 .rearrange("(c p) t -> p c t", p=128)),
             waits=waits, inc=sGa, inc_amt=16)
        ga_val = sGa.n

        for c in range(OCH):
            ci = sub * OCH + c
            ob_ = owp[ci % 2]
            waits = []
            if ci >= 2:
                waits.append((sPm, ow_war[ci - 2]))
            p.op("gpsimd",
                 lambda e, ob_=ob_, c=c: e.dma_start(
                     out=ob_[:, :, :],
                     in_=ow[:, c * 128:(c + 1) * 128].rearrange("(i p) c -> p i c", p=128)),
                 waits=waits, inc=sOwl, inc_amt=16)
            owl_val = sOwl.n
            xb = xtp[ci % 4]
            waits = [(sCastM, 16)]
            if ci >= 4:
                waits.append((sPv, xt_war[ci - 4]))
            p.op("sync",
                 lambda e, xb=xb, c=c, sub=sub: e.dma_start(
                     out=xb[:, :],
                     in_=x16md[sub * SUB:(sub + 1) * SUB, c * 128:(c + 1) * 128],
                     transpose=True),
                 waits=waits, inc=sXtp, inc_amt=16)
            xtp_val = sXtp.n
            bank = ci % 4
            waits = [(sOwl, owl_val), (sGa, ga_val)]
            if ci >= 4:
                waits.append((sPe, evac_war[ci - 4]))
            for i in range(ICH):
                p.op("tensor",
                     lambda e, ob_=ob_, i=i, bank=bank: e.matmul(
                         ps_op[:, bank * SUB:(bank + 1) * SUB],
                         ob_[:, i, :], ctxp[:, i, :],
                         start=(i == 0), stop=(i == ICH - 1)),
                     waits=waits if i == 0 else (),
                     inc=sPm if i == ICH - 1 else None)
            ow_war[ci] = sPm.n
            mm_val = sPm.n
            tn = tnorm[ci % 3]
            waits = [(sPm, mm_val), (sCst, OBT_LD)]
            if tnorm_war[ci % 3] > 0:
                waits.append((sPv, tnorm_war[ci % 3]))
            p.op("scalar",
                 lambda e, tn=tn, bank=bank, c=c: e.activation(
                     tn[:, :], ps_op[:, bank * SUB:(bank + 1) * SUB],
                     AF.Identity, bias=obt_sb[:, c:c + 1], scale=1.0),
                 waits=waits, inc=sPe)
            evac_war[ci] = sPe.n
            p.op("vector",
                 lambda e, c=c, xb=xb, tn=tn: e.tensor_add(ysb[:, c, :], tn[:, :], xb[:, :]),
                 waits=[(sPe, sPe.n), (sXtp, xtp_val)], inc=sPv)
            tnorm_war[ci % 3] = sPv.n
            xt_war[ci] = sPv.n
            p.op("vector",
                 lambda e, c=c: e.tensor_copy(y16t[:, :], ysb[:, c, :]),
                 waits=(), inc=sPv)
            p.op("vector",
                 lambda e: e.tensor_mul(sq16t[:, :], y16t[:, :], y16t[:, :]),
                 waits=(), inc=sPv)
            sqv = sPv.n
            p.op("tensor",
                 lambda e, c=c: e.matmul(ps_st1[0:1, 0:SUB], ones16[:, :], y16t[:, :],
                                         start=(c == 0), stop=(c == OCH - 1)),
                 waits=[(sPv, sqv)],
                 inc=None)
            p.op("tensor",
                 lambda e, c=c: e.matmul(ps_st1[0:1, SUB:2 * SUB], ones16[:, :], sq16t[:, :],
                                         start=(c == 0), stop=(c == OCH - 1)),
                 waits=(),
                 inc=sPm if c == OCH - 1 else None)
        ga_war = sPm.n
        st_done = sPm.n
        p.op("vector",
             lambda e: e.tensor_single_scalar(mrow[0:1, 0:SUB], ps_st1[0:1, 0:SUB],
                                              1.0 / D, ALU.mult),
             waits=[(sPm, st_done)], inc=sPv)
        p.op("vector",
             lambda e: e.tensor_single_scalar(mrow[0:1, SUB:2 * SUB], ps_st1[0:1, SUB:2 * SUB],
                                              1.0 / D, ALU.mult),
             waits=(), inc=sPv)
        p.op("vector",
             lambda e: e.tensor_mul(vrow[0:1, 0:SUB], mrow[0:1, 0:SUB], mrow[0:1, 0:SUB]),
             waits=(), inc=sPv)
        p.op("vector",
             lambda e: e.tensor_sub(vrow[0:1, 0:SUB], mrow[0:1, SUB:2 * SUB],
                                    vrow[0:1, 0:SUB]),
             waits=(), inc=sPv)
        p.op("vector",
             lambda e: e.tensor_single_scalar(vrow[0:1, 0:SUB], vrow[0:1, 0:SUB],
                                              EPS, ALU.add),
             waits=(), inc=sPv)
        varv = sPv.n
        p.op("scalar",
             lambda e: e.activation(vrow[0:1, SUB:2 * SUB], vrow[0:1, 0:SUB], AF.Ln),
             waits=[(sPv, varv)], inc=sPs)
        p.op("scalar",
             lambda e: e.activation(vrow[0:1, 0:SUB], vrow[0:1, SUB:2 * SUB], AF.Exp,
                                    scale=-0.5),
             waits=(), inc=None)
        p.op("scalar",
             lambda e: e.activation(vrow[0:1, SUB:2 * SUB], vrow[0:1, 0:SUB], AF.Identity),
             waits=(), inc=sPs)
        rstd_val = sPs.n
        p.op("tensor",
             lambda e: e.matmul(ps_bc[:, :], ones32r[:, :], mrow[0:1, 0:SUB],
                                start=True, stop=True),
             waits=[(sPs, rstd_val)], inc=sPm)
        p.op("vector",
             lambda e: e.tensor_copy(mb_sb[:, :], ps_bc[:, :]),
             waits=[(sPm, sPm.n)], inc=sPv)
        p.op("tensor",
             lambda e: e.matmul(ps_bc[:, :], ones32r[:, :], vrow[0:1, 0:SUB],
                                start=True, stop=True),
             waits=[(sPv, sPv.n)], inc=sPm)
        p.op("vector",
             lambda e: e.tensor_copy(rstdb[:, :], ps_bc[:, :]),
             waits=[(sPm, sPm.n)], inc=sPv)
        bc_done = sPv.n
        for c in range(OCH):
            tn = tnorm[c % 3]
            waits = [(sPv, bc_done)]
            p.op("vector",
                 lambda e, c=c, tn=tn: e.tensor_sub(tn[:, :], ysb[:, c, :], mb_sb[:, :]),
                 waits=waits, inc=None)
            p.op("vector",
                 lambda e, c=c, sub=sub, tn=tn: e.tensor_mul(
                     y1_16[:, c, sub * SUB:(sub + 1) * SUB], tn[:, :], rstdb[:, :]),
                 waits=(), inc=sPv)
        tnorm_war = [sPv.n] * 3
    Y1_DONE = sPv.n
    pA.close()

    pBs = contextlib.ExitStack()
    bm16 = pBs.enter_context(nc.sbuf_tensor("bm16", [32, D], F16))
    pr16 = pBs.enter_context(nc.sbuf_tensor("pr16", [32, NSUB, SUB], F16))
    pB = contextlib.ExitStack()
    a16 = pB.enter_context(nc.sbuf_tensor("a16", [32, D], F16))
    aT16 = pB.enter_context(nc.sbuf_tensor("aT16", [128, ICH, 32], F16))
    wr16 = pB.enter_context(nc.sbuf_tensor("wr16", [4, D], F16))
    wrT16 = pB.enter_context(nc.sbuf_tensor("wrT16", [128, ICH, 4], F16))
    sel16 = pB.enter_context(nc.sbuf_tensor("sel16", [4, 32], F16))
    gl32 = pB.enter_context(nc.sbuf_tensor("gl32", [4, SUB], F32))
    ge32 = pB.enter_context(nc.sbuf_tensor("ge32", [4, SUB], F32))
    rb4 = pB.enter_context(nc.sbuf_tensor("rb4", [4, SUB], F32))
    grow2 = pB.enter_context(nc.sbuf_tensor("grow2", [1, SUB], F32))
    ones4 = pB.enter_context(nc.sbuf_tensor("ones4", [4, 1], F32))
    gem16 = pB.enter_context(nc.sbuf_tensor("gem16", [4, SUB], F16))
    grep_sb = pB.enter_context(nc.sbuf_tensor("grep_sb", [32, SUB], F32))
    ps_g = pB.enter_context(nc.psum_tensor("ps_g", [128, 512], F32))
    ps_g2 = pB.enter_context(nc.psum_tensor("ps_g2", [128, 512], F32))
    ps_tA = pB.enter_context(nc.psum_tensor("ps_tA", [128, 64], F16))

    p.op("vector", lambda e: e.memset(ones4[:, :], 1.0),
         waits=[(sPv, Y1_DONE)], inc=sPv)
    CONST_V = sPv.n
    p.op("gpsimd", lambda e: e.dma_start(out=a16[:, :], in_=A[:, :]),
         waits=[(sPv, Y1_DONE)], inc=sGa, inc_amt=16)
    p.op("gpsimd", lambda e: e.dma_start(out=wr16[:, :], in_=Wr[:, :]), inc=sGa, inc_amt=16)
    p.op("gpsimd", lambda e: e.dma_start(out=bm16[:, :], in_=BmT[:, :]), inc=sGa, inc_amt=16)
    p.op("sync", lambda e: e.dma_start(out=sel16[:, :], in_=sel[:, :]),
         waits=[(sPv, Y1_DONE)], inc=sCst, inc_amt=16)
    SMALL_G = sGa.n
    SMALL_C = sCst.n
    for i in range(ICH):
        p.op("tensor",
             lambda e, i=i: e.transpose(ps_tA[:, 0:32],
                                        a16[:, i * 128:(i + 1) * 128], ident16[0:32, 0:32]),
             waits=[(sGa, SMALL_G), (sCst, CONST_LD), (sPv, sPv.n)], inc=sPm)
        p.op("vector",
             lambda e, i=i: e.tensor_copy(aT16[:, i, :], ps_tA[:, 0:32]),
             waits=[(sPm, sPm.n)], inc=sPv)
        p.op("tensor",
             lambda e, i=i: e.transpose(ps_tA[:, 32:36],
                                        wr16[:, i * 128:(i + 1) * 128], ident16[0:4, 0:4]),
             waits=[(sPv, sPv.n)], inc=sPm)
        p.op("vector",
             lambda e, i=i: e.tensor_copy(wrT16[:, i, :], ps_tA[:, 32:36]),
             waits=[(sPm, sPm.n)], inc=sPv)
    TRS_V = sPv.n

    for sub in range(NSUB):
        for i in range(ICH):
            p.op("tensor",
                 lambda e, i=i, sub=sub: e.matmul(
                     ps_g[0:4, 0:SUB], wrT16[:, i, :],
                     y1_16[:, i, sub * SUB:(sub + 1) * SUB],
                     start=(i == 0), stop=(i == ICH - 1)),
                 waits=[(sPv, TRS_V)] if i == 0 else (),
                 inc=sPm if i == ICH - 1 else None)
        p.op("vector",
             lambda e: e.tensor_copy(gl32[:, :], ps_g[0:4, 0:SUB]),
             waits=[(sPm, sPm.n)], inc=sPv)
        glv = sPv.n
        p.op("scalar",
             lambda e: e.activation(ge32[:, :], gl32[:, :], AF.Exp),
             waits=[(sPv, glv)], inc=None)
        p.op("scalar",
             lambda e: e.activation(rb4[:, :], ge32[:, :], AF.Identity),
             waits=(), inc=sPs)
        exp_drain = sPs.n
        p.op("tensor",
             lambda e: e.matmul(ps_g2[0:1, 0:SUB], ones4[:, :], ge32[:, :],
                                start=True, stop=True),
             waits=[(sPs, exp_drain), (sPv, CONST_V)], inc=sPm)
        p.op("vector",
             lambda e: e.reciprocal(grow2[0:1, :], ps_g2[0:1, 0:SUB]),
             waits=[(sPm, sPm.n)], inc=sPv)
        p.op("tensor",
             lambda e: e.matmul(ps_g2[0:4, 0:SUB], ones32r[0:1, 0:4], grow2[0:1, :],
                                start=True, stop=True),
             waits=[(sPv, sPv.n)], inc=sPm)
        p.op("vector",
             lambda e: e.tensor_copy(rb4[:, :], ps_g2[0:4, 0:SUB]),
             waits=[(sPm, sPm.n)], inc=sPv)
        p.op("vector",
             lambda e: e.tensor_tensor(gem16[:, :], ge32[:, :], rb4[:, :], op=ALU.mult),
             waits=(), inc=sPv)
        gem_done = sPv.n
        p.op("tensor",
             lambda e: e.matmul(ps_g[0:32, 0:SUB], sel16[:, :], gem16[:, :],
                                start=True, stop=True),
             waits=[(sPv, gem_done), (sCst, SMALL_C)], inc=sPm)
        p.op("vector",
             lambda e: e.tensor_copy(grep_sb[:, :], ps_g[0:32, 0:SUB]),
             waits=[(sPm, sPm.n)], inc=sPv)
        for i in range(ICH):
            p.op("tensor",
                 lambda e, i=i, sub=sub: e.matmul(
                     ps_g2[0:32, 0:SUB], aT16[:, i, :],
                     y1_16[:, i, sub * SUB:(sub + 1) * SUB],
                     start=(i == 0), stop=(i == ICH - 1)),
                 waits=[(sPv, sPv.n)] if i == 0 else (),
                 inc=sPm if i == ICH - 1 else None)
        p.op("vector",
             lambda e, sub=sub: e.tensor_tensor(pr16[:, sub, :], ps_g2[0:32, 0:SUB],
                                                grep_sb[:, :], op=ALU.mult),
             waits=[(sPm, sPm.n)], inc=sPv)
    PR_DONE = sPv.n
    pB.close()

    pC = contextlib.ExitStack()
    y2_16 = pC.enter_context(nc.sbuf_tensor("y2_16", [128, OCH, TLOC], F16, side="right"))
    ps_st2 = pC.enter_context(nc.psum_tensor("ps_st2", [128, 2 * NSUB * SUB], F32, side="right"))
    pW = contextlib.ExitStack()
    OSUP = 256
    wtp = [pW.enter_context(nc.sbuf_tensor(f"wtp{i}", [128, ICH, OSUP], F16)) for i in range(3)]
    ftmp = [pW.enter_context(nc.sbuf_tensor(f"ftmp{i}", [128, SUB], F32)) for i in range(3)]
    fsq = pW.enter_context(nc.sbuf_tensor("fsq", [128, SUB], F16))
    bt_sb = pW.enter_context(nc.sbuf_tensor("bt_sb", [128, OCH], F32))
    ps_f = pW.enter_context(nc.psum_tensor("ps_f", [128, 3 * SUB], F32))

    p.op("sync", lambda e: e.dma_start(out=bt_sb[:, :], in_=bt[:, :]),
         waits=[(sPv, PR_DONE)], inc=sCst, inc_amt=16)
    BT_LD = sCst.n

    NOS = D // OSUP
    OCPS = OSUP // 128
    wt_war = {}
    f_evac_war = {}
    ftmp_war = [0, 0, 0]
    for osup in range(NOS):
        wb = wtp[osup % 3]
        waits = [(sCastW, 16 * 8)]
        if osup == 0:
            waits.append((sPv, PR_DONE))
        if osup >= 3:
            waits.append((sPm, wt_war[osup - 3]))
        for i in range(ICH):
            p.op("sync",
                 lambda e, wb=wb, osup=osup, i=i: e.dma_start(
                     out=wb[:, i, :],
                     in_=w16d[osup * OSUP:(osup + 1) * OSUP, i * 128:(i + 1) * 128],
                     transpose=True),
                 waits=waits if i == 0 else (), inc=sWtl, inc_amt=16)
        wt_val = sWtl.n
        for sub in range(NSUB):
            for oc in range(OCPS):
                og = osup * OCPS + oc
                fi = (osup * NSUB + sub) * OCPS + oc
                bank = fi % 3
                waits = [(sWtl, wt_val), (sPv, PR_DONE)]
                if fi >= 3:
                    waits.append((sPe, f_evac_war[fi - 3]))
                for i in range(ICH):
                    p.op("tensor",
                         lambda e, wb=wb, i=i, oc=oc, sub=sub, bank=bank: e.matmul(
                             ps_f[:, bank * SUB:(bank + 1) * SUB],
                             wb[:, i, oc * 128:(oc + 1) * 128],
                             y1_16[:, i, sub * SUB:(sub + 1) * SUB],
                             start=(i == 0), stop=False),
                         waits=waits if i == 0 else (), inc=None)
                p.op("tensor",
                     lambda e, og=og, sub=sub, bank=bank: e.matmul(
                         ps_f[:, bank * SUB:(bank + 1) * SUB],
                         bm16[:, og * 128:(og + 1) * 128],
                         pr16[:, sub, :],
                         start=False, stop=True),
                     waits=(), inc=sPm)
                mm_val = sPm.n
                ft = ftmp[fi % 3]
                waits = [(sPm, mm_val), (sCst, BT_LD)]
                if ftmp_war[fi % 3] > 0:
                    waits.append((sPv, ftmp_war[fi % 3]))
                p.op("scalar",
                     lambda e, ft=ft, bank=bank, og=og: e.activation(
                         ft[:, :], ps_f[:, bank * SUB:(bank + 1) * SUB],
                         AF.Identity, bias=bt_sb[:, og:og + 1], scale=1.0),
                     waits=waits, inc=sPe)
                f_evac_war[fi] = sPe.n
                p.op("vector",
                     lambda e, og=og, sub=sub, ft=ft: e.tensor_add(
                         y2_16[:, og, sub * SUB:(sub + 1) * SUB],
                         ft[:, :],
                         y1_16[:, og, sub * SUB:(sub + 1) * SUB]),
                     waits=[(sPe, sPe.n)], inc=sPv)
                ftmp_war[fi % 3] = sPv.n
                p.op("vector",
                     lambda e, og=og, sub=sub: e.tensor_mul(
                         fsq[:, :],
                         y2_16[:, og, sub * SUB:(sub + 1) * SUB],
                         y2_16[:, og, sub * SUB:(sub + 1) * SUB]),
                     waits=(), inc=sPv)
                sq_val = sPv.n
                first = (osup == 0 and oc == 0)
                last = (osup == NOS - 1 and oc == OCPS - 1)
                p.op("tensor",
                     lambda e, og=og, sub=sub, first=first, last=last: e.matmul(
                         ps_st2[0:1, (2 * sub) * SUB:(2 * sub + 1) * SUB], ones16[:, :],
                         y2_16[:, og, sub * SUB:(sub + 1) * SUB],
                         start=first, stop=last),
                     waits=[(sPv, sq_val)], inc=None)
                p.op("tensor",
                     lambda e, sub=sub, first=first, last=last: e.matmul(
                         ps_st2[0:1, (2 * sub + 1) * SUB:(2 * sub + 2) * SUB],
                         ones16[:, :], fsq[:, :],
                         start=first, stop=last),
                     waits=(), inc=sPm if last else None)
        wt_war[osup] = sPm.n
    FFN_DONE = sPm.n
    pW.close()
    pBs.close()
    ph3.close()

    pD = contextlib.ExitStack()
    m2row = pD.enter_context(nc.sbuf_tensor("m2row", [1, 2 * SUB], F32))
    v2row = pD.enter_context(nc.sbuf_tensor("v2row", [1, 2 * SUB], F32))
    mtok = pD.enter_context(nc.sbuf_tensor("mtok", [128, 2], F32))
    y2tok = pD.enter_context(nc.sbuf_tensor("y2tok", [128, OCH * 128], F16))
    fin32 = [pD.enter_context(nc.sbuf_tensor(f"fin{i}", [128, D], F32)) for i in range(2)]
    ps_t = pD.enter_context(nc.psum_tensor("ps_t", [128, 1024], F32))
    ps_t16 = ps_t.ap().bitcast(F16)

    fin_dma_val = [0, 0]
    for sub in range(NSUB):
        p.op("vector",
             lambda e, sub=sub: e.tensor_single_scalar(
                 m2row[0:1, 0:SUB], ps_st2[0:1, (2 * sub) * SUB:(2 * sub + 1) * SUB],
                 1.0 / D, ALU.mult),
             waits=[(sPm, FFN_DONE)], inc=sPv)
        p.op("vector",
             lambda e, sub=sub: e.tensor_single_scalar(
                 m2row[0:1, SUB:2 * SUB],
                 ps_st2[0:1, (2 * sub + 1) * SUB:(2 * sub + 2) * SUB],
                 1.0 / D, ALU.mult),
             waits=(), inc=sPv)
        p.op("vector",
             lambda e: e.tensor_mul(v2row[0:1, 0:SUB], m2row[0:1, 0:SUB],
                                    m2row[0:1, 0:SUB]),
             waits=(), inc=sPv)
        p.op("vector",
             lambda e: e.tensor_sub(v2row[0:1, 0:SUB], m2row[0:1, SUB:2 * SUB],
                                    v2row[0:1, 0:SUB]),
             waits=(), inc=sPv)
        p.op("vector",
             lambda e: e.tensor_single_scalar(v2row[0:1, 0:SUB], v2row[0:1, 0:SUB],
                                              EPS, ALU.add),
             waits=(), inc=sPv)
        varv = sPv.n
        p.op("scalar",
             lambda e: e.activation(v2row[0:1, SUB:2 * SUB], v2row[0:1, 0:SUB], AF.Ln),
             waits=[(sPv, varv)], inc=sPs)
        p.op("scalar",
             lambda e: e.activation(v2row[0:1, 0:SUB], v2row[0:1, SUB:2 * SUB], AF.Exp,
                                    scale=-0.5),
             waits=(), inc=None)
        p.op("scalar",
             lambda e: e.activation(v2row[0:1, SUB:2 * SUB], v2row[0:1, 0:SUB], AF.Identity),
             waits=(), inc=sPs)
        row_val = sPs.n
        for j in range(SUB // 128):
            p.op("tensor",
                 lambda e, j=j: e.transpose(ps_t.ap()[:, 0:1],
                                            m2row[0:1, j * 128:(j + 1) * 128],
                                            ident32[0:1, 0:1]),
                 waits=[(sPs, row_val), (sPv, sPv.n)], inc=sPm)
            p.op("vector",
                 lambda e: e.tensor_copy(mtok[:, 0:1], ps_t.ap()[:, 0:1]),
                 waits=[(sPm, sPm.n)], inc=sPv)
            p.op("tensor",
                 lambda e, j=j: e.transpose(ps_t.ap()[:, 0:1],
                                            v2row[0:1, j * 128:(j + 1) * 128],
                                            ident32[0:1, 0:1]),
                 waits=[(sPv, sPv.n)], inc=sPm)
            p.op("vector",
                 lambda e: e.tensor_copy(mtok[:, 1:2], ps_t.ap()[:, 0:1]),
                 waits=[(sPm, sPm.n)], inc=sPv)
            tr_copy_val = [sPv.n, sPv.n]
            for oc in range(OCH):
                par = oc % 2
                reg = ps_t16[:, 0:128] if par == 0 else ps_t16[:, 1024:1152]
                p.op("tensor",
                     lambda e, oc=oc, sub=sub, j=j, reg=reg: e.transpose(
                         reg,
                         y2_16[:, oc, sub * SUB + j * 128: sub * SUB + (j + 1) * 128],
                         ident16[:, :]),
                     waits=[(sPv, tr_copy_val[par])], inc=sPm)
                p.op("vector",
                     lambda e, oc=oc, reg=reg: e.tensor_copy(
                         y2tok[:, oc * 128:(oc + 1) * 128], reg),
                     waits=[(sPm, sPm.n)], inc=sPv)
                tr_copy_val[par] = sPv.n
            fb = fin32[j % 2]
            waits = [(sPv, sPv.n)]
            if fin_dma_val[j % 2] > 0:
                waits.append((sOut, fin_dma_val[j % 2]))
            p.op("vector",
                 lambda e, fb=fb: e.tensor_scalar(
                     fb[:, :], y2tok[:, :],
                     mtok[:, 0:1], mtok[:, 1:2],
                     op0=ALU.subtract, op1=ALU.mult),
                 waits=waits, inc=sPv)
            p.op("gpsimd",
                 lambda e, fb=fb, sub=sub, j=j: e.dma_start(
                     out=y_out[sub * SUB + j * 128: sub * SUB + (j + 1) * 128, :],
                     in_=fb[:, :]),
                 waits=[(sPv, sPv.n)], inc=sOut, inc_amt=16)
            fin_dma_val[j % 2] = sOut.n

    p.op("gpsimd", lambda e: None, waits=[(sOut, sOut.n)])

    with nc.Block() as block:
        @block.sync
        def _(e):
            for f in p.q["sync"]:
                f(e)

        @block.tensor
        def _(e):
            for f in p.q["tensor"]:
                f(e)

        @block.vector
        def _(e):
            for f in p.q["vector"]:
                f(e)

        @block.scalar
        def _(e):
            for f in p.q["scalar"]:
                f(e)

        @block.gpsimd
        def _(e):
            for f in p.q["gpsimd"]:
                f(e)

    return nc


def kernel(L0_attn_x, L0_attn_attn_qkvw, L0_attn_attn_qkvb, L0_attn_attn_ow,
           L0_attn_attn_ob, L0_ffn_W, L0_ffn_b, L0_ffn_A, L0_ffn_Bm, L0_ffn_Wr):
    import ml_dtypes
    x = np.ascontiguousarray(np.asarray(L0_attn_x, dtype=np.float32).reshape(S, D))
    qkvw = np.asarray(L0_attn_attn_qkvw, dtype=np.float32)
    qkvb = np.asarray(L0_attn_attn_qkvb, dtype=np.float32)
    ow = np.ascontiguousarray(np.asarray(L0_attn_attn_ow, dtype=np.float32))
    ob = np.asarray(L0_attn_attn_ob, dtype=np.float32)
    W = np.ascontiguousarray(np.asarray(L0_ffn_W, dtype=np.float32))
    b = np.asarray(L0_ffn_b, dtype=np.float32)
    A = np.ascontiguousarray(np.asarray(L0_ffn_A, dtype=np.float32).reshape(32, D))
    Bm = np.asarray(L0_ffn_Bm, dtype=np.float32)
    Wr = np.ascontiguousarray(np.asarray(L0_ffn_Wr, dtype=np.float32))

    BmT = np.ascontiguousarray(Bm.transpose(0, 2, 1).reshape(32, D))
    sel = np.zeros((4, 32), dtype=np.float16)
    for e_ in range(4):
        sel[e_, e_ * 8:(e_ + 1) * 8] = 2.0
    id16 = np.eye(128, dtype=np.float16)
    idbf = np.eye(128).astype(ml_dtypes.bfloat16).view(np.uint16)
    id32 = np.eye(128, dtype=np.float32)
    obt = np.ascontiguousarray(ob.reshape(OCH, 128).T)
    btt = np.ascontiguousarray(b.reshape(OCH, 128).T)

    nc = build_nc()

    in_maps = []
    for r in range(NCORES):
        heads = range(NHL * r, NHL * (r + 1))
        cols = []
        bcols = []
        for h in heads:
            for k in range(3):
                cols.append(qkvw[:, k * D + h * 128:k * D + (h + 1) * 128])
                bcols.append(qkvb[k * D + h * 128:k * D + (h + 1) * 128])
        qkvw_r = np.ascontiguousarray(np.concatenate(cols, axis=1))
        qkvb_r = np.ascontiguousarray(np.stack(bcols, axis=1))
        gi = np.zeros((128, OCH), dtype=np.int32)
        for c in range(OCH):
            lh = c % NHL
            rp = c // NHL
            gi[:, c] = (lh * 1024 + rp * 128 + np.arange(128)) * S + r * TLOC
        in_maps.append({
            "x32": x,
            "x32m": np.ascontiguousarray(x[r * TLOC:(r + 1) * TLOC, :]),
            "qkvw": qkvw_r,
            "qkvb": qkvb_r,
            "ow": ow,
            "obt": obt,
            "W": W,
            "bt": btt,
            "A": A,
            "BmT": BmT,
            "Wr": Wr,
            "sel": sel,
            "id16": id16,
            "idbf": idbf,
            "id32": id32,
            "gidx": gi,
        })

    res = run_bass_kernel_spmd(nc, in_maps, core_ids=list(range(NCORES)))
    out = np.concatenate([res.results[r]["y_out"] for r in range(NCORES)], axis=0)
    return out.reshape(1, S, D).astype(np.float32)


# version 17 (final)
# speedup vs baseline: 1.1858x; 1.0087x over previous
import contextlib
import math
import numpy as np

import concourse.bass as bass
import concourse.mybir as mybir
from concourse.bass import IndirectOffsetOnAxis
from concourse.bass_utils import run_bass_kernel_spmd

F32 = mybir.dt.float32
F16 = mybir.dt.float16
BF16 = mybir.dt.bfloat16
I32 = mybir.dt.int32
AF = mybir.ActivationFunctionType
ALU = mybir.AluOpType
AXX = mybir.AxisListType.X

NCORES = 8
S = 8192
D = 4096
HD = 128
NHL = 4
TLOC = S // NCORES
SCALE = 1.0 / math.sqrt(HD)
EPS = 1e-5

CCH = 3 * NHL
ICH = D // 128
TT = 512
NTT = S // TT
TQ = 512
NQT = S // TQ
KCH = S // 128
GS = 3
SUB = 512
NSUB = TLOC // SUB
OCH = D // 128


class Sem:
    def __init__(self, handle, name):
        self.h = handle
        self.name = name
        self.n = 0


class Prog:
    QUEUES = ("sync", "tensor", "vector", "scalar", "gpsimd")

    def __init__(self):
        self.q = {k: [] for k in self.QUEUES}

    def op(self, qn, fn, waits=(), inc=None, inc_amt=1, defer=None):
        for s, v in waits:
            assert v <= s.n, f"wait on {s.name} for {v} but only {s.n} recorded"
        wl = [(s.h, v) for s, v in waits if v > 0]
        if inc is not None:
            inc.n += inc_amt
            ih, ia = inc.h, inc_amt
        else:
            ih, ia = None, 0

        def run(eng, wl=wl, fn=fn, ih=ih, ia=ia):
            for h, v in wl:
                eng.wait_ge(h, v)
            ins = fn(eng)
            if ih is not None:
                if ins is None:
                    ins = eng.nop()
                ins.then_inc(ih, ia)
        if defer is not None:
            defer.append((qn, run))
        else:
            self.q[qn].append(run)

    def flush(self, defer):
        for qn, run in defer:
            self.q[qn].append(run)
        defer.clear()


def build_nc():
    nc = bass.Bass("TRN2", target_bir_lowering=False)

    x32 = nc.dram_tensor("x32", [S, D], F32, kind="ExternalInput").ap()
    x32m = nc.dram_tensor("x32m", [TLOC, D], F32, kind="ExternalInput").ap()
    qkvw = nc.dram_tensor("qkvw", [D, CCH * 128], F32, kind="ExternalInput").ap()
    qkvb = nc.dram_tensor("qkvb", [128, CCH], F32, kind="ExternalInput").ap()
    ow = nc.dram_tensor("ow", [D, D], F32, kind="ExternalInput").ap()
    obt = nc.dram_tensor("obt", [128, OCH], F32, kind="ExternalInput").ap()
    Wt = nc.dram_tensor("W", [D, D], F32, kind="ExternalInput").ap()
    bt = nc.dram_tensor("bt", [128, OCH], F32, kind="ExternalInput").ap()
    A = nc.dram_tensor("A", [32, D], F32, kind="ExternalInput").ap()
    BmT = nc.dram_tensor("BmT", [32, D], F32, kind="ExternalInput").ap()
    Wr = nc.dram_tensor("Wr", [4, D], F32, kind="ExternalInput").ap()
    sel = nc.dram_tensor("sel", [4, 32], F16, kind="ExternalInput").ap()
    id16 = nc.dram_tensor("id16", [128, 128], F16, kind="ExternalInput").ap()
    idbf = nc.dram_tensor("idbf", [128, 128], BF16, kind="ExternalInput").ap()
    id32 = nc.dram_tensor("id32", [128, 128], F32, kind="ExternalInput").ap()
    gidx = nc.dram_tensor("gidx", [128, OCH], I32, kind="ExternalInput").ap()

    y_out = nc.dram_tensor("y_out", [TLOC, D], F32, kind="ExternalOutput").ap()
    x16md = nc.dram_tensor("x16md", [TLOC, D], F16).ap()
    w16d = nc.dram_tensor("w16d", [D, D], F16).ap()
    qTd = nc.dram_tensor("qTd", [NHL * 128, S], F16).ap()
    kTd = nc.dram_tensor("kTd", [NHL * 128, S], F16).ap()
    vd = nc.dram_tensor("vd", [S, NHL * 128], BF16).ap()
    ag_in = nc.dram_tensor("ag_in", [NHL * 128, S], F16).ap()
    ag_out = nc.dram_tensor("ag_out", [NHL * 1024, S], F16, addr_space="Shared").ap()
    ctx_mine_d = nc.dram_tensor("ctx_mine_d", [D, TLOC], F16).ap()
    ag_out_flat = ag_out.rearrange("a b -> (a b)").rearrange("(n one) -> n one", one=1)

    p = Prog()
    stack = contextlib.ExitStack()
    nc._kernel_stack = stack

    def sem(name):
        return Sem(stack.enter_context(nc.semaphore(name)), name)

    sCast = sem("sCast")
    sCastM = sem("sCastM")
    sCastW = sem("sCastW")
    sWld = sem("sWld")
    sXT = sem("sXT")
    sTx = sem("sTx")
    sTxe = sem("sTxe")
    sMM = sem("sMM")
    sEv = sem("sEv")
    sVt = sem("sVt")
    sVe = sem("sVe")
    sSp = sem("sSp")
    sNq = sem("sNq")
    sNm = sem("sNm")
    sBh = sem("sBh")
    sLd = sem("sLd")
    sST = sem("sST")
    sEX = sem("sEX")
    sPV = sem("sPV")
    sTR = sem("sTR")
    sTRb = sem("sTRb")
    sEp = sem("sEp")
    sAG = sem("sAG")
    sCC = sem("sCC")
    sGa = sem("sGa")
    sOwl = sem("sOwl")
    sXtp = sem("sXtp")
    sPm = sem("sPm")
    sPe = sem("sPe")
    sPv = sem("sPv")
    sWtl = sem("sWtl")
    sPs = sem("sPs")
    sOut = sem("sOut")
    sIdx = sem("sIdx")
    sCst = sem("sCst")

    ident16 = stack.enter_context(nc.sbuf_tensor("ident16", [128, 128], F16))
    identbf = stack.enter_context(nc.sbuf_tensor("identbf", [128, 128], BF16))
    ident32 = stack.enter_context(nc.sbuf_tensor("ident32", [128, 128], F32))
    ones16 = stack.enter_context(nc.sbuf_tensor("ones16", [128, 1], F16))
    onesbf = stack.enter_context(nc.sbuf_tensor("onesbf", [128, 1], BF16))
    ones32r = stack.enter_context(nc.sbuf_tensor("ones32r", [1, 128], F32))
    qkvb_sb = stack.enter_context(nc.sbuf_tensor("qkvb_sb", [128, CCH], F32))
    bias_h = stack.enter_context(nc.sbuf_tensor("bias_h", [128, NHL], F32))
    norm_acc = stack.enter_context(nc.sbuf_tensor("norm_acc", [1, 2 * NHL], F32))
    rowtmp = stack.enter_context(nc.sbuf_tensor("rowtmp", [1, 8], F32))

    p.op("sync", lambda e: e.dma_start(out=ident16[:, :], in_=id16[:, :]), inc=sCst, inc_amt=16)
    p.op("sync", lambda e: e.dma_start(out=identbf[:, :], in_=idbf[:, :]), inc=sCst, inc_amt=16)
    p.op("sync", lambda e: e.dma_start(out=ident32[:, :], in_=id32[:, :]), inc=sCst, inc_amt=16)
    p.op("sync", lambda e: e.dma_start(out=qkvb_sb[:, :], in_=qkvb[:, :]), inc=sCst, inc_amt=16)
    p.op("vector", lambda e: e.memset(ones16[:, :], 1.0), inc=sPv)
    p.op("vector", lambda e: e.memset(onesbf[:, :], 1.0), inc=sPv)
    p.op("vector", lambda e: e.memset(ones32r[:, :], 1.0), inc=sPv)
    p.op("vector", lambda e: e.memset(norm_acc[:, :], 0.0), inc=sPv)
    CONST_LD = sCst.n
    CONST_V = sPv.n

    ph1 = contextlib.ExitStack()
    qkvw_sb = ph1.enter_context(nc.sbuf_tensor("qkvw_sb", [128, CCH * ICH, 128], F16))
    xt_sb = [ph1.enter_context(nc.sbuf_tensor(f"xt{i}", [128, ICH, TT], F16)) for i in range(2)]
    stg = [ph1.enter_context(nc.sbuf_tensor(f"stg{i}", [128, TT], F16)) for i in range(3)]
    stgv = [ph1.enter_context(nc.sbuf_tensor(f"stgv{i}", [128, TT], BF16)) for i in range(2)]
    vstg = [ph1.enter_context(nc.sbuf_tensor(f"vstg{i}", [128, TT], BF16)) for i in range(2)]
    sq_sb = [ph1.enter_context(nc.sbuf_tensor(f"sq_sb{i}", [128, TT], F16)) for i in range(2)]
    xtok = [ph1.enter_context(nc.sbuf_tensor(f"xtok{i}", [128, D], F16)) for i in range(4)]
    ps_tx = [ph1.enter_context(nc.psum_tensor(f"ps_tx{i}", [128, TT], F16)) for i in range(2)]
    ps_qkv = ph1.enter_context(nc.psum_tensor("ps_qkv", [128, 3 * TT], F32))
    ps_vt = ph1.enter_context(nc.psum_tensor("ps_vt", [128, TT], BF16))
    ps_qn = ph1.enter_context(nc.psum_tensor("ps_qn", [128, 512], F32))
    ps_bb = ph1.enter_context(nc.psum_tensor("ps_bb", [128, 512], F32))

    for c in range(CCH):
        p.op("gpsimd",
             lambda e, c=c: e.dma_start(
                 out=qkvw_sb[:, c * ICH:(c + 1) * ICH, :],
                 in_=qkvw[:, c * 128:(c + 1) * 128].rearrange("(i p) c -> p i c", p=128)),
             inc=sWld, inc_amt=16)
    QKVW_LD = sWld.n

    panel_mm_done = {}
    qk_spill_val = [0, 0, 0]
    v_spill_val = [0, 0]
    vstg_evac_val = [0, 0]
    sq_mm_val = [0, 0]
    qk_ctr = 0
    v_ctr = 0
    norm_defer = []
    panel_mm_done = {}
    tx_copy_done = [0, 0]
    xtok_free = [0, 0, 0, 0]
    for t in range(NTT):
        for q in range(4):
            qidx = 4 * t + q
            xb = xtok[qidx % 4]
            cwaits = []
            if xtok_free[qidx % 4] > 0:
                cwaits.append((sTx, xtok_free[qidx % 4]))
            if q == 0 and t >= 2:
                cwaits.append((sMM, panel_mm_done[t - 2]))
            p.op("gpsimd",
                 lambda e, t=t, q=q, xb=xb: e.dma_start(
                     out=xb[:, :],
                     in_=x32[t * TT + q * 128: t * TT + (q + 1) * 128, :]),
                 waits=cwaits, inc=sCast, inc_amt=16)
            cast_val = sCast.n
            for g4 in range(8):
                bank = (qidx * 8 + g4) % 2
                for k4 in range(4):
                    i = g4 * 4 + k4
                    qidx_l = qidx
                    waits = []
                    if k4 == 0:
                        waits.append((sCast, cast_val))
                        if qidx == 0 and g4 == 0:
                            waits.append((sCst, CONST_LD))
                            waits.append((sPv, CONST_V))
                        if tx_copy_done[bank] > 0:
                            waits.append((sTxe, tx_copy_done[bank]))
                    p.op("tensor",
                         lambda e, xb=xb, i=i, bank=bank, k4=k4: e.transpose(
                             ps_tx[bank][:, k4 * 128:(k4 + 1) * 128],
                             xb[:, i * 128:(i + 1) * 128],
                             ident16[:, :]),
                         waits=waits, inc=sTx if k4 == 3 else None)
                tx_val = sTx.n
                p.op("vector",
                     lambda e, t=t, q=q, g4=g4, bank=bank: e.tensor_copy(
                         xt_sb[t % 2][:, g4 * 4:(g4 + 1) * 4, q * 128:(q + 1) * 128],
                         ps_tx[bank][:, :].rearrange("p (a b) -> p a b", a=4)),
                     waits=[(sTx, tx_val)], inc=sTxe)
                tx_copy_done[bank] = sTxe.n
            xtok_free[qidx % 4] = sTx.n
        panel_ready = sTxe.n

        for c in range(CCH):
            tile_idx = t * CCH + c
            bank = tile_idx % 3
            waits = [(sTxe, panel_ready)]
            if tile_idx == 0:
                waits.append((sWld, QKVW_LD))
                waits.append((sCst, CONST_LD))
                waits.append((sPv, CONST_V))
            if tile_idx >= 3:
                waits.append((sEv, tile_idx - 3 + 1))
            for i in range(ICH):
                p.op("tensor",
                     lambda e, c=c, t=t, i=i, bank=bank: e.matmul(
                         ps_qkv[:, bank * TT:(bank + 1) * TT],
                         qkvw_sb[:, c * ICH + i, :],
                         xt_sb[t % 2][:, i, :],
                         start=(i == 0), stop=(i == ICH - 1)),
                     waits=waits if i == 0 else (),
                     inc=sMM if i == ICH - 1 else None)
            mm_val = sMM.n
            p.flush(norm_defer)

            lh, kind = c // 3, c % 3
            if kind in (0, 1):
                bidx = qk_ctr % 3
                sqb = sq_sb[qk_ctr % 2]
                sq_war = sq_mm_val[qk_ctr % 2]
                qk_ctr += 1
                sb = stg[bidx]
                waits = [(sMM, mm_val), (sCst, CONST_LD)]
                if qk_spill_val[bidx] > 0:
                    waits.append((sSp, qk_spill_val[bidx]))
                p.op("scalar",
                     lambda e, sb=sb, bank=bank, c=c: e.activation(
                         sb[:, :], ps_qkv[:, bank * TT:(bank + 1) * TT],
                         AF.Identity, bias=qkvb_sb[:, c:c + 1], scale=1.0),
                     waits=waits, inc=sEv)
                ev_val = sEv.n
                dst = qTd if kind == 0 else kTd
                p.op("sync",
                     lambda e, sb=sb, dst=dst, lh=lh, t=t: e.dma_start(
                         out=dst[lh * 128:(lh + 1) * 128, t * TT:(t + 1) * TT],
                         in_=sb[:, :]),
                     waits=[(sEv, ev_val)], inc=sSp, inc_amt=16)
                qk_spill_val[bidx] = sSp.n
                waits = [(sEv, ev_val)]
                if sq_war > 0:
                    waits.append((sNm, sq_war))
                p.op("vector",
                     lambda e, sb=sb, sqb=sqb: e.tensor_mul(sqb[:, :], sb[:, :], sb[:, :]),
                     waits=waits, inc=sNq)
                p.op("tensor",
                     lambda e, sqb=sqb: e.matmul(ps_qn[0:1, 0:TT], ones16[:, :], sqb[:, :],
                                                 start=True, stop=True),
                     waits=[(sNq, sNq.n), (sNq, sNq.n)], inc=sNm, defer=norm_defer)
                sq_mm_val[(qk_ctr - 1) % 2] = sNm.n
                acc_col = lh * 2 + kind
                p.op("vector",
                     lambda e: e.reduce_max(rowtmp[0:1, 0:1], ps_qn[0:1, 0:TT], axis=AXX),
                     waits=[(sNm, sNm.n)], inc=sNq)
                p.op("vector",
                     lambda e, acc_col=acc_col: e.tensor_max(
                         norm_acc[0:1, acc_col:acc_col + 1],
                         norm_acc[0:1, acc_col:acc_col + 1],
                         rowtmp[0:1, 0:1]),
                     waits=(), inc=sNq)
            else:
                sbv = stgv[v_ctr % 2]
                vb = vstg[v_ctr % 2]
                waits = [(sMM, mm_val), (sCst, CONST_LD)]
                if vstg_evac_val[v_ctr % 2] > 0:
                    waits.append((sVt, vstg_evac_val[v_ctr % 2]))
                p.op("scalar",
                     lambda e, sbv=sbv, bank=bank, c=c: e.activation(
                         sbv[:, :], ps_qkv[:, bank * TT:(bank + 1) * TT],
                         AF.Identity, bias=qkvb_sb[:, c:c + 1], scale=1.0),
                     waits=waits, inc=sEv)
                ev_val = sEv.n
                for j in range(4):
                    p.op("tensor",
                         lambda e, sbv=sbv, j=j: e.transpose(
                             ps_vt[:, j * 128:(j + 1) * 128],
                             sbv[:, j * 128:(j + 1) * 128], identbf[:, :]),
                         waits=([(sEv, ev_val), (sVe, sVe.n)] if j == 0 else ()),
                         inc=sVt if j == 3 else None)
                vstg_evac_val[v_ctr % 2] = sVt.n
                vt_val = sVt.n
                waits = [(sVt, vt_val)]
                if v_spill_val[v_ctr % 2] > 0:
                    waits.append((sSp, v_spill_val[v_ctr % 2]))
                p.op("vector",
                     lambda e, vb=vb: e.tensor_copy(vb[:, :], ps_vt[:, :]),
                     waits=waits, inc=sVe)
                ve_val = sVe.n
                p.op("sync",
                     lambda e, vb=vb, lh=lh, t=t: e.dma_start(
                         out=vd[t * TT:(t + 1) * TT, lh * 128:(lh + 1) * 128]
                         .rearrange("(j p) d -> p j d", p=128),
                         in_=vb[:, :].rearrange("p (j d) -> p j d", j=4)),
                     waits=[(sVe, ve_val)], inc=sSp, inc_amt=16)
                v_spill_val[v_ctr % 2] = sSp.n
                v_ctr += 1
        panel_mm_done[t] = sMM.n
    p.flush(norm_defer)

    nq0 = sNq.n
    for lh in range(NHL):
        p.op("vector",
             lambda e, lh=lh: e.tensor_mul(rowtmp[0:1, 1:2],
                                           norm_acc[0:1, 2 * lh:2 * lh + 1],
                                           norm_acc[0:1, 2 * lh + 1:2 * lh + 2]),
             waits=[(sNq, nq0)], inc=sNq)
        p.op("scalar",
             lambda e: e.activation(rowtmp[0:1, 2:3], rowtmp[0:1, 1:2], AF.Ln),
             waits=[(sNq, sNq.n)], inc=sBh)
        p.op("scalar",
             lambda e: e.activation(rowtmp[0:1, 3:4], rowtmp[0:1, 2:3], AF.Exp, scale=0.5),
             waits=(), inc=None)
        p.op("scalar",
             lambda e: e.activation(rowtmp[0:1, 5:6], rowtmp[0:1, 3:4], AF.Identity),
             waits=(), inc=sBh)
        p.op("vector",
             lambda e: e.tensor_single_scalar(rowtmp[0:1, 4:5], rowtmp[0:1, 3:4],
                                              -SCALE, ALU.mult),
             waits=[(sBh, sBh.n)], inc=sNq)
        p.op("tensor",
             lambda e: e.matmul(ps_bb[:, 0:1], ones32r[:, :], rowtmp[0:1, 4:5],
                                start=True, stop=True),
             waits=[(sNq, sNq.n), (sBh, sBh.n)], inc=sNm)
        p.op("vector",
             lambda e, lh=lh: e.tensor_copy(bias_h[:, lh:lh + 1], ps_bb[:, 0:1]),
             waits=[(sNm, sNm.n)], inc=sBh)
    BH_DONE = sBh.n
    P1_SPILLS = sSp.n
    p.op("gpsimd", lambda e: e.dma_start(out=x16md[:, :], in_=x32m[:, :]),
         waits=[(sCast, 16 * NTT)], inc=sCastM, inc_amt=16)
    for c in range(8):
        p.op("gpsimd", lambda e, c=c: e.dma_start(out=w16d[c * 512:(c + 1) * 512, :],
                                                  in_=Wt[c * 512:(c + 1) * 512, :]),
             inc=sCastW, inc_amt=16)

    ph1.close()
    ph2 = contextlib.ExitStack()
    kT_sb = [ph2.enter_context(nc.sbuf_tensor(f"kT{i}", [128, S], F16)) for i in range(2)]
    qT_sb = [ph2.enter_context(nc.sbuf_tensor(f"qT{i}", [128, S], F16)) for i in range(2)]
    v_sb = [ph2.enter_context(nc.sbuf_tensor(f"v{i}", [128, KCH, 128], BF16)) for i in range(2)]
    pt_sb = [ph2.enter_context(nc.sbuf_tensor(f"pt{i}", [128, 2 * TQ], BF16)) for i in range(3)]
    tree_bufs = [ph2.enter_context(nc.sbuf_tensor(f"tr{i}", [128, TQ], BF16)) for i in range(16)]
    rrow = ph2.enter_context(nc.sbuf_tensor("rrow", [1, 2 * TQ], F32))
    rb_sb = ph2.enter_context(nc.sbuf_tensor("rb_sb", [128, 2 * TQ], F32))
    ctxn = [ph2.enter_context(nc.sbuf_tensor(f"ctxn{i}", [128, TQ], F16)) for i in range(2)]
    ps_st = [ph2.enter_context(nc.psum_tensor(f"ps_stt{i}", [128, 2 * TQ], F32))
             for i in range(3)]
    ps_ctx2 = ph2.enter_context(nc.psum_tensor("ps_ctx2", [128, 2 * TQ], F32))

    st_free = [0, 0, 0]
    pt_free_pv = [0, 0, 0]
    pt_free_trA = [0, 0, 0]
    pt_free_trB = [0, 0, 0]
    ctxn_dma_val = [0, 0]
    head_pv_done = {}

    def ap_of(tok):
        if tok[0] == "pt":
            return pt_sb[tok[1]][:, tok[2] * TQ:(tok[2] + 1) * TQ]
        return tok[1][:, :]

    class Tree:

        def __init__(self, pool):
            self.free = list(pool)
            self.occ = {}

        def insert_pair_ops(self, tokx, toky):
            ops = []
            outb = self.free.pop()
            ops.append(lambda e, a=tokx, bb=toky, outb=outb: e.tensor_add(
                outb[:, :], ap_of(a), ap_of(bb)))
            cur = ("buf", outb)
            lvl = 0
            while lvl in self.occ:
                prev = self.occ.pop(lvl)
                outb = prev[1]
                ops.append(lambda e, a=cur, bb=prev, outb=outb: e.tensor_add(
                    outb[:, :], ap_of(a), ap_of(bb)))
                if cur[1] is not outb:
                    self.free.append(cur[1])
                cur = ("buf", outb)
                lvl += 1
            self.occ[lvl] = cur
            return ops

        def root(self):
            assert len(self.occ) == 1
            return self.occ.popitem()[1][1]

    for lh in range(NHL):
        b = lh % 2
        waits = [(sSp, P1_SPILLS), (sBh, BH_DONE)]
        if lh >= 2:
            waits.append((sPV, head_pv_done[lh - 2]))
        p.op("sync", lambda e, b=b, lh=lh: e.dma_start(
            out=kT_sb[b][:, :], in_=kTd[lh * 128:(lh + 1) * 128, :]),
            waits=waits, inc=sLd, inc_amt=16)
        p.op("sync", lambda e, b=b, lh=lh: e.dma_start(
            out=qT_sb[b][:, :], in_=qTd[lh * 128:(lh + 1) * 128, :]),
            inc=sLd, inc_amt=16)
        p.op("sync", lambda e, b=b, lh=lh: e.dma_start(
            out=v_sb[b][:, :, :],
            in_=vd[:, lh * 128:(lh + 1) * 128].rearrange("(j p) d -> p j d", p=128)),
            inc=sLd, inc_amt=16)
        ld_val = sLd.n

        for qp in range(NQT // 2):
            qA, qB = 2 * qp, 2 * qp + 1
            ctx_free_val = sEp.n
            treeA = Tree(tree_bufs[0:8])
            treeB = Tree(tree_bufs[8:16])
            exp_vals = {}

            def emit_pv(kc, ctx_free_val=None):
                par = kc % 3
                ptb = pt_sb[par]
                w2 = [(sEX, exp_vals[kc])]
                if kc == 0:
                    w2.append((sEp, ctx_free_val))
                p.op("tensor",
                     lambda e, b=b, kc=kc, ptb=ptb: e.matmul(
                         ps_ctx2[:, 0:TQ], v_sb[b][:, kc, :], ptb[:, 0:TQ],
                         start=(kc == 0), stop=(kc == KCH - 1)),
                     waits=w2, inc=None)
                p.op("tensor",
                     lambda e, b=b, kc=kc, ptb=ptb: e.matmul(
                         ps_ctx2[:, TQ:2 * TQ], v_sb[b][:, kc, :], ptb[:, TQ:2 * TQ],
                         start=(kc == 0), stop=(kc == KCH - 1)),
                     waits=(), inc=sPV)
                pt_free_pv[par] = sPV.n

            for kc in range(KCH):
                par = kc % 3
                stps = ps_st[par]
                ptb = pt_sb[par]
                waits = []
                if kc == 0 and qp == 0:
                    waits.append((sLd, ld_val))
                    if lh == 0:
                        waits.append((sBh, BH_DONE))
                if kc == 0:
                    waits.append((sEp, ctx_free_val))
                if st_free[par] > 0:
                    waits.append((sEX, st_free[par]))
                p.op("tensor",
                     lambda e, b=b, kc=kc, qA=qA, stps=stps: e.matmul(
                         stps[:, 0:TQ],
                         kT_sb[b][:, kc * 128:(kc + 1) * 128],
                         qT_sb[b][:, qA * TQ:(qA + 1) * TQ],
                         start=True, stop=True),
                     waits=waits, inc=None)
                p.op("tensor",
                     lambda e, b=b, kc=kc, qB=qB, stps=stps: e.matmul(
                         stps[:, TQ:2 * TQ],
                         kT_sb[b][:, kc * 128:(kc + 1) * 128],
                         qT_sb[b][:, qB * TQ:(qB + 1) * TQ],
                         start=True, stop=True),
                     waits=(), inc=sST)
                st_val = sST.n
                if kc >= 1:
                    emit_pv(kc - 1, ctx_free_val)

                waits = [(sST, st_val)]
                if pt_free_pv[par] > 0:
                    waits.append((sPV, pt_free_pv[par]))
                    waits.append((sTR, pt_free_trA[par]))
                    waits.append((sTRb, pt_free_trB[par]))
                p.op("scalar",
                     lambda e, ptb=ptb, stps=stps, lh=lh: e.activation(
                         ptb[:, :], stps[:, :], AF.Exp,
                         bias=bias_h[:, lh:lh + 1], scale=SCALE),
                     waits=waits, inc=sEX)
                exp_vals[kc] = sEX.n
                st_free[par] = sEX.n

                if kc % 2 == 1:
                    par0 = (kc - 1) % 3
                    opsA = treeA.insert_pair_ops(("pt", par0, 0), ("pt", par, 0))
                    for oi, fn_ in enumerate(opsA):
                        p.op("vector", fn_,
                             waits=[(sEX, exp_vals[kc])] if oi == 0 else (),
                             inc=sTR if oi == len(opsA) - 1 else None)
                    pt_free_trA[par0] = pt_free_trA[par] = sTR.n
                    opsB = treeB.insert_pair_ops(("pt", par0, 1), ("pt", par, 1))
                    for oi, fn_ in enumerate(opsB):
                        p.op("vector", fn_,
                             waits=(),
                             inc=sTRb if oi == len(opsB) - 1 else None)
                    pt_free_trB[par0] = pt_free_trB[par] = sTRb.n
            emit_pv(KCH - 1)
            pv_all = sPV.n

            rootA, rootB = treeA.root(), treeB.root()
            trA_val, trB_val = sTR.n, sTRb.n
            p.op("tensor",
                 lambda e, rootA=rootA: e.matmul(ps_st[0][0:1, 0:TQ], onesbf[:, :],
                                                 rootA[:, :], start=True, stop=True),
                 waits=[(sTR, trA_val), (sEX, exp_vals[KCH - 1]), (sPv, CONST_V)],
                 inc=None)
            p.op("tensor",
                 lambda e, rootB=rootB: e.matmul(ps_st[1][0:1, 0:TQ], onesbf[:, :],
                                                 rootB[:, :], start=True, stop=True),
                 waits=[(sTRb, trB_val)], inc=sNm)
            mm_val = sNm.n
            p.op("vector",
                 lambda e: e.reciprocal(rrow[0:1, 0:TQ], ps_st[0][0:1, 0:TQ]),
                 waits=[(sNm, mm_val)], inc=None)
            p.op("vector",
                 lambda e: e.reciprocal(rrow[0:1, TQ:2 * TQ], ps_st[1][0:1, 0:TQ]),
                 waits=(), inc=sEp)
            rec_val = sEp.n
            p.op("tensor",
                 lambda e: e.matmul(ps_st[0][:, 0:TQ], ones32r[:, :], rrow[0:1, 0:TQ],
                                    start=True, stop=True),
                 waits=[(sEp, rec_val)], inc=None)
            p.op("tensor",
                 lambda e: e.matmul(ps_st[1][:, 0:TQ], ones32r[:, :], rrow[0:1, TQ:2 * TQ],
                                    start=True, stop=True),
                 waits=(), inc=sNm)
            bc_val = sNm.n
            p.op("vector",
                 lambda e: e.tensor_copy(rb_sb[:, 0:TQ], ps_st[0][:, 0:TQ]),
                 waits=[(sNm, bc_val)], inc=None)
            p.op("vector",
                 lambda e: e.tensor_copy(rb_sb[:, TQ:2 * TQ], ps_st[1][:, 0:TQ]),
                 waits=(), inc=sEp)
            rb_val = sEp.n
            waits = [(sPV, pv_all), (sEp, rb_val)]
            if ctxn_dma_val[0] > 0:
                waits.append((sAG, max(ctxn_dma_val[0], ctxn_dma_val[1])))
            p.op("vector",
                 lambda e: e.tensor_tensor(ctxn[0][:, :], ps_ctx2[:, 0:TQ],
                                           rb_sb[:, 0:TQ], op=ALU.mult),
                 waits=waits, inc=sEp)
            evA = sEp.n
            p.op("vector",
                 lambda e: e.tensor_tensor(ctxn[1][:, :], ps_ctx2[:, TQ:2 * TQ],
                                           rb_sb[:, TQ:2 * TQ], op=ALU.mult),
                 waits=(), inc=sEp)
            evB = sEp.n
            p.op("sync",
                 lambda e, lh=lh, qA=qA: e.dma_start(
                     out=ag_in[lh * 128:(lh + 1) * 128, qA * TQ:(qA + 1) * TQ],
                     in_=ctxn[0][:, :]),
                 waits=[(sEp, evA)], inc=sAG, inc_amt=16)
            ctxn_dma_val[0] = sAG.n
            p.op("sync",
                 lambda e, lh=lh, qB=qB: e.dma_start(
                     out=ag_in[lh * 128:(lh + 1) * 128, qB * TQ:(qB + 1) * TQ],
                     in_=ctxn[1][:, :]),
                 waits=[(sEp, evB)], inc=sAG, inc_amt=16)
            ctxn_dma_val[1] = sAG.n
        head_pv_done[lh] = sPV.n
        ag_head_val = sAG.n
        p.op("gpsimd",
             lambda e, lh=lh: e.collective_compute(
                 "AllGather", ALU.bypass, replica_groups=[list(range(NCORES))],
                 ins=[ag_in[lh * 128:(lh + 1) * 128, :].opt()],
                 outs=[ag_out[lh * 1024:(lh + 1) * 1024, :].opt()]),
             waits=[(sAG, ag_head_val)], inc=sCC, inc_amt=1)

    AG_ALL = sAG.n

    ph2.close()
    ph3 = contextlib.ExitStack()
    y1_16 = ph3.enter_context(nc.sbuf_tensor("y1_16", [128, OCH, TLOC], F16))
    idx_sb = ph3.enter_context(nc.sbuf_tensor("idx_sb", [128, OCH], I32))
    gstg = [ph3.enter_context(nc.sbuf_tensor(f"gstg{i}", [128, TLOC], F16)) for i in range(2)]
    p.op("sync", lambda e: e.dma_start(out=idx_sb[:, :], in_=gidx[:, :]),
         waits=[(sAG, AG_ALL)], inc=sIdx, inc_amt=16)
    gstg_out_val = [0, 0]
    for c in range(OCH):
        waits = [(sIdx, 16), (sCC, NHL)] if c == 0 else []
        if gstg_out_val[c % 2] > 0:
            waits.append((sGa, gstg_out_val[c % 2]))
        p.op("gpsimd",
             lambda e, c=c: e.indirect_dma_start(
                 gstg[c % 2][:, :], None, ag_out_flat[0:128, :],
                 IndirectOffsetOnAxis(idx_sb[:, c:c + 1], axis=0)),
             waits=waits, inc=sGa, inc_amt=16)
        gv = sGa.n
        p.op("gpsimd",
             lambda e, c=c: e.dma_start(
                 out=ctx_mine_d[c * 128:(c + 1) * 128, :], in_=gstg[c % 2][:, :]),
             waits=[(sGa, gv)], inc=sGa, inc_amt=16)
        gstg_out_val[c % 2] = sGa.n
    GATHER_ALL = sGa.n

    pA = contextlib.ExitStack()
    ctxp = pA.enter_context(nc.sbuf_tensor("ctxp", [128, OCH, SUB], F16))
    owp = [pA.enter_context(nc.sbuf_tensor(f"owp{i}", [128, ICH, 128], F16)) for i in range(2)]
    xtp = [pA.enter_context(nc.sbuf_tensor(f"xtp{i}", [128, SUB], F16)) for i in range(4)]
    ysb = pA.enter_context(nc.sbuf_tensor("ysb", [128, OCH, SUB], F32))
    y16t = pA.enter_context(nc.sbuf_tensor("y16t", [128, SUB], F16))
    sq16t = pA.enter_context(nc.sbuf_tensor("sq16t", [128, SUB], F16))
    obt_sb = pA.enter_context(nc.sbuf_tensor("obt_sb", [128, OCH], F32))
    mrow = pA.enter_context(nc.sbuf_tensor("mrow", [1, 2 * SUB], F32))
    vrow = pA.enter_context(nc.sbuf_tensor("vrow", [1, 2 * SUB], F32))
    mb_sb = pA.enter_context(nc.sbuf_tensor("mb_sb", [128, SUB], F32))
    rstdb = pA.enter_context(nc.sbuf_tensor("rstdb", [128, SUB], F32))
    tnorm = [pA.enter_context(nc.sbuf_tensor(f"tnorm{i}", [128, SUB], F32)) for i in range(3)]
    ps_op = pA.enter_context(nc.psum_tensor("ps_op", [128, 4 * SUB], F32))
    ps_st1 = pA.enter_context(nc.psum_tensor("ps_st1", [128, 2 * SUB], F32))
    ps_bc = pA.enter_context(nc.psum_tensor("ps_bc", [128, SUB], F32))

    p.op("sync", lambda e: e.dma_start(out=obt_sb[:, :], in_=obt[:, :]),
         waits=[(sAG, AG_ALL)], inc=sCst, inc_amt=16)
    OBT_LD = sCst.n

    ow_war = {}
    xt_war = {}
    evac_war = {}
    tnorm_war = [0, 0, 0]
    ga_war = 0
    for sub in range(NSUB):
        waits = [(sGa, GATHER_ALL)]
        if sub > 0:
            waits.append((sPm, ga_war))
        p.op("sync",
             lambda e, sub=sub: e.dma_start(
                 out=ctxp[:, :, :],
                 in_=ctx_mine_d[:, sub * SUB:(sub + 1) * SUB]
                 .rearrange("(c p) t -> p c t", p=128)),
             waits=waits, inc=sGa, inc_amt=16)
        ga_val = sGa.n

        for c in range(OCH):
            ci = sub * OCH + c
            ob_ = owp[ci % 2]
            waits = []
            if ci >= 2:
                waits.append((sPm, ow_war[ci - 2]))
            p.op("gpsimd",
                 lambda e, ob_=ob_, c=c: e.dma_start(
                     out=ob_[:, :, :],
                     in_=ow[:, c * 128:(c + 1) * 128].rearrange("(i p) c -> p i c", p=128)),
                 waits=waits, inc=sOwl, inc_amt=16)
            owl_val = sOwl.n
            xb = xtp[ci % 4]
            waits = [(sCastM, 16)]
            if ci >= 4:
                waits.append((sPv, xt_war[ci - 4]))
            p.op("sync",
                 lambda e, xb=xb, c=c, sub=sub: e.dma_start(
                     out=xb[:, :],
                     in_=x16md[sub * SUB:(sub + 1) * SUB, c * 128:(c + 1) * 128],
                     transpose=True),
                 waits=waits, inc=sXtp, inc_amt=16)
            xtp_val = sXtp.n
            bank = ci % 4
            waits = [(sOwl, owl_val), (sGa, ga_val)]
            if ci >= 4:
                waits.append((sPe, evac_war[ci - 4]))
            for i in range(ICH):
                p.op("tensor",
                     lambda e, ob_=ob_, i=i, bank=bank: e.matmul(
                         ps_op[:, bank * SUB:(bank + 1) * SUB],
                         ob_[:, i, :], ctxp[:, i, :],
                         start=(i == 0), stop=(i == ICH - 1)),
                     waits=waits if i == 0 else (),
                     inc=sPm if i == ICH - 1 else None)
            ow_war[ci] = sPm.n
            mm_val = sPm.n
            tn = tnorm[ci % 3]
            waits = [(sPm, mm_val), (sCst, OBT_LD)]
            if tnorm_war[ci % 3] > 0:
                waits.append((sPv, tnorm_war[ci % 3]))
            p.op("scalar",
                 lambda e, tn=tn, bank=bank, c=c: e.activation(
                     tn[:, :], ps_op[:, bank * SUB:(bank + 1) * SUB],
                     AF.Identity, bias=obt_sb[:, c:c + 1], scale=1.0),
                 waits=waits, inc=sPe)
            evac_war[ci] = sPe.n
            p.op("vector",
                 lambda e, c=c, xb=xb, tn=tn: e.tensor_add(ysb[:, c, :], tn[:, :], xb[:, :]),
                 waits=[(sPe, sPe.n), (sXtp, xtp_val)], inc=sPv)
            tnorm_war[ci % 3] = sPv.n
            xt_war[ci] = sPv.n
            p.op("vector",
                 lambda e, c=c: e.tensor_copy(y16t[:, :], ysb[:, c, :]),
                 waits=(), inc=sPv)
            p.op("vector",
                 lambda e: e.tensor_mul(sq16t[:, :], y16t[:, :], y16t[:, :]),
                 waits=(), inc=sPv)
            sqv = sPv.n
            p.op("tensor",
                 lambda e, c=c: e.matmul(ps_st1[0:1, 0:SUB], ones16[:, :], y16t[:, :],
                                         start=(c == 0), stop=(c == OCH - 1)),
                 waits=[(sPv, sqv)],
                 inc=None)
            p.op("tensor",
                 lambda e, c=c: e.matmul(ps_st1[0:1, SUB:2 * SUB], ones16[:, :], sq16t[:, :],
                                         start=(c == 0), stop=(c == OCH - 1)),
                 waits=(),
                 inc=sPm if c == OCH - 1 else None)
        ga_war = sPm.n
        st_done = sPm.n
        p.op("vector",
             lambda e: e.tensor_single_scalar(mrow[0:1, 0:SUB], ps_st1[0:1, 0:SUB],
                                              1.0 / D, ALU.mult),
             waits=[(sPm, st_done)], inc=sPv)
        p.op("vector",
             lambda e: e.tensor_single_scalar(mrow[0:1, SUB:2 * SUB], ps_st1[0:1, SUB:2 * SUB],
                                              1.0 / D, ALU.mult),
             waits=(), inc=sPv)
        p.op("vector",
             lambda e: e.tensor_mul(vrow[0:1, 0:SUB], mrow[0:1, 0:SUB], mrow[0:1, 0:SUB]),
             waits=(), inc=sPv)
        p.op("vector",
             lambda e: e.tensor_sub(vrow[0:1, 0:SUB], mrow[0:1, SUB:2 * SUB],
                                    vrow[0:1, 0:SUB]),
             waits=(), inc=sPv)
        p.op("vector",
             lambda e: e.tensor_single_scalar(vrow[0:1, 0:SUB], vrow[0:1, 0:SUB],
                                              EPS, ALU.add),
             waits=(), inc=sPv)
        varv = sPv.n
        p.op("scalar",
             lambda e: e.activation(vrow[0:1, SUB:2 * SUB], vrow[0:1, 0:SUB], AF.Ln),
             waits=[(sPv, varv)], inc=sPs)
        p.op("scalar",
             lambda e: e.activation(vrow[0:1, 0:SUB], vrow[0:1, SUB:2 * SUB], AF.Exp,
                                    scale=-0.5),
             waits=(), inc=None)
        p.op("scalar",
             lambda e: e.activation(vrow[0:1, SUB:2 * SUB], vrow[0:1, 0:SUB], AF.Identity),
             waits=(), inc=sPs)
        rstd_val = sPs.n
        p.op("tensor",
             lambda e: e.matmul(ps_bc[:, :], ones32r[:, :], mrow[0:1, 0:SUB],
                                start=True, stop=True),
             waits=[(sPs, rstd_val)], inc=sPm)
        p.op("vector",
             lambda e: e.tensor_copy(mb_sb[:, :], ps_bc[:, :]),
             waits=[(sPm, sPm.n)], inc=sPv)
        p.op("tensor",
             lambda e: e.matmul(ps_bc[:, :], ones32r[:, :], vrow[0:1, 0:SUB],
                                start=True, stop=True),
             waits=[(sPv, sPv.n)], inc=sPm)
        p.op("vector",
             lambda e: e.tensor_copy(rstdb[:, :], ps_bc[:, :]),
             waits=[(sPm, sPm.n)], inc=sPv)
        bc_done = sPv.n
        for c in range(OCH):
            tn = tnorm[c % 3]
            waits = [(sPv, bc_done)]
            p.op("vector",
                 lambda e, c=c, tn=tn: e.tensor_sub(tn[:, :], ysb[:, c, :], mb_sb[:, :]),
                 waits=waits, inc=None)
            p.op("vector",
                 lambda e, c=c, sub=sub, tn=tn: e.tensor_mul(
                     y1_16[:, c, sub * SUB:(sub + 1) * SUB], tn[:, :], rstdb[:, :]),
                 waits=(), inc=sPv)
        tnorm_war = [sPv.n] * 3
    Y1_DONE = sPv.n
    pA.close()

    pBs = contextlib.ExitStack()
    bm16 = pBs.enter_context(nc.sbuf_tensor("bm16", [32, D], F16))
    pr16 = pBs.enter_context(nc.sbuf_tensor("pr16", [32, NSUB, SUB], F16))
    pB = contextlib.ExitStack()
    a16 = pB.enter_context(nc.sbuf_tensor("a16", [32, D], F16))
    aT16 = pB.enter_context(nc.sbuf_tensor("aT16", [128, ICH, 32], F16))
    wr16 = pB.enter_context(nc.sbuf_tensor("wr16", [4, D], F16))
    wrT16 = pB.enter_context(nc.sbuf_tensor("wrT16", [128, ICH, 4], F16))
    sel16 = pB.enter_context(nc.sbuf_tensor("sel16", [4, 32], F16))
    gl32 = pB.enter_context(nc.sbuf_tensor("gl32", [4, SUB], F32))
    ge32 = pB.enter_context(nc.sbuf_tensor("ge32", [4, SUB], F32))
    rb4 = pB.enter_context(nc.sbuf_tensor("rb4", [4, SUB], F32))
    grow2 = pB.enter_context(nc.sbuf_tensor("grow2", [1, SUB], F32))
    ones4 = pB.enter_context(nc.sbuf_tensor("ones4", [4, 1], F32))
    gem16 = pB.enter_context(nc.sbuf_tensor("gem16", [4, SUB], F16))
    grep_sb = pB.enter_context(nc.sbuf_tensor("grep_sb", [32, SUB], F32))
    ps_g = pB.enter_context(nc.psum_tensor("ps_g", [128, 512], F32))
    ps_g2 = pB.enter_context(nc.psum_tensor("ps_g2", [128, 512], F32))
    ps_tA = pB.enter_context(nc.psum_tensor("ps_tA", [128, 64], F16))

    p.op("vector", lambda e: e.memset(ones4[:, :], 1.0),
         waits=[(sPv, Y1_DONE)], inc=sPv)
    CONST_V = sPv.n
    p.op("gpsimd", lambda e: e.dma_start(out=a16[:, :], in_=A[:, :]),
         waits=[(sPv, Y1_DONE)], inc=sGa, inc_amt=16)
    p.op("gpsimd", lambda e: e.dma_start(out=wr16[:, :], in_=Wr[:, :]), inc=sGa, inc_amt=16)
    p.op("gpsimd", lambda e: e.dma_start(out=bm16[:, :], in_=BmT[:, :]), inc=sGa, inc_amt=16)
    p.op("sync", lambda e: e.dma_start(out=sel16[:, :], in_=sel[:, :]),
         waits=[(sPv, Y1_DONE)], inc=sCst, inc_amt=16)
    SMALL_G = sGa.n
    SMALL_C = sCst.n
    for i in range(ICH):
        p.op("tensor",
             lambda e, i=i: e.transpose(ps_tA[:, 0:32],
                                        a16[:, i * 128:(i + 1) * 128], ident16[0:32, 0:32]),
             waits=[(sGa, SMALL_G), (sCst, CONST_LD), (sPv, sPv.n)], inc=sPm)
        p.op("vector",
             lambda e, i=i: e.tensor_copy(aT16[:, i, :], ps_tA[:, 0:32]),
             waits=[(sPm, sPm.n)], inc=sPv)
        p.op("tensor",
             lambda e, i=i: e.transpose(ps_tA[:, 32:36],
                                        wr16[:, i * 128:(i + 1) * 128], ident16[0:4, 0:4]),
             waits=[(sPv, sPv.n)], inc=sPm)
        p.op("vector",
             lambda e, i=i: e.tensor_copy(wrT16[:, i, :], ps_tA[:, 32:36]),
             waits=[(sPm, sPm.n)], inc=sPv)
    TRS_V = sPv.n

    for sub in range(NSUB):
        for i in range(ICH):
            p.op("tensor",
                 lambda e, i=i, sub=sub: e.matmul(
                     ps_g[0:4, 0:SUB], wrT16[:, i, :],
                     y1_16[:, i, sub * SUB:(sub + 1) * SUB],
                     start=(i == 0), stop=(i == ICH - 1)),
                 waits=[(sPv, TRS_V)] if i == 0 else (),
                 inc=sPm if i == ICH - 1 else None)
        p.op("vector",
             lambda e: e.tensor_copy(gl32[:, :], ps_g[0:4, 0:SUB]),
             waits=[(sPm, sPm.n)], inc=sPv)
        glv = sPv.n
        p.op("scalar",
             lambda e: e.activation(ge32[:, :], gl32[:, :], AF.Exp),
             waits=[(sPv, glv)], inc=None)
        p.op("scalar",
             lambda e: e.activation(rb4[:, :], ge32[:, :], AF.Identity),
             waits=(), inc=sPs)
        exp_drain = sPs.n
        p.op("tensor",
             lambda e: e.matmul(ps_g2[0:1, 0:SUB], ones4[:, :], ge32[:, :],
                                start=True, stop=True),
             waits=[(sPs, exp_drain), (sPv, CONST_V)], inc=sPm)
        p.op("vector",
             lambda e: e.reciprocal(grow2[0:1, :], ps_g2[0:1, 0:SUB]),
             waits=[(sPm, sPm.n)], inc=sPv)
        p.op("tensor",
             lambda e: e.matmul(ps_g2[0:4, 0:SUB], ones32r[0:1, 0:4], grow2[0:1, :],
                                start=True, stop=True),
             waits=[(sPv, sPv.n)], inc=sPm)
        p.op("vector",
             lambda e: e.tensor_copy(rb4[:, :], ps_g2[0:4, 0:SUB]),
             waits=[(sPm, sPm.n)], inc=sPv)
        p.op("vector",
             lambda e: e.tensor_tensor(gem16[:, :], ge32[:, :], rb4[:, :], op=ALU.mult),
             waits=(), inc=sPv)
        gem_done = sPv.n
        p.op("tensor",
             lambda e: e.matmul(ps_g[0:32, 0:SUB], sel16[:, :], gem16[:, :],
                                start=True, stop=True),
             waits=[(sPv, gem_done), (sCst, SMALL_C)], inc=sPm)
        p.op("vector",
             lambda e: e.tensor_copy(grep_sb[:, :], ps_g[0:32, 0:SUB]),
             waits=[(sPm, sPm.n)], inc=sPv)
        for i in range(ICH):
            p.op("tensor",
                 lambda e, i=i, sub=sub: e.matmul(
                     ps_g2[0:32, 0:SUB], aT16[:, i, :],
                     y1_16[:, i, sub * SUB:(sub + 1) * SUB],
                     start=(i == 0), stop=(i == ICH - 1)),
                 waits=[(sPv, sPv.n)] if i == 0 else (),
                 inc=sPm if i == ICH - 1 else None)
        p.op("vector",
             lambda e, sub=sub: e.tensor_tensor(pr16[:, sub, :], ps_g2[0:32, 0:SUB],
                                                grep_sb[:, :], op=ALU.mult),
             waits=[(sPm, sPm.n)], inc=sPv)
    PR_DONE = sPv.n
    pB.close()

    pC = contextlib.ExitStack()
    y2_16 = pC.enter_context(nc.sbuf_tensor("y2_16", [128, OCH, TLOC], F16, side="right"))
    ps_st2 = pC.enter_context(nc.psum_tensor("ps_st2", [128, 2 * NSUB * SUB], F32, side="right"))
    pW = contextlib.ExitStack()
    OSUP = 256
    wtp = [pW.enter_context(nc.sbuf_tensor(f"wtp{i}", [128, ICH, OSUP], F16)) for i in range(3)]
    ftmp = [pW.enter_context(nc.sbuf_tensor(f"ftmp{i}", [128, SUB], F32)) for i in range(3)]
    fsq = pW.enter_context(nc.sbuf_tensor("fsq", [128, SUB], F16))
    bt_sb = pW.enter_context(nc.sbuf_tensor("bt_sb", [128, OCH], F32))
    ps_f = pW.enter_context(nc.psum_tensor("ps_f", [128, 3 * SUB], F32))

    p.op("sync", lambda e: e.dma_start(out=bt_sb[:, :], in_=bt[:, :]),
         waits=[(sPv, PR_DONE)], inc=sCst, inc_amt=16)
    BT_LD = sCst.n

    NOS = D // OSUP
    OCPS = OSUP // 128
    wt_war = {}
    f_evac_war = {}
    ftmp_war = [0, 0, 0]
    for osup in range(NOS):
        wb = wtp[osup % 3]
        waits = [(sCastW, 16 * 8)]
        if osup == 0:
            waits.append((sPv, PR_DONE))
        if osup >= 3:
            waits.append((sPm, wt_war[osup - 3]))
        for i in range(ICH):
            p.op("sync",
                 lambda e, wb=wb, osup=osup, i=i: e.dma_start(
                     out=wb[:, i, :],
                     in_=w16d[osup * OSUP:(osup + 1) * OSUP, i * 128:(i + 1) * 128],
                     transpose=True),
                 waits=waits if i == 0 else (), inc=sWtl, inc_amt=16)
        wt_val = sWtl.n
        for sub in range(NSUB):
            for oc in range(OCPS):
                og = osup * OCPS + oc
                fi = (osup * NSUB + sub) * OCPS + oc
                bank = fi % 3
                waits = [(sWtl, wt_val), (sPv, PR_DONE)]
                if fi >= 3:
                    waits.append((sPe, f_evac_war[fi - 3]))
                for i in range(ICH):
                    p.op("tensor",
                         lambda e, wb=wb, i=i, oc=oc, sub=sub, bank=bank: e.matmul(
                             ps_f[:, bank * SUB:(bank + 1) * SUB],
                             wb[:, i, oc * 128:(oc + 1) * 128],
                             y1_16[:, i, sub * SUB:(sub + 1) * SUB],
                             start=(i == 0), stop=False),
                         waits=waits if i == 0 else (), inc=None)
                p.op("tensor",
                     lambda e, og=og, sub=sub, bank=bank: e.matmul(
                         ps_f[:, bank * SUB:(bank + 1) * SUB],
                         bm16[:, og * 128:(og + 1) * 128],
                         pr16[:, sub, :],
                         start=False, stop=True),
                     waits=(), inc=sPm)
                mm_val = sPm.n
                ft = ftmp[fi % 3]
                waits = [(sPm, mm_val), (sCst, BT_LD)]
                if ftmp_war[fi % 3] > 0:
                    waits.append((sPv, ftmp_war[fi % 3]))
                p.op("scalar",
                     lambda e, ft=ft, bank=bank, og=og: e.activation(
                         ft[:, :], ps_f[:, bank * SUB:(bank + 1) * SUB],
                         AF.Identity, bias=bt_sb[:, og:og + 1], scale=1.0),
                     waits=waits, inc=sPe)
                f_evac_war[fi] = sPe.n
                p.op("vector",
                     lambda e, og=og, sub=sub, ft=ft: e.tensor_add(
                         y2_16[:, og, sub * SUB:(sub + 1) * SUB],
                         ft[:, :],
                         y1_16[:, og, sub * SUB:(sub + 1) * SUB]),
                     waits=[(sPe, sPe.n)], inc=sPv)
                ftmp_war[fi % 3] = sPv.n
                p.op("vector",
                     lambda e, og=og, sub=sub: e.tensor_mul(
                         fsq[:, :],
                         y2_16[:, og, sub * SUB:(sub + 1) * SUB],
                         y2_16[:, og, sub * SUB:(sub + 1) * SUB]),
                     waits=(), inc=sPv)
                sq_val = sPv.n
                first = (osup == 0 and oc == 0)
                last = (osup == NOS - 1 and oc == OCPS - 1)
                p.op("tensor",
                     lambda e, og=og, sub=sub, first=first, last=last: e.matmul(
                         ps_st2[0:1, (2 * sub) * SUB:(2 * sub + 1) * SUB], ones16[:, :],
                         y2_16[:, og, sub * SUB:(sub + 1) * SUB],
                         start=first, stop=last),
                     waits=[(sPv, sq_val)], inc=None)
                p.op("tensor",
                     lambda e, sub=sub, first=first, last=last: e.matmul(
                         ps_st2[0:1, (2 * sub + 1) * SUB:(2 * sub + 2) * SUB],
                         ones16[:, :], fsq[:, :],
                         start=first, stop=last),
                     waits=(), inc=sPm if last else None)
        wt_war[osup] = sPm.n
    FFN_DONE = sPm.n
    pW.close()
    pBs.close()
    ph3.close()

    pD = contextlib.ExitStack()
    m2row = pD.enter_context(nc.sbuf_tensor("m2row", [1, 2 * SUB], F32))
    v2row = pD.enter_context(nc.sbuf_tensor("v2row", [1, 2 * SUB], F32))
    mtok = pD.enter_context(nc.sbuf_tensor("mtok", [128, 2], F32))
    y2tok = pD.enter_context(nc.sbuf_tensor("y2tok", [128, OCH * 128], F16))
    fin32 = [pD.enter_context(nc.sbuf_tensor(f"fin{i}", [128, D], F32)) for i in range(2)]
    ps_t = pD.enter_context(nc.psum_tensor("ps_t", [128, 1024], F32))
    ps_t16 = ps_t.ap().bitcast(F16)

    fin_dma_val = [0, 0]
    for sub in range(NSUB):
        p.op("vector",
             lambda e, sub=sub: e.tensor_single_scalar(
                 m2row[0:1, 0:SUB], ps_st2[0:1, (2 * sub) * SUB:(2 * sub + 1) * SUB],
                 1.0 / D, ALU.mult),
             waits=[(sPm, FFN_DONE)], inc=sPv)
        p.op("vector",
             lambda e, sub=sub: e.tensor_single_scalar(
                 m2row[0:1, SUB:2 * SUB],
                 ps_st2[0:1, (2 * sub + 1) * SUB:(2 * sub + 2) * SUB],
                 1.0 / D, ALU.mult),
             waits=(), inc=sPv)
        p.op("vector",
             lambda e: e.tensor_mul(v2row[0:1, 0:SUB], m2row[0:1, 0:SUB],
                                    m2row[0:1, 0:SUB]),
             waits=(), inc=sPv)
        p.op("vector",
             lambda e: e.tensor_sub(v2row[0:1, 0:SUB], m2row[0:1, SUB:2 * SUB],
                                    v2row[0:1, 0:SUB]),
             waits=(), inc=sPv)
        p.op("vector",
             lambda e: e.tensor_single_scalar(v2row[0:1, 0:SUB], v2row[0:1, 0:SUB],
                                              EPS, ALU.add),
             waits=(), inc=sPv)
        varv = sPv.n
        p.op("scalar",
             lambda e: e.activation(v2row[0:1, SUB:2 * SUB], v2row[0:1, 0:SUB], AF.Ln),
             waits=[(sPv, varv)], inc=sPs)
        p.op("scalar",
             lambda e: e.activation(v2row[0:1, 0:SUB], v2row[0:1, SUB:2 * SUB], AF.Exp,
                                    scale=-0.5),
             waits=(), inc=None)
        p.op("scalar",
             lambda e: e.activation(v2row[0:1, SUB:2 * SUB], v2row[0:1, 0:SUB], AF.Identity),
             waits=(), inc=sPs)
        row_val = sPs.n
        for j in range(SUB // 128):
            p.op("tensor",
                 lambda e, j=j: e.transpose(ps_t.ap()[:, 0:1],
                                            m2row[0:1, j * 128:(j + 1) * 128],
                                            ident32[0:1, 0:1]),
                 waits=[(sPs, row_val), (sPv, sPv.n)], inc=sPm)
            p.op("vector",
                 lambda e: e.tensor_copy(mtok[:, 0:1], ps_t.ap()[:, 0:1]),
                 waits=[(sPm, sPm.n)], inc=sPv)
            p.op("tensor",
                 lambda e, j=j: e.transpose(ps_t.ap()[:, 0:1],
                                            v2row[0:1, j * 128:(j + 1) * 128],
                                            ident32[0:1, 0:1]),
                 waits=[(sPv, sPv.n)], inc=sPm)
            p.op("vector",
                 lambda e: e.tensor_copy(mtok[:, 1:2], ps_t.ap()[:, 0:1]),
                 waits=[(sPm, sPm.n)], inc=sPv)
            tr_copy_val = [sPv.n, sPv.n]
            for oc in range(OCH):
                par = oc % 2
                reg = ps_t16[:, 0:128] if par == 0 else ps_t16[:, 1024:1152]
                p.op("tensor",
                     lambda e, oc=oc, sub=sub, j=j, reg=reg: e.transpose(
                         reg,
                         y2_16[:, oc, sub * SUB + j * 128: sub * SUB + (j + 1) * 128],
                         ident16[:, :]),
                     waits=[(sPv, tr_copy_val[par])], inc=sPm)
                p.op("vector",
                     lambda e, oc=oc, reg=reg: e.tensor_copy(
                         y2tok[:, oc * 128:(oc + 1) * 128], reg),
                     waits=[(sPm, sPm.n)], inc=sPv)
                tr_copy_val[par] = sPv.n
            fb = fin32[j % 2]
            waits = [(sPv, sPv.n)]
            if fin_dma_val[j % 2] > 0:
                waits.append((sOut, fin_dma_val[j % 2]))
            p.op("vector",
                 lambda e, fb=fb: e.tensor_scalar(
                     fb[:, :], y2tok[:, :],
                     mtok[:, 0:1], mtok[:, 1:2],
                     op0=ALU.subtract, op1=ALU.mult),
                 waits=waits, inc=sPv)
            p.op("gpsimd",
                 lambda e, fb=fb, sub=sub, j=j: e.dma_start(
                     out=y_out[sub * SUB + j * 128: sub * SUB + (j + 1) * 128, :],
                     in_=fb[:, :]),
                 waits=[(sPv, sPv.n)], inc=sOut, inc_amt=16)
            fin_dma_val[j % 2] = sOut.n

    p.op("gpsimd", lambda e: None, waits=[(sOut, sOut.n)])

    with nc.Block() as block:
        @block.sync
        def _(e):
            for f in p.q["sync"]:
                f(e)

        @block.tensor
        def _(e):
            for f in p.q["tensor"]:
                f(e)

        @block.vector
        def _(e):
            for f in p.q["vector"]:
                f(e)

        @block.scalar
        def _(e):
            for f in p.q["scalar"]:
                f(e)

        @block.gpsimd
        def _(e):
            for f in p.q["gpsimd"]:
                f(e)

    return nc


def kernel(L0_attn_x, L0_attn_attn_qkvw, L0_attn_attn_qkvb, L0_attn_attn_ow,
           L0_attn_attn_ob, L0_ffn_W, L0_ffn_b, L0_ffn_A, L0_ffn_Bm, L0_ffn_Wr):
    import ml_dtypes
    x = np.ascontiguousarray(np.asarray(L0_attn_x, dtype=np.float32).reshape(S, D))
    qkvw = np.asarray(L0_attn_attn_qkvw, dtype=np.float32)
    qkvb = np.asarray(L0_attn_attn_qkvb, dtype=np.float32)
    ow = np.ascontiguousarray(np.asarray(L0_attn_attn_ow, dtype=np.float32))
    ob = np.asarray(L0_attn_attn_ob, dtype=np.float32)
    W = np.ascontiguousarray(np.asarray(L0_ffn_W, dtype=np.float32))
    b = np.asarray(L0_ffn_b, dtype=np.float32)
    A = np.ascontiguousarray(np.asarray(L0_ffn_A, dtype=np.float32).reshape(32, D))
    Bm = np.asarray(L0_ffn_Bm, dtype=np.float32)
    Wr = np.ascontiguousarray(np.asarray(L0_ffn_Wr, dtype=np.float32))

    BmT = np.ascontiguousarray(Bm.transpose(0, 2, 1).reshape(32, D))
    sel = np.zeros((4, 32), dtype=np.float16)
    for e_ in range(4):
        sel[e_, e_ * 8:(e_ + 1) * 8] = 2.0
    id16 = np.eye(128, dtype=np.float16)
    idbf = np.eye(128).astype(ml_dtypes.bfloat16).view(np.uint16)
    id32 = np.eye(128, dtype=np.float32)
    obt = np.ascontiguousarray(ob.reshape(OCH, 128).T)
    btt = np.ascontiguousarray(b.reshape(OCH, 128).T)

    nc = build_nc()

    in_maps = []
    for r in range(NCORES):
        heads = range(NHL * r, NHL * (r + 1))
        cols = []
        bcols = []
        for h in heads:
            for k in range(3):
                cols.append(qkvw[:, k * D + h * 128:k * D + (h + 1) * 128])
                bcols.append(qkvb[k * D + h * 128:k * D + (h + 1) * 128])
        qkvw_r = np.ascontiguousarray(np.concatenate(cols, axis=1))
        qkvb_r = np.ascontiguousarray(np.stack(bcols, axis=1))
        gi = np.zeros((128, OCH), dtype=np.int32)
        for c in range(OCH):
            lh = c % NHL
            rp = c // NHL
            gi[:, c] = (lh * 1024 + rp * 128 + np.arange(128)) * S + r * TLOC
        in_maps.append({
            "x32": x,
            "x32m": np.ascontiguousarray(x[r * TLOC:(r + 1) * TLOC, :]),
            "qkvw": qkvw_r,
            "qkvb": qkvb_r,
            "ow": ow,
            "obt": obt,
            "W": W,
            "bt": btt,
            "A": A,
            "BmT": BmT,
            "Wr": Wr,
            "sel": sel,
            "id16": id16,
            "idbf": idbf,
            "id32": id32,
            "gidx": gi,
        })

    res = run_bass_kernel_spmd(nc, in_maps, core_ids=list(range(NCORES)))
    out = np.concatenate([res.results[r]["y_out"] for r in range(NCORES)], axis=0)
    return out.reshape(1, S, D).astype(np.float32)
